# version 02
# baseline (speedup 1.0000x reference)
import numpy as np
from contextlib import ExitStack

import concourse.bass as bass
import concourse.mybir as mybir
from concourse.bass_utils import run_bass_kernel_spmd

F32 = mybir.dt.float32
BF16 = mybir.dt.bfloat16
AF = mybir.ActivationFunctionType
ALU = mybir.AluOpType
AX = mybir.AxisListType

B_TOT, S, H, L, DI, NST, KCONV = 16, 2048, 1024, 4, 2048, 16, 4
DFF, NEXP, NCORE = 4096, 8, 8
BC = B_TOT // NCORE
TOK = BC * S
TC = 256
NCH = S // TC
MC = 512
NMC = TOK // MC
EPS_A, EPS_M = 1e-5, 1e-6
FIFO_SAFE_FD = 128

ENGINES = ("pe", "dve", "act", "pool", "sp")
DMAQ = ("dma_sp", "dma_act", "dma_pool")


def _sm_layout():
    cols, c = {}, 0

    def take(name, n):
        nonlocal c
        cols[name] = c
        c += n
    take("in_norm_g", 8); take("in_norm_b", 8)
    take("ln_g", L * 8); take("ln_b", L * 8)
    take("conv_w", L * 2 * KCONV * 16); take("conv_b", L * 2 * 16)
    take("dt_w", L * 16); take("dt_b", L * 16)
    take("Dp", L * 16)
    take("mln_g", L * 16); take("mln_b", L * 16)
    take("op1_b", 8); take("op2_b", 8)
    take("b1", NEXP * 32); take("b2", NEXP * 8)
    take("gamma", NEXP * 8); take("beta", NEXP * 8)
    return cols, c


SM, SM_COLS = _sm_layout()
CV_ONE, CV_EPSA, CV_EPSM, CV_NH, CV_INVH, CV_INVDI = 0, 1, 2, 3, 4, 5
NCONST = 6


class Op:
    __slots__ = ("eng", "fn", "reads", "writes", "sreads", "fd", "idx",
                 "inc_count", "waits", "dma", "meta")

    def __init__(self, eng, fn, reads, writes, sreads, fd, idx, dma):
        self.eng, self.fn = eng, fn
        self.reads, self.writes, self.sreads = list(reads), list(writes), list(sreads)
        self.fd, self.idx, self.dma = fd, idx, dma
        self.inc_count, self.waits = None, []
        self.meta = None


DMA_SLOTS = {"dma_sp": 1, "dma_act": 1, "dma_pool": 8}


class Emitter:

    def __init__(self, nc):
        self.nc = nc
        self.ops = []
        self.last_writer = {}
        self.readers = {}
        self.counts = {}
        for e in ENGINES:
            self.counts[e] = 0
        self.dma_qkeys = []
        for q, ns in DMA_SLOTS.items():
            for s in range(ns):
                k = f"{q}#{s}"
                self.counts[k] = 0
                self.dma_qkeys.append(k)
        self.dma_rr = {q: 0 for q in DMA_SLOTS}
        self.last_dma_on_slot = {}
        self.meta_provider = None
        self.meta_apply = None

    @staticmethod
    def _same_stream(a, b):
        if a.dma is not None or b.dma is not None:
            return False
        return a.eng == b.eng

    def _need_edge(self, prod, cons, scalar=False):
        if prod is None:
            return False
        if scalar:
            return True
        if self._same_stream(prod, cons):
            if prod.eng == "pe":
                return False
            return prod.fd < FIFO_SAFE_FD or cons.fd < FIFO_SAFE_FD
        return True

    def _assign_inc(self, d):
        if d.inc_count is None:
            d.inc_count = -1
            q = d.dma if d.dma is not None else d.eng
            self.counts[q] += 16 if d.dma is not None else 1

    def add(self, eng, fn, reads=(), writes=(), sreads=(), fd=512, dma=None):
        if dma is not None:
            slot = self.dma_rr[dma]
            self.dma_rr[dma] = (slot + 1) % DMA_SLOTS[dma]
            qkey = f"{dma}#{slot}"
        else:
            qkey = None
        op = Op(eng, fn, reads, writes, sreads, fd, len(self.ops), qkey)
        if self.meta_provider is not None:
            op.meta = self.meta_provider()
        deps = set()
        for b in op.reads:
            w = self.last_writer.get(b)
            if self._need_edge(w, op):
                deps.add(w)
        for b in op.sreads:
            w = self.last_writer.get(b)
            if self._need_edge(w, op, scalar=True):
                deps.add(w)
        for b in op.writes:
            w = self.last_writer.get(b)
            if self._need_edge(w, op):
                deps.add(w)
            for r in self.readers.get(b, ()):
                if r is not op and self._need_edge(r, op):
                    deps.add(r)
        by_q = {}
        for d in deps:
            q = d.dma if d.dma is not None else d.eng
            if d.dma is not None:
                by_q[(q, d.idx)] = d
            else:
                cur = by_q.get(q)
                if cur is None or cur.idx < d.idx:
                    by_q[q] = d
        if qkey is not None:
            prev = self.last_dma_on_slot.get(qkey)
            if prev is not None:
                op.waits.append(prev)
            self.last_dma_on_slot[qkey] = op
        for d in sorted(by_q.values(), key=lambda o: o.idx):
            self._assign_inc(d)
            op.waits.append(d)
        self.ops.append(op)
        if qkey is not None:
            self._assign_inc(op)
        for b in op.reads + op.sreads:
            self.readers.setdefault(b, []).append(op)
        for b in op.writes:
            self.last_writer[b] = op
            self.readers[b] = []
        return op

    def barrier(self):
        last_by_q = {}
        for o in self.ops:
            q = o.dma if o.dma is not None else o.eng
            last_by_q[q] = o
        for q, o in last_by_q.items():
            self._assign_inc(o)
        for eng in ENGINES:
            op = Op(eng, lambda b: b.nop(), [], [], [], 1, len(self.ops), None)
            for q, o in last_by_q.items():
                if q != eng:
                    op.waits.append(o)
            self.ops.append(op)

    def emit(self, st):
        nc = self.nc
        allq = list(ENGINES) + self.dma_qkeys
        ordinals = {q: 0 for q in allq}
        for op in self.ops:
            if op.inc_count is not None:
                q = op.dma if op.dma is not None else op.eng
                ordinals[q] += 16 if op.dma is not None else 1
                op.inc_count = ordinals[q]
        for q in allq:
            assert ordinals[q] < 60000, f"sem overflow {q}: {ordinals[q]}"
        sems = {q: st.enter_context(nc.semaphore(f"sem_{q.replace('#','_')}"))
                for q in allq}
        block = st.enter_context(nc.Block())
        per_eng = {e: [] for e in ENGINES}
        for op in self.ops:
            per_eng[op.eng].append(op)
        streams = {"pe": block.tensor, "dve": block.vector, "act": block.scalar,
                   "pool": block.gpsimd, "sp": block.sync}

        for eng in ENGINES:
            ops = per_eng[eng]

            def make(ops=ops):
                def body(b):
                    clock = {q: 0 for q in allq}
                    for op in ops:
                        for d in op.waits:
                            q = d.dma if d.dma is not None else d.eng
                            v = d.inc_count
                            if clock[q] < v:
                                b.wait_ge(sems[q], v)
                                clock[q] = v
                        if op.meta is not None and self.meta_apply is not None:
                            self.meta_apply(op.meta)
                        inst = op.fn(b)
                        if op.inc_count is not None:
                            q = op.dma if op.dma is not None else op.eng
                            step = 16 if op.dma is not None else 1
                            inst.then_inc(sems[q], step)
                return body
            streams[eng](make())


def build_program(debug=False):
    nc = bass.Bass("TRN2", target_bir_lowering=False, detect_race_conditions=False)
    em = Emitter(nc)
    stx = ExitStack()

    def din(name, shape, dt=F32):
        return nc.dram_tensor(name, list(shape), dt, kind="ExternalInput").ap()

    def dout(name, shape, dt=F32):
        return nc.dram_tensor(name, list(shape), dt, kind="ExternalOutput").ap()

    def dint(name, shape, dt=F32):
        kind = "ExternalOutput" if debug else "Internal"
        return nc.dram_tensor(name, list(shape), dt, kind=kind).ap()

    x0 = din("x0", [H, TOK])
    win = din("win", [L, 32, 8, 128, 128])
    wout = din("wout", [L, 8, 16, 128, 128])
    wxp = din("wxp", [L, 16, 128, 33])
    wop1 = din("wop1", [8, 128, 1024])
    wop2 = din("wop2", [8, 128, 1024])
    wrt = din("wrt", [8, 128, 8])
    w1e = din("w1e", [NEXP, 8, 128, DFF])
    w2e = din("w2e", [NEXP, 32, 128, H])
    smalls = din("smalls", [128, SM_COLS])
    consts = din("consts", [128, NCONST])
    selp = din("selp", [33, 34 * 128])
    idmat = din("idmat", [128, 128])
    onescol = din("onescol", [128, 1])
    onesrow = din("onesrow", [1, 128])

    out_fm = dout("out_fm", [H, TOK])
    xn32 = dint("xn32", [H, TOK])
    xnbf = dint("xnbf", [H, TOK], BF16)
    xaping = [dint("xa0", [H, TOK]), dint("xa1", [H, TOK])]
    s32 = dint("s32", [H, TOK])
    ymoe = dint("ymoe", [H, TOK])
    pdram = dint("pdram", [H, TOK])
    if debug:
        dbg_xl = dout("dbg_xl", [H, TOK], BF16)
        dbg_xi2 = dout("dbg_xi2", [DI, TOK])
        dbg_proj = dout("dbg_proj", [33, TOK])
        dbg_y = dout("dbg_y", [DI, TOK])
        dbg_sig = dout("dbg_sig", [DI, TOK])

    def sballoc(st, name, shape, dt=F32):
        return st.enter_context(nc.sbuf_tensor(name, shape, dt))

    cons = sballoc(stx, "cons", [128, NCONST])
    smal = sballoc(stx, "smal", [128, SM_COLS])
    ident = sballoc(stx, "ident", [128, 128])
    id_bf = sballoc(stx, "id_bf", [128, 128], BF16)
    ones_bf = sballoc(stx, "ones_bf", [128, 1], BF16)
    onesrow_f = sballoc(stx, "onesrow_f", [1, 128])
    onesrow_b = sballoc(stx, "onesrow_b", [1, 128], BF16)
    ln_bf = sballoc(stx, "ln_bf", [128, MC], BF16)
    ln_sq = sballoc(stx, "ln_sq", [128, MC], BF16)
    ln_t1 = sballoc(stx, "ln_t1", [128, MC])
    cv1t = sballoc(stx, "cv1t", [128, MC])
    rows = sballoc(stx, "rows", [1, 5 * MC])

    def row(i, width):
        return rows[:, i * MC:i * MC + width]
    R_MU, R_M2, R_VAR, R_R, R_MR = range(5)

    _pb_gens = {}
    _pbcur = {"t": None, "gen": 0}

    def set_psum(st_, width):
        _pbcur["gen"] += 1
        g = _pbcur["gen"]
        _pbcur["t"] = [st_.enter_context(
            nc.psum_tensor(f"pb{g}_{i}", [128, width], F32)) for i in range(8)]
        _pb_gens[g] = _pbcur["t"]

    def PB(i):
        return _pbcur["t"][i]

    em.meta_provider = lambda: _pbcur["gen"]
    em.meta_apply = lambda g: (_pbcur.update(t=_pb_gens[g]) if g in _pb_gens else None)

    PBK = [f"pb{i}" for i in range(8)]
    PB_MAIN = (0, 1)
    PB_R, PB_M, PB_Y, PB_DT, PB_S1, PB_S2 = 2, 3, 4, 5, 6, 7

    def dma(queue, out_ap, in_ap, reads, writes):
        eng = {"dma_sp": "sp", "dma_act": "act", "dma_pool": "pool"}[queue]
        em.add(eng, lambda b, o=out_ap, i=in_ap: b.dma_start(o, i),
               reads=reads, writes=writes, fd=512, dma=queue)

    def dma_acc(out_ap, in_ap, reads, writes):
        em.add("pool", lambda b, o=out_ap, i=in_ap: b.dma_start(o, i, accum_op=ALU.add),
               reads=reads, writes=writes, fd=512, dma="dma_pool")

    def cvap(col, p1=False):
        a = cons[:, col:col + 1]
        return a[0:1, :] if p1 else a

    def smap(name, idx):
        c = SM[name] + idx
        return smal[:, c:c + 1]

    dma("dma_sp", cons[:, :], consts, [], ["cons"])
    dma("dma_sp", smal[:, :], smalls, [], ["smal"])
    dma("dma_sp", ident[:, :], idmat, [], ["ident"])
    dma("dma_pool", id_bf[:, :], idmat, [], ["id_bf"])
    dma("dma_pool", ones_bf[:, :], onescol, [], ["ones_bf"])
    dma("dma_sp", onesrow_f[:, :], onesrow, [], ["onesrow_f"])
    dma("dma_pool", onesrow_b[:, :], onesrow, [], ["onesrow_b"])

    def tok_slice(d_ap, t0, w):
        return d_ap[:, t0:t0 + w].rearrange("(a p) t -> p a t", p=128)

    def ln_feed(ap, key, i, nblk, width, s1=PB_S1, s2=PB_S2):
        em.add("act", lambda b, a=ap: b.copy(ln_bf[:, :width], a),
               reads=[key], writes=["ln_bf"], fd=width)
        em.add("pe", lambda b, s=(i == 0), e=(i == nblk - 1): b.matmul(
            PB(s1)[0:1, :width], ones_bf[:, :], ln_bf[:, :width], start=s, stop=e),
            reads=["ln_bf", "ones_bf"], writes=[PBK[s1]], fd=width)
        em.add("act", lambda b, a=ap: b.activation(ln_sq[:, :width], a, AF.Square),
               reads=[key], writes=["ln_sq"], fd=width)
        em.add("pe", lambda b, s=(i == 0), e=(i == nblk - 1): b.matmul(
            PB(s2)[0:1, :width], ones_bf[:, :], ln_sq[:, :width], start=s, stop=e),
            reads=["ln_sq", "ones_bf"], writes=[PBK[s2]], fd=width)

    def ln_rows(width, inv_col, eps_col):
        em.add("act", lambda b: b.activation(row(R_MU, width), PB(PB_S1)[0:1, :width],
                                             AF.Identity, scale=cvap(inv_col, True)),
               reads=[PBK[PB_S1]], writes=["rows"], sreads=["cons"], fd=width)
        em.add("act", lambda b: b.activation(row(R_M2, width), PB(PB_S2)[0:1, :width],
                                             AF.Identity, scale=cvap(inv_col, True)),
               reads=[PBK[PB_S2]], writes=["rows"], sreads=["cons"], fd=width)
        em.add("dve", lambda b: b.tensor_mul(row(R_VAR, width), row(R_MU, width),
                                             row(R_MU, width)),
               reads=["rows"], writes=["rows"], fd=width)
        em.add("dve", lambda b: b.tensor_sub(row(R_VAR, width), row(R_M2, width),
                                             row(R_VAR, width)),
               reads=["rows"], writes=["rows"], fd=width)
        em.add("act", lambda b: b.activation(row(R_R, width), row(R_VAR, width), AF.Ln,
                                             bias=cvap(eps_col, True)),
               reads=["rows"], writes=["rows"], sreads=["cons"], fd=width)
        em.add("act", lambda b: b.activation(row(R_R, width), row(R_R, width), AF.Exp,
                                             scale=cvap(CV_NH, True)),
               reads=["rows"], writes=["rows"], sreads=["cons"], fd=width)
        em.add("dve", lambda b: b.tensor_mul(row(R_MR, width), row(R_MU, width),
                                             row(R_R, width)),
               reads=["rows"], writes=["rows"], fd=width)
        em.add("pe", lambda b: b.matmul(PB(PB_R)[:, :width], onesrow_f[:, :],
                                        row(R_R, width), start=True, stop=True),
               reads=["rows", "onesrow_f"], writes=[PBK[PB_R]], fd=width)
        em.add("pe", lambda b: b.matmul(PB(PB_M)[:, :width], onesrow_f[:, :],
                                        row(R_MR, width), start=True, stop=True),
               reads=["rows", "onesrow_f"], writes=[PBK[PB_M]], fd=width)

    def ln_center(ap, key, width):
        em.add("dve", lambda b, a=ap: b.tensor_mul(ln_t1[:, :width], a,
                                                   PB(PB_R)[:, :width]),
               reads=[key, PBK[PB_R]], writes=["ln_t1"], fd=width)
        em.add("dve", lambda b: b.tensor_sub(ln_t1[:, :width], ln_t1[:, :width],
                                             PB(PB_M)[:, :width]),
               reads=["ln_t1", PBK[PB_M]], writes=["ln_t1"], fd=width)

    sta = ExitStack()
    set_psum(sta, TC)
    xat = sballoc(sta, "xat", [128, 8, TC])
    xnt = sballoc(sta, "xnt", [128, 8, TC])
    xnbt = sballoc(sta, "xnbt", [128, 8, TC], BF16)
    xabt = sballoc(sta, "xabt", [128, 8, TC], BF16)
    sel_bf = sballoc(sta, "sel_bf", [33, 34 * 128], BF16)
    sel_f0 = sballoc(sta, "sel_f0", [33, 128])
    xpT = sballoc(sta, "xpT", [128, 16, 33])
    wst = sballoc(sta, "wst", [128, 2, 8, 128], BF16)
    wost = sballoc(sta, "wost", [128, 16, 128], BF16)
    xi0 = sballoc(sta, "xi0", [128, 16, TC + 3])
    xi1 = sballoc(sta, "xi1", [128, 16, TC + 3])
    xi2 = sballoc(sta, "xi2", [128, 16, TC])
    sig = sballoc(sta, "sig", [128, 16, TC])
    projf = sballoc(sta, "projf", [33, TC])
    projb = sballoc(sta, "projb", [33, TC], BF16)
    brepb = sballoc(sta, "brepb", [128, 16, TC], BF16)
    crepb = sballoc(sta, "crepb", [128, 16, TC], BF16)
    zt = sballoc(sta, "zt", [128, TC])
    mt = sballoc(sta, "mt", [128, TC])
    t1t = sballoc(sta, "t1t", [128, TC])
    dtt = sballoc(sta, "dtt", [128, TC])
    dxt = sballoc(sta, "dxt", [128, TC])
    a_t = sballoc(sta, "a_t", [128, TC])
    b_t = sballoc(sta, "b_t", [128, TC])
    hg = [sballoc(sta, f"hg{i}", [128, 4, TC]) for i in range(2)]
    yht = [sballoc(sta, f"yht{i}", [128, TC], BF16) for i in range(2)]
    carry = sballoc(sta, "carry", [128, 256])
    y32 = sballoc(sta, "y32", [128, 16, TC])
    ygb = sballoc(sta, "ygb", [128, 16, TC], BF16)

    dma("dma_pool", sel_bf[:, :], selp, [], ["sel_bf"])
    dma("dma_sp", sel_f0[:, :], selp[:, 0:128], [], ["sel_f0"])

    for sq in range(BC):
        for c in range(NCH):
            t0 = sq * S + c * TC
            dma("dma_sp", xat[:, :, :], tok_slice(x0, t0, TC), ["dram_x0"], ["xat"])
            for i in range(8):
                ln_feed(xat[:, i, :], "xat", i, 8, TC)
            ln_rows(TC, CV_INVH, CV_EPSA)
            for i in range(8):
                ln_center(xat[:, i, :], "xat", TC)
                em.add("act", lambda b, i=i: b.copy(xnt[:, i, :], ln_t1[:, :TC]),
                       reads=["ln_t1"], writes=["xnt"], fd=TC)
                em.add("act", lambda b, i=i: b.copy(xnbt[:, i, :], ln_t1[:, :TC]),
                       reads=["ln_t1"], writes=["xnbt"], fd=TC)
            dma("dma_sp", tok_slice(xn32, t0, TC), xnt[:, :, :], ["xnt"], ["dram_xn32"])
            dma("dma_act", tok_slice(xnbf, t0, TC), xnbt[:, :, :], ["xnbt"], ["dram_xnbf"])
            for i in range(8):
                ln_feed(xnt[:, i, :], "xnt", i, 8, TC)
            ln_rows(TC, CV_INVH, CV_EPSA)
            for i in range(8):
                ln_center(xnt[:, i, :], "xnt", TC)
                em.add("act", lambda b, i=i: b.activation(
                    xat[:, i, :], ln_t1[:, :TC], AF.Identity,
                    scale=smap("in_norm_g", i), bias=smap("in_norm_b", i)),
                    reads=["ln_t1"], writes=["xat"], sreads=["smal"], fd=TC)
            dma("dma_sp", tok_slice(xaping[0], t0, TC), xat[:, :, :], ["xat"], ["dram_xa0"])

    def conv_block(l, c, j):
        for (src, srck, ci) in ((xi0, "xi0", 0), (xi1, "xi1", 1)):
            if c == 0:
                em.add("dve", lambda b, src=src, j=j: b.memset(src[:, j, 0:3], 0.0),
                       reads=[], writes=[srck], fd=3)
            wbase = SM["conv_w"] + l * (2 * KCONV * 16) + ci * (KCONV * 16)
            em.add("dve", lambda b, src=src, j=j, col=wbase + (KCONV - 1) * 16 + j:
                   b.tensor_scalar(
                       cv1t[:, :TC], src[:, j, 3:3 + TC], smal[:, col:col + 1],
                       None, ALU.mult),
                   reads=[srck], writes=["cv1t"], sreads=["smal"], fd=TC)
            for k in range(1, KCONV):
                em.add("dve", lambda b, src=src, j=j, k=k,
                       col=wbase + (KCONV - 1 - k) * 16 + j:
                       b.scalar_tensor_tensor(
                           cv1t[:, :TC], src[:, j, 3 - k:3 - k + TC],
                           smal[:, col:col + 1], cv1t[:, :TC], ALU.mult, ALU.add),
                       reads=[srck, "cv1t"], writes=["cv1t"], sreads=["smal"], fd=TC)
            em.add("dve", lambda b, src=src, j=j: b.tensor_copy(
                src[:, j, 0:3], src[:, j, TC:TC + 3]),
                reads=[srck], writes=[srck], fd=3)
            bcol = SM["conv_b"] + l * 32 + ci * 16 + j
            if ci == 0:
                em.add("act", lambda b, j=j, bcol=bcol: b.activation(
                    xi1[:, j, 3:3 + TC], cv1t[:, :TC], AF.Silu,
                    bias=smal[:, bcol:bcol + 1]),
                    reads=["cv1t"], writes=["xi1"], sreads=["smal"], fd=TC)
            else:
                em.add("act", lambda b, j=j, bcol=bcol: b.activation(
                    xi2[:, j, :], cv1t[:, :TC], AF.Silu,
                    bias=smal[:, bcol:bcol + 1]),
                    reads=["cv1t"], writes=["xi2"], sreads=["smal"], fd=TC)

    def dt_chain(l, j):
        em.add("act", lambda b, j=j: b.activation(
            zt[:, :], PB(PB_DT)[:, :TC], AF.Identity,
            scale=smap("dt_w", l * 16 + j), bias=smap("dt_b", l * 16 + j)),
            reads=[PBK[PB_DT]], writes=["zt"], sreads=["smal"], fd=TC)
        em.add("dve", lambda b: b.tensor_scalar_max(mt[:, :], zt[:, :], 0.0),
               reads=["zt"], writes=["mt"], fd=TC)
        em.add("dve", lambda b: b.scalar_tensor_tensor(t1t[:, :], mt[:, :], -2.0,
                                                       zt[:, :], ALU.mult, ALU.add),
               reads=["mt", "zt"], writes=["t1t"], fd=TC)
        em.add("act", lambda b: b.activation(zt[:, :], t1t[:, :], AF.Exp),
               reads=["t1t"], writes=["zt"], fd=TC)
        em.add("act", lambda b: b.activation(t1t[:, :], zt[:, :], AF.Ln, bias=1.0),
               reads=["zt"], writes=["t1t"], fd=TC)
        em.add("dve", lambda b: b.tensor_add(dtt[:, :], mt[:, :], t1t[:, :]),
               reads=["mt", "t1t"], writes=["dtt"], fd=TC)
        em.add("dve", lambda b, j=j: b.tensor_mul(dxt[:, :], dtt[:, :], xi2[:, j, :]),
               reads=["dtt", "xi2"], writes=["dxt"], fd=TC)

    for l in range(L):
        xa_in, xa_out = xaping[l % 2], xaping[(l + 1) % 2]
        ki, ko = f"dram_xa{l % 2}", f"dram_xa{(l + 1) % 2}"
        dma("dma_pool", xpT[:, :, :], wxp[l].rearrange("a p n -> p a n"), [], ["xpT"])
        for sq in range(BC):
            for c in range(NCH):
                t0 = sq * S + c * TC
                dma("dma_sp", xat[:, :, :], tok_slice(xa_in, t0, TC), [ki], ["xat"])
                for i in range(8):
                    ln_feed(xat[:, i, :], "xat", i, 8, TC)
                ln_rows(TC, CV_INVH, CV_EPSA)
                for i in range(8):
                    ln_center(xat[:, i, :], "xat", TC)
                    em.add("act", lambda b, i=i: b.activation(
                        xabt[:, i, :], ln_t1[:, :TC], AF.Identity,
                        scale=smap("ln_g", l * 8 + i), bias=smap("ln_b", l * 8 + i)),
                        reads=["ln_t1"], writes=["xabt"], sreads=["smal"], fd=TC)
                if debug and l == 0:
                    dma("dma_sp", dbg_xl[:, t0:t0 + TC].rearrange("(a p) t -> p a t", p=128),
                        xabt[:, :, :], ["xabt"], ["dbg_xl"])
                for j in range(32):
                    sl = j % 2
                    dma("dma_pool", wst[:, sl, :, :],
                        win[l, j].rearrange("a p n -> p a n"), [], [f"wst{sl}"])
                    bank = PB_MAIN[j % 2]
                    for k in range(8):
                        em.add("pe", lambda b, k=k, sl=sl, bank=bank, s=(k == 0),
                               e=(k == 7): b.matmul(
                                   PB(bank)[:, :TC], wst[:, sl, k, :],
                                   xabt[:, k, :], start=s, stop=e),
                               reads=[f"wst{sl}", "xabt"], writes=[PBK[bank]], fd=TC)
                    if j < 16:
                        em.add("act", lambda b, j=j, bank=bank: b.activation(
                            xi0[:, j, 3:3 + TC], PB(bank)[:, :TC], AF.Silu),
                            reads=[PBK[bank]], writes=["xi0"], fd=TC)
                        conv_block(l, c, j)
                    else:
                        em.add("act", lambda b, j=j, bank=bank: b.activation(
                            sig[:, j - 16, :], PB(bank)[:, :TC], AF.Sigmoid),
                            reads=[PBK[bank]], writes=["sig"], fd=TC)
                for k in range(16):
                    em.add("pe", lambda b, k=k, s=(k == 0), e=(k == 15): b.matmul(
                        PB(PB_MAIN[0])[0:33, :TC], xpT[:, k, :], xi2[:, k, :],
                        start=s, stop=e),
                        reads=["xpT", "xi2"], writes=[PBK[PB_MAIN[0]]], fd=TC)
                em.add("act", lambda b: b.copy(projf[:, :], PB(PB_MAIN[0])[0:33, :TC]),
                       reads=[PBK[PB_MAIN[0]]], writes=["projf"], fd=TC)
                em.add("act", lambda b: b.copy(projb[:, :], projf[:, :]),
                       reads=["projf"], writes=["projb"], fd=TC)
                if debug and l == 0:
                    dma("dma_sp", dbg_xi2[:, t0:t0 + TC].rearrange("(a p) t -> p a t", p=128),
                        xi2[:, :, :], ["xi2"], ["dbg_xi2"])
                    dma("dma_sp", dbg_sig[:, t0:t0 + TC].rearrange("(a p) t -> p a t", p=128),
                        sig[:, :, :], ["sig"], ["dbg_sig"])
                    dma("dma_sp", dbg_proj[:, t0:t0 + TC], projf[:, :], ["projf"], ["dbg_proj"])
                em.add("pe", lambda b: b.matmul(PB(PB_DT)[:, :TC], sel_f0[:, :],
                                                projf[:, :], start=True, stop=True),
                       reads=["sel_f0", "projf"], writes=[PBK[PB_DT]], fd=TC)
                for n in range(NST):
                    bank = PB_MAIN[n % 2]
                    em.add("pe", lambda b, n=n, bank=bank: b.matmul(
                        PB(bank)[:, :TC], sel_bf[:, (1 + n) * 128:(2 + n) * 128],
                        projb[:, :], start=True, stop=True),
                        reads=["sel_bf", "projb"], writes=[PBK[bank]], fd=TC)
                    em.add("act", lambda b, n=n, bank=bank: b.copy(
                        brepb[:, n, :], PB(bank)[:, :TC]),
                        reads=[PBK[bank]], writes=["brepb"], fd=TC)
                    em.add("pe", lambda b, n=n, bank=bank: b.matmul(
                        PB(bank)[:, :TC], sel_bf[:, (17 + n) * 128:(18 + n) * 128],
                        projb[:, :], start=True, stop=True),
                        reads=["sel_bf", "projb"], writes=[PBK[bank]], fd=TC)
                    em.add("act", lambda b, n=n, bank=bank: b.copy(
                        crepb[:, n, :], PB(bank)[:, :TC]),
                        reads=[PBK[bank]], writes=["crepb"], fd=TC)
                for j in range(16):
                    dt_chain(l, j)
                    for g in range(4):
                        hp, hk = hg[g % 2], f"hg{g % 2}"
                        for nn in range(4):
                            n = g * 4 + nn
                            em.add("dve", lambda b, n=n: b.tensor_scalar(
                                a_t[:, :], dtt[:, :], -(float(n + 1)), 1.0,
                                ALU.mult, ALU.add),
                                reads=["dtt"], writes=["a_t"], fd=TC)
                            em.add("dve", lambda b, n=n: b.tensor_mul(
                                b_t[:, :], dxt[:, :], brepb[:, n, :]),
                                reads=["dxt", "brepb"], writes=["b_t"], fd=TC)
                            if c == 0:
                                init, sr = 0.0, []
                            else:
                                init = carry[:, j * 16 + n: j * 16 + n + 1]
                                sr = ["carry"]
                            em.add("dve", lambda b, nn=nn, hp=hp, init=init:
                                   b.tensor_tensor_scan(
                                       hp[:, nn, :], a_t[:, :], b_t[:, :], init,
                                       ALU.mult, ALU.add),
                                   reads=["a_t", "b_t"], writes=[hk], sreads=sr, fd=TC)
                            yp, yk = yht[nn % 2], f"yht{nn % 2}"
                            em.add("pool", lambda b, nn=nn, hp=hp, yp=yp, n=n:
                                   b.tensor_mul(yp[:, :], hp[:, nn, :],
                                                crepb[:, n, :]),
                                   reads=[hk, "crepb"], writes=[yk], fd=TC)
                            em.add("pe", lambda b, yp=yp, s=(n == 0), e=(n == NST - 1):
                                   b.matmul(PB(PB_Y)[:, :TC], id_bf[:, :], yp[:, :],
                                            start=s, stop=e),
                                   reads=[yk, "id_bf"], writes=[PBK[PB_Y]], fd=TC)
                        em.add("dve", lambda b, j=j, g=g, hp=hp: b.tensor_copy(
                            carry[:, j * 16 + g * 4: j * 16 + g * 4 + 4],
                            hp[:, :, TC - 1]),
                            reads=[hk], writes=["carry"], fd=4)
                    em.add("dve", lambda b, j=j: b.scalar_tensor_tensor(
                        y32[:, j, :], xi2[:, j, :], smap("Dp", l * 16 + j),
                        PB(PB_Y)[:, :TC], ALU.mult, ALU.add),
                        reads=["xi2", PBK[PB_Y]], writes=["y32"], sreads=["smal"], fd=TC)
                    ln_feed(y32[:, j, :], "y32", j, 16, TC)
                if debug and l == 0:
                    dma("dma_sp", dbg_y[:, t0:t0 + TC].rearrange("(a p) t -> p a t", p=128),
                        y32[:, :, :], ["y32"], ["dbg_y"])
                ln_rows(TC, CV_INVDI, CV_EPSA)
                for j in range(16):
                    ln_center(y32[:, j, :], "y32", TC)
                    em.add("act", lambda b, j=j: b.activation(
                        cv1t[:, :TC], ln_t1[:, :TC], AF.Identity,
                        scale=smap("mln_g", l * 16 + j), bias=smap("mln_b", l * 16 + j)),
                        reads=["ln_t1"], writes=["cv1t"], sreads=["smal"], fd=TC)
                    em.add("dve", lambda b, j=j: b.tensor_mul(
                        ygb[:, j, :], cv1t[:, :TC], sig[:, j, :]),
                        reads=["cv1t", "sig"], writes=["ygb"], fd=TC)
                for j in range(8):
                    dma("dma_pool", wost[:, :, :],
                        wout[l, j].rearrange("a p n -> p a n"), [], ["wost"])
                    bank = PB_MAIN[j % 2]
                    for k in range(16):
                        em.add("pe", lambda b, k=k, bank=bank, s=(k == 0), e=(k == 15):
                               b.matmul(PB(bank)[:, :TC], wost[:, k, :],
                                        ygb[:, k, :], start=s, stop=e),
                               reads=["wost", "ygb"], writes=[PBK[bank]], fd=TC)
                    em.add("dve", lambda b, j=j, bank=bank: b.tensor_add(
                        xat[:, j, :], PB(bank)[:, :TC], xat[:, j, :]),
                        reads=[PBK[bank], "xat"], writes=["xat"], fd=TC)
                dma("dma_sp", tok_slice(xa_out, t0, TC), xat[:, :, :], ["xat"], [ko])

    em.barrier()
    sta.close()

    stb = ExitStack()
    set_psum(stb, MC)
    op1T = sballoc(stb, "op1T", [128, 8, 1024], BF16)
    op2T = sballoc(stb, "op2T", [128, 8, 1024], BF16)
    mx32 = sballoc(stb, "mx32", [128, 8, MC])
    mxb = sballoc(stb, "mxb", [128, 8, MC], BF16)
    g1b = sballoc(stb, "g1b", [128, 8, MC], BF16)
    mst = sballoc(stb, "mst", [128, 8, MC])
    dma("dma_pool", op1T[:, :, :], wop1.rearrange("a p n -> p a n"), [], ["op1T"])
    dma("dma_pool", op2T[:, :, :], wop2.rearrange("a p n -> p a n"), [], ["op2T"])
    xa_fin = xaping[L % 2]
    kfin = f"dram_xa{L % 2}"
    for mc_i in range(NMC):
        t0 = mc_i * MC
        dma("dma_sp", mx32[:, :, :], tok_slice(xa_fin, t0, MC), [kfin], ["mx32"])
        for i in range(8):
            em.add("dve", lambda b, i=i: b.tensor_copy(mxb[:, i, :], mx32[:, i, :]),
                   reads=["mx32"], writes=["mxb"], fd=MC)
        for j in range(8):
            bank = PB_MAIN[j % 2]
            for k in range(8):
                em.add("pe", lambda b, j=j, k=k, bank=bank, s=(k == 0), e=(k == 7):
                       b.matmul(PB(bank)[:, :MC], op1T[:, k, j * 128:(j + 1) * 128],
                                mxb[:, k, :], start=s, stop=e),
                       reads=["op1T", "mxb"], writes=[PBK[bank]], fd=MC)
            em.add("act", lambda b, j=j, bank=bank: b.activation(
                g1b[:, j, :], PB(bank)[:, :MC], AF.Gelu, bias=smap("op1_b", j)),
                reads=[PBK[bank]], writes=["g1b"], sreads=["smal"], fd=MC)
        dma("dma_sp", mx32[:, :, :], tok_slice(xn32, t0, MC), ["dram_xn32"], ["mx32"])
        for j in range(8):
            bank = PB_MAIN[j % 2]
            for k in range(8):
                em.add("pe", lambda b, j=j, k=k, bank=bank, s=(k == 0), e=(k == 7):
                       b.matmul(PB(bank)[:, :MC], op2T[:, k, j * 128:(j + 1) * 128],
                                g1b[:, k, :], start=s, stop=e),
                       reads=["op2T", "g1b"], writes=[PBK[bank]], fd=MC)
            em.add("dve", lambda b, j=j, bank=bank: b.scalar_tensor_tensor(
                mst[:, j, :], PB(bank)[:, :MC], smap("op2_b", j), mx32[:, j, :],
                ALU.add, ALU.add),
                reads=[PBK[bank], "mx32"], writes=["mst"], sreads=["smal"], fd=MC)
        dma("dma_sp", tok_slice(s32, t0, MC), mst[:, :, :], ["mst"], ["dram_s32"])

    em.barrier()
    stb.close()

    stc = ExitStack()
    set_psum(stc, MC)
    gfm = sballoc(stc, "gfm", [8, TOK], BF16)
    sel8 = sballoc(stc, "sel8", [8, 8 * 128], BF16)
    dma("dma_pool", sel8[:, :], selp[0:8, 0:8 * 128], [], ["sel8"])
    std = ExitStack()
    rtb = sballoc(std, "rtb", [128, 8, 8])
    xng = sballoc(std, "xng", [128, 8, 128])
    ltok = sballoc(std, "ltok", [128, 8])
    gv1 = sballoc(std, "gv1", [128, 1])
    gm1 = sballoc(std, "gm1", [128, 8])
    gl2 = sballoc(std, "gl2", [128, 8])
    gv2 = sballoc(std, "gv2", [128, 1])
    gm2 = sballoc(std, "gm2", [128, 8])
    gdd = sballoc(std, "gdd", [128, 1])
    gee = sballoc(std, "gee", [128, 1])
    gsv = sballoc(std, "gsv", [128, 1])
    gw2 = sballoc(std, "gw2", [128, 1])
    ggt = sballoc(std, "ggt", [128, 8])
    dma("dma_sp", rtb[:, :, :], wrt.rearrange("a p n -> p a n"), [], ["rtb"])
    for tg in range(TOK // 128):
        t0 = tg * 128
        dma("dma_sp", xng[:, :, :], tok_slice(xn32, t0, 128), ["dram_xn32"], ["xng"])
        for k in range(8):
            em.add("pe", lambda b, k=k, s=(k == 0), e=(k == 7): b.matmul(
                PB(PB_MAIN[0])[:, 0:8], xng[:, k, :], rtb[:, k, :], start=s, stop=e),
                reads=["xng", "rtb"], writes=[PBK[PB_MAIN[0]]], fd=8)
        em.add("act", lambda b: b.copy(ltok[:, :], PB(PB_MAIN[0])[:, 0:8]),
               reads=[PBK[PB_MAIN[0]]], writes=["ltok"], fd=8)
        em.add("dve", lambda b: b.reduce_max(gv1[:, :], ltok[:, :], axis=AX.X),
               reads=["ltok"], writes=["gv1"], fd=8)
        em.add("dve", lambda b: b.tensor_scalar(gm1[:, :], ltok[:, :], gv1[:, :],
                                                None, ALU.is_ge),
               reads=["ltok"], writes=["gm1"], sreads=["gv1"], fd=8)
        em.add("dve", lambda b: b.scalar_tensor_tensor(gl2[:, :], gm1[:, :], -1e30,
                                                       ltok[:, :], ALU.mult, ALU.add),
               reads=["gm1", "ltok"], writes=["gl2"], fd=8)
        em.add("dve", lambda b: b.reduce_max(gv2[:, :], gl2[:, :], axis=AX.X),
               reads=["gl2"], writes=["gv2"], fd=8)
        em.add("dve", lambda b: b.tensor_scalar(gm2[:, :], gl2[:, :], gv2[:, :],
                                                None, ALU.is_ge),
               reads=["gl2"], writes=["gm2"], sreads=["gv2"], fd=8)
        em.add("dve", lambda b: b.tensor_sub(gdd[:, :], gv2[:, :], gv1[:, :]),
               reads=["gv2", "gv1"], writes=["gdd"], fd=1)
        em.add("act", lambda b: b.activation(gee[:, :], gdd[:, :], AF.Exp),
               reads=["gdd"], writes=["gee"], fd=1)
        em.add("dve", lambda b: b.tensor_scalar_add(gsv[:, :], gee[:, :], 1.0),
               reads=["gee"], writes=["gsv"], fd=1)
        em.add("dve", lambda b: b.reciprocal(gsv[:, :], gsv[:, :]),
               reads=["gsv"], writes=["gsv"], fd=1)
        em.add("dve", lambda b: b.tensor_mul(gw2[:, :], gee[:, :], gsv[:, :]),
               reads=["gee", "gsv"], writes=["gw2"], fd=1)
        em.add("dve", lambda b: b.tensor_scalar(ggt[:, :], gm1[:, :], gsv[:, :],
                                                None, ALU.mult),
               reads=["gm1"], writes=["ggt"], sreads=["gsv"], fd=8)
        em.add("dve", lambda b: b.scalar_tensor_tensor(ggt[:, :], gm2[:, :], gw2[:, :],
                                                       ggt[:, :], ALU.mult, ALU.add),
               reads=["gm2", "ggt"], writes=["ggt"], sreads=["gw2"], fd=8)
        em.add("pe", lambda b: b.transpose(PB(PB_MAIN[1])[0:8, 0:128], ggt[:, :],
                                           ident[:, :]),
               reads=["ggt", "ident"], writes=[PBK[PB_MAIN[1]]], fd=128)
        em.add("act", lambda b, t0=t0: b.copy(gfm[:, t0:t0 + 128],
                                              PB(PB_MAIN[1])[0:8, 0:128]),
               reads=[PBK[PB_MAIN[1]]], writes=["gfm"], fd=128)

    em.barrier()
    std.close()

    ste = ExitStack()
    w1h = sballoc(ste, "w1h", [128, 8, DFF // 2], BF16)
    w2h = sballoc(ste, "w2h", [128, 16, H], BF16)
    exb = sballoc(ste, "exb", [128, 8, MC], BF16)
    ex32 = sballoc(ste, "ex32", [128, 8, MC])
    h1h = sballoc(ste, "h1h", [128, 16, MC], BF16)
    rest = sballoc(ste, "rest", [128, 8, MC])
    ptile = sballoc(ste, "ptile", [128, 8, MC])
    for e in range(NEXP):
        for hf in range(2):
            f0 = hf * 16
            for a in range(8):
                dma("dma_pool", w1h[:, a, :],
                    w1e[e, a, :, hf * (DFF // 2):(hf + 1) * (DFF // 2)], [], ["w1h"])
            for a in range(16):
                dma("dma_pool", w2h[:, a, :], w2e[e, f0 + a], [], ["w2h"])
            for mc_i in range(NMC):
                t0 = mc_i * MC
                dma("dma_sp", exb[:, :, :], tok_slice(xnbf, t0, MC),
                    ["dram_xnbf"], ["exb"])
                for f in range(16):
                    bank = PB_MAIN[f % 2]
                    for k in range(8):
                        em.add("pe", lambda b, f=f, k=k, bank=bank, s=(k == 0),
                               e_=(k == 7): b.matmul(
                                   PB(bank)[:, :MC], w1h[:, k, f * 128:(f + 1) * 128],
                                   exb[:, k, :], start=s, stop=e_),
                               reads=["w1h", "exb"], writes=[PBK[bank]], fd=MC)
                    em.add("act", lambda b, f=f, e=e, f0=f0, bank=bank: b.activation(
                        h1h[:, f, :], PB(bank)[:, :MC], AF.Gelu,
                        bias=smap("b1", e * 32 + f0 + f)),
                        reads=[PBK[bank]], writes=["h1h"], sreads=["smal"], fd=MC)
                if hf == 1:
                    dma("dma_act", ptile[:, :, :], tok_slice(pdram, t0, MC),
                        ["dram_pdram"], ["ptile"])
                    dma("dma_act", ex32[:, :, :], tok_slice(xn32, t0, MC),
                        ["dram_xn32"], ["ex32"])
                    em.add("pe", lambda b, e=e, t0=t0: b.matmul(
                        PB(PB_DT)[:, :MC], sel8[:, e * 128:(e + 1) * 128],
                        gfm[:, t0:t0 + MC], start=True, stop=True),
                        reads=["gfm", "sel8"], writes=[PBK[PB_DT]], fd=MC)
                for dj in range(8):
                    bank = PB_MAIN[dj % 2]
                    for k in range(16):
                        em.add("pe", lambda b, dj=dj, k=k, bank=bank, s=(k == 0),
                               e_=(k == 15): b.matmul(
                                   PB(bank)[:, :MC], w2h[:, k, dj * 128:(dj + 1) * 128],
                                   h1h[:, k, :], start=s, stop=e_),
                               reads=["w2h", "h1h"], writes=[PBK[bank]], fd=MC)
                    if hf == 0:
                        em.add("act", lambda b, dj=dj, bank=bank: b.copy(
                            ptile[:, dj, :], PB(bank)[:, :MC]),
                            reads=[PBK[bank]], writes=["ptile"], fd=MC)
                    else:
                        em.add("dve", lambda b, dj=dj, bank=bank: b.tensor_add(
                            ptile[:, dj, :], PB(bank)[:, :MC], ptile[:, dj, :]),
                            reads=[PBK[bank], "ptile"], writes=["ptile"], fd=MC)
                        em.add("dve", lambda b, dj=dj, e=e: b.scalar_tensor_tensor(
                            rest[:, dj, :], ptile[:, dj, :], smap("b2", e * 8 + dj),
                            ex32[:, dj, :], ALU.add, ALU.add),
                            reads=["ptile", "ex32"], writes=["rest"],
                            sreads=["smal"], fd=MC)
                if hf == 0:
                    dma("dma_act", tok_slice(pdram, t0, MC), ptile[:, :, :],
                        ["ptile"], ["dram_pdram"])
                    continue
                for i in range(8):
                    ln_feed(rest[:, i, :], "rest", i, 8, MC)
                ln_rows(MC, CV_INVH, CV_EPSM)
                for dj in range(8):
                    ln_center(rest[:, dj, :], "rest", MC)
                    em.add("act", lambda b, dj=dj, e=e: b.activation(
                        cv1t[:, :MC], ln_t1[:, :MC], AF.Identity,
                        scale=smap("gamma", e * 8 + dj), bias=smap("beta", e * 8 + dj)),
                        reads=["ln_t1"], writes=["cv1t"], sreads=["smal"], fd=MC)
                    em.add("dve", lambda b, dj=dj: b.tensor_mul(
                        rest[:, dj, :], cv1t[:, :MC], PB(PB_DT)[:, :MC]),
                        reads=["cv1t", PBK[PB_DT]], writes=["rest"], fd=MC)
                if e == 0:
                    dma("dma_pool", tok_slice(ymoe, t0, MC), rest[:, :, :],
                        ["rest"], ["dram_ymoe"])
                else:
                    dma_acc(tok_slice(ymoe, t0, MC), rest[:, :, :],
                            ["rest"], ["dram_ymoe"])

    em.barrier()
    ste.close()
    stc.close()

    stf = ExitStack()
    cs = sballoc(stf, "cs", [128, 8, MC])
    cy = sballoc(stf, "cy", [128, 8, MC])
    cx = sballoc(stf, "cx", [128, 8, MC])
    co = sballoc(stf, "co", [128, 8, MC])
    for mc_i in range(NMC):
        t0 = mc_i * MC
        dma("dma_sp", cs[:, :, :], tok_slice(s32, t0, MC), ["dram_s32"], ["cs"])
        dma("dma_act", cy[:, :, :], tok_slice(ymoe, t0, MC), ["dram_ymoe"], ["cy"])
        dma("dma_sp", cx[:, :, :], tok_slice(x0, t0, MC), ["dram_x0"], ["cx"])
        for i in range(8):
            em.add("act", lambda b, i=i: b.activation(cy[:, i, :], cy[:, i, :], AF.Silu),
                   reads=["cy"], writes=["cy"], fd=MC)
            em.add("dve", lambda b, i=i: b.tensor_mul(co[:, i, :], cs[:, i, :],
                                                      cy[:, i, :]),
                   reads=["cs", "cy"], writes=["co"], fd=MC)
            em.add("dve", lambda b, i=i: b.tensor_add(co[:, i, :], co[:, i, :],
                                                      cx[:, i, :]),
                   reads=["co", "cx"], writes=["co"], fd=MC)
        dma("dma_sp", tok_slice(out_fm, t0, MC), co[:, :, :], ["co"], ["dram_out"])

    em.barrier()
    em.emit(stx)
    stf.close()
    stx.close()
    return nc



def _pack_smalls(ip):
    sm = np.zeros((128, SM_COLS), np.float32)

    def put(name, vec, nblk, idx0=0):
        v = np.asarray(vec, np.float32).reshape(nblk, 128).T
        sm[:, SM[name] + idx0:SM[name] + idx0 + nblk] = v
    put("in_norm_g", ip["L0_attn_in_norm_g"], 8)
    put("in_norm_b", ip["L0_attn_in_norm_b"], 8)
    for l in range(L):
        put("ln_g", ip["L0_attn_ln_g"][l], 8, l * 8)
        put("ln_b", ip["L0_attn_ln_b"][l], 8, l * 8)
        for ci in range(2):
            for k in range(KCONV):
                put("conv_w", ip["L0_attn_conv_w"][l, ci, :, k], 16,
                    l * (2 * KCONV * 16) + ci * (KCONV * 16) + k * 16)
            put("conv_b", ip["L0_attn_conv_b"][l, ci], 16, l * 32 + ci * 16)
        put("dt_w", ip["L0_attn_dt_proj_w"][l, :, 0], 16, l * 16)
        put("dt_b", ip["L0_attn_dt_proj_b"][l], 16, l * 16)
        put("Dp", ip["L0_attn_D"][l], 16, l * 16)
        put("mln_g", ip["L0_attn_mamba_ln_g"][l], 16, l * 16)
        put("mln_b", ip["L0_attn_mamba_ln_b"][l], 16, l * 16)
    put("op1_b", ip["L0_attn_op1_b"], 8)
    put("op2_b", ip["L0_attn_op2_b"], 8)
    for e in range(NEXP):
        put("b1", ip["L0_ffn_b1"][e], 32, e * 32)
        put("b2", ip["L0_ffn_b2"][e], 8, e * 8)
        put("gamma", ip["L0_ffn_gamma"][e], 8, e * 8)
        put("beta", ip["L0_ffn_beta"][e], 8, e * 8)
    return sm


def _host_shared(ip):
    f = np.float32
    arrs = {}
    w = np.asarray(ip["L0_attn_in_proj_w"], f)
    arrs["win"] = np.ascontiguousarray(
        w.reshape(L, 32, 128, 8, 128).transpose(0, 1, 3, 4, 2))
    w = np.asarray(ip["L0_attn_out_proj_w"], f)
    arrs["wout"] = np.ascontiguousarray(
        w.reshape(L, 8, 128, 16, 128).transpose(0, 1, 3, 4, 2))
    w = np.asarray(ip["L0_attn_x_proj_w"], f)
    arrs["wxp"] = np.ascontiguousarray(w.transpose(0, 2, 1).reshape(L, 16, 128, 33))
    arrs["wop1"] = np.ascontiguousarray(
        np.asarray(ip["L0_attn_op1_w"], f).T.reshape(8, 128, 1024))
    arrs["wop2"] = np.ascontiguousarray(
        np.asarray(ip["L0_attn_op2_w"], f).T.reshape(8, 128, 1024))
    arrs["wrt"] = np.ascontiguousarray(
        np.asarray(ip["L0_ffn_router_w"], f).reshape(8, 128, 8))
    arrs["w1e"] = np.ascontiguousarray(
        np.asarray(ip["L0_ffn_w1"], f).reshape(NEXP, 8, 128, DFF))
    arrs["w2e"] = np.ascontiguousarray(
        np.asarray(ip["L0_ffn_w2"], f).reshape(NEXP, 32, 128, H))
    arrs["smalls"] = _pack_smalls(ip)
    cvv = np.zeros((128, NCONST), f)
    cvv[:, CV_ONE] = 1.0
    cvv[:, CV_EPSA] = EPS_A
    cvv[:, CV_EPSM] = EPS_M
    cvv[:, CV_NH] = -0.5
    cvv[:, CV_INVH] = 1.0 / H
    cvv[:, CV_INVDI] = 1.0 / DI
    arrs["consts"] = cvv
    sel = np.zeros((33, 34 * 128), f)
    for i in range(33):
        sel[i, i * 128:(i + 1) * 128] = 1.0
    arrs["selp"] = sel
    arrs["idmat"] = np.eye(128, dtype=f)
    arrs["onescol"] = np.ones((128, 1), f)
    arrs["onesrow"] = np.ones((1, 128), f)
    return arrs


def kernel(**inputs):
    hs = np.asarray(inputs["L0_attn_hidden_states"], np.float32)
    shared = _host_shared(inputs)
    nc = build_program(debug=False)
    in_maps = []
    for c in range(NCORE):
        m = dict(shared)
        xc = hs[c * BC:(c + 1) * BC]
        m["x0"] = np.ascontiguousarray(xc.transpose(2, 0, 1).reshape(H, TOK))
        in_maps.append(m)
    res = run_bass_kernel_spmd(nc, in_maps, core_ids=list(range(NCORE)))
    outs = []
    for c in range(NCORE):
        o = np.asarray(res.results[c]["out_fm"])
        outs.append(o.reshape(H, BC, S).transpose(1, 2, 0))
    return np.concatenate(outs, axis=0).astype(np.float32)


if __name__ == "__main__":
    import time
    t = time.time()
    nc = build_program()
    print(f"built in {time.time() - t:.1f}s")


# version 03
# speedup vs baseline: 1.0652x; 1.0652x over previous
import numpy as np
from contextlib import ExitStack

import concourse.bass as bass
import concourse.mybir as mybir
from concourse.bass_utils import run_bass_kernel_spmd

F32 = mybir.dt.float32
BF16 = mybir.dt.bfloat16
AF = mybir.ActivationFunctionType
ALU = mybir.AluOpType
AX = mybir.AxisListType

B_TOT, S, H, L, DI, NST, KCONV = 16, 2048, 1024, 4, 2048, 16, 4
DFF, NEXP, NCORE = 4096, 8, 8
BC = B_TOT // NCORE
TOK = BC * S
TC = 256
NCH = S // TC
MC = 512
NMC = TOK // MC
EPS_A, EPS_M = 1e-5, 1e-6
FIFO_SAFE_FD = 128

ENGINES = ("pe", "dve", "act", "pool", "sp")
DMAQ = ("dma_sp", "dma_act", "dma_pool")


def _sm_layout():
    cols, c = {}, 0

    def take(name, n):
        nonlocal c
        cols[name] = c
        c += n
    take("in_norm_g", 8); take("in_norm_b", 8)
    take("ln_g", L * 8); take("ln_b", L * 8)
    take("conv_w", L * 2 * KCONV * 16); take("conv_b", L * 2 * 16)
    take("dt_w", L * 16); take("dt_b", L * 16)
    take("Dp", L * 16)
    take("mln_g", L * 16); take("mln_b", L * 16)
    take("op1_b", 8); take("op2_b", 8)
    take("b1", NEXP * 32); take("b2", NEXP * 8)
    take("gamma", NEXP * 8); take("beta", NEXP * 8)
    return cols, c


SM, SM_COLS = _sm_layout()
CV_ONE, CV_EPSA, CV_EPSM, CV_NH, CV_INVH, CV_INVDI = 0, 1, 2, 3, 4, 5
NCONST = 6


class Op:
    __slots__ = ("eng", "fn", "reads", "writes", "sreads", "fd", "idx",
                 "inc_count", "waits", "dma", "meta")

    def __init__(self, eng, fn, reads, writes, sreads, fd, idx, dma):
        self.eng, self.fn = eng, fn
        self.reads, self.writes, self.sreads = list(reads), list(writes), list(sreads)
        self.fd, self.idx, self.dma = fd, idx, dma
        self.inc_count, self.waits = None, []
        self.meta = None


DMA_SLOTS = {"dma_sp": 1, "dma_act": 1, "dma_pool": 8}


class Emitter:

    def __init__(self, nc):
        self.nc = nc
        self.ops = []
        self.last_writer = {}
        self.readers = {}
        self.counts = {}
        for e in ENGINES:
            self.counts[e] = 0
        self.dma_qkeys = []
        for q, ns in DMA_SLOTS.items():
            for s in range(ns):
                k = f"{q}#{s}"
                self.counts[k] = 0
                self.dma_qkeys.append(k)
        self.dma_rr = {q: 0 for q in DMA_SLOTS}
        self.last_dma_on_slot = {}
        self.meta_provider = None
        self.meta_apply = None

    @staticmethod
    def _same_stream(a, b):
        if a.dma is not None or b.dma is not None:
            return False
        return a.eng == b.eng

    def _need_edge(self, prod, cons, scalar=False):
        if prod is None:
            return False
        if scalar:
            return True
        if self._same_stream(prod, cons):
            if prod.eng == "pe":
                return False
            return prod.fd < FIFO_SAFE_FD or cons.fd < FIFO_SAFE_FD
        return True

    def _assign_inc(self, d):
        if d.inc_count is None:
            d.inc_count = -1
            q = d.dma if d.dma is not None else d.eng
            self.counts[q] += 16 if d.dma is not None else 1

    def add(self, eng, fn, reads=(), writes=(), sreads=(), fd=512, dma=None):
        if dma is not None:
            slot = self.dma_rr[dma]
            self.dma_rr[dma] = (slot + 1) % DMA_SLOTS[dma]
            qkey = f"{dma}#{slot}"
        else:
            qkey = None
        op = Op(eng, fn, reads, writes, sreads, fd, len(self.ops), qkey)
        if self.meta_provider is not None:
            op.meta = self.meta_provider()
        deps = set()
        for b in op.reads:
            w = self.last_writer.get(b)
            if self._need_edge(w, op):
                deps.add(w)
        for b in op.sreads:
            w = self.last_writer.get(b)
            if self._need_edge(w, op, scalar=True):
                deps.add(w)
        for b in op.writes:
            w = self.last_writer.get(b)
            if self._need_edge(w, op):
                deps.add(w)
            for r in self.readers.get(b, ()):
                if r is not op and self._need_edge(r, op):
                    deps.add(r)
        by_q = {}
        for d in deps:
            q = d.dma if d.dma is not None else d.eng
            if d.dma is not None:
                by_q[(q, d.idx)] = d
            else:
                cur = by_q.get(q)
                if cur is None or cur.idx < d.idx:
                    by_q[q] = d
        if qkey is not None:
            prev = self.last_dma_on_slot.get(qkey)
            if prev is not None:
                op.waits.append(prev)
            self.last_dma_on_slot[qkey] = op
        for d in sorted(by_q.values(), key=lambda o: o.idx):
            self._assign_inc(d)
            op.waits.append(d)
        self.ops.append(op)
        if qkey is not None:
            self._assign_inc(op)
        for b in op.reads + op.sreads:
            self.readers.setdefault(b, []).append(op)
        for b in op.writes:
            self.last_writer[b] = op
            self.readers[b] = []
        return op

    def barrier(self):
        last_by_q = {}
        for o in self.ops:
            q = o.dma if o.dma is not None else o.eng
            last_by_q[q] = o
        for q, o in last_by_q.items():
            self._assign_inc(o)
        for eng in ENGINES:
            op = Op(eng, lambda b: b.nop(), [], [], [], 1, len(self.ops), None)
            for q, o in last_by_q.items():
                if q != eng:
                    op.waits.append(o)
            self.ops.append(op)

    def emit(self, st):
        nc = self.nc
        allq = list(ENGINES) + self.dma_qkeys
        ordinals = {q: 0 for q in allq}
        for op in self.ops:
            if op.inc_count is not None:
                q = op.dma if op.dma is not None else op.eng
                ordinals[q] += 16 if op.dma is not None else 1
                op.inc_count = ordinals[q]
        for q in allq:
            assert ordinals[q] < 60000, f"sem overflow {q}: {ordinals[q]}"
        sems = {q: st.enter_context(nc.semaphore(f"sem_{q.replace('#','_')}"))
                for q in allq}
        block = st.enter_context(nc.Block())
        per_eng = {e: [] for e in ENGINES}
        for op in self.ops:
            per_eng[op.eng].append(op)
        streams = {"pe": block.tensor, "dve": block.vector, "act": block.scalar,
                   "pool": block.gpsimd, "sp": block.sync}

        for eng in ENGINES:
            ops = per_eng[eng]

            def make(ops=ops):
                def body(b):
                    clock = {q: 0 for q in allq}
                    for op in ops:
                        for d in op.waits:
                            q = d.dma if d.dma is not None else d.eng
                            v = d.inc_count
                            if clock[q] < v:
                                b.wait_ge(sems[q], v)
                                clock[q] = v
                        if op.meta is not None and self.meta_apply is not None:
                            self.meta_apply(op.meta)
                        inst = op.fn(b)
                        if op.inc_count is not None:
                            q = op.dma if op.dma is not None else op.eng
                            step = 16 if op.dma is not None else 1
                            inst.then_inc(sems[q], step)
                return body
            streams[eng](make())


def build_program(debug=False):
    nc = bass.Bass("TRN2", target_bir_lowering=False, detect_race_conditions=False)
    em = Emitter(nc)
    stx = ExitStack()

    def din(name, shape, dt=F32):
        return nc.dram_tensor(name, list(shape), dt, kind="ExternalInput").ap()

    def dout(name, shape, dt=F32):
        return nc.dram_tensor(name, list(shape), dt, kind="ExternalOutput").ap()

    def dint(name, shape, dt=F32):
        kind = "ExternalOutput" if debug else "Internal"
        return nc.dram_tensor(name, list(shape), dt, kind=kind).ap()

    x0 = din("x0", [H, TOK])
    win = din("win", [L, 32, 8, 128, 128])
    wout = din("wout", [L, 8, 16, 128, 128])
    wxp = din("wxp", [L, 16, 128, 33])
    wop1 = din("wop1", [8, 128, 1024])
    wop2 = din("wop2", [8, 128, 1024])
    wrt = din("wrt", [8, 128, 8])
    w1e = din("w1e", [NEXP, 8, 128, DFF])
    w2e = din("w2e", [NEXP, 32, 128, H])
    smalls = din("smalls", [128, SM_COLS])
    consts = din("consts", [128, NCONST])
    selp = din("selp", [33, 34 * 128])
    idmat = din("idmat", [128, 128])
    onescol = din("onescol", [128, 1])
    onesrow = din("onesrow", [1, 128])

    out_fm = dout("out_fm", [H, TOK])
    xn32 = dint("xn32", [H, TOK])
    xnbf = dint("xnbf", [H, TOK], BF16)
    xaping = [dint("xa0", [H, TOK]), dint("xa1", [H, TOK])]
    s32 = dint("s32", [H, TOK])
    ymoe = dint("ymoe", [H, TOK])
    pdram = dint("pdram", [H, TOK])
    if debug:
        dbg_xl = dout("dbg_xl", [H, TOK], BF16)
        dbg_xi2 = dout("dbg_xi2", [DI, TOK])
        dbg_proj = dout("dbg_proj", [33, TOK])
        dbg_y = dout("dbg_y", [DI, TOK])
        dbg_sig = dout("dbg_sig", [DI, TOK])

    def sballoc(st, name, shape, dt=F32):
        return st.enter_context(nc.sbuf_tensor(name, shape, dt))

    cons = sballoc(stx, "cons", [128, NCONST])
    smal = sballoc(stx, "smal", [128, SM_COLS])
    ident = sballoc(stx, "ident", [128, 128])
    id_bf = sballoc(stx, "id_bf", [128, 128], BF16)
    ones_bf = sballoc(stx, "ones_bf", [128, 1], BF16)
    onesrow_f = sballoc(stx, "onesrow_f", [1, 128])
    onesrow_b = sballoc(stx, "onesrow_b", [1, 128], BF16)
    ln_bf = [sballoc(stx, f"ln_bf{i}", [128, MC], BF16) for i in range(2)]
    ln_sq = [sballoc(stx, f"ln_sq{i}", [128, MC], BF16) for i in range(2)]
    ln_t1 = [sballoc(stx, f"ln_t1_{i}", [128, MC]) for i in range(2)]
    cvt = [sballoc(stx, f"cvt{i}", [128, MC]) for i in range(2)]
    rows = sballoc(stx, "rows", [1, 5 * MC])

    def row(i, width):
        return rows[:, i * MC:i * MC + width]
    R_MU, R_M2, R_VAR, R_R, R_MR = range(5)

    _pb_gens = {}
    _pbcur = {"t": None, "gen": 0}

    def set_psum(st_, width):
        _pbcur["gen"] += 1
        g = _pbcur["gen"]
        _pbcur["t"] = [st_.enter_context(
            nc.psum_tensor(f"pb{g}_{i}", [128, width], F32)) for i in range(8)]
        _pb_gens[g] = _pbcur["t"]

    def PB(i):
        return _pbcur["t"][i]

    em.meta_provider = lambda: _pbcur["gen"]
    em.meta_apply = lambda g: (_pbcur.update(t=_pb_gens[g]) if g in _pb_gens else None)

    PBK = [f"pb{i}" for i in range(8)]
    PB_MAIN = (0, 1)
    PB_R, PB_M, PB_Y, PB_DT, PB_S1, PB_S2 = 2, 3, 4, 5, 6, 7

    def dma(queue, out_ap, in_ap, reads, writes):
        eng = {"dma_sp": "sp", "dma_act": "act", "dma_pool": "pool"}[queue]
        em.add(eng, lambda b, o=out_ap, i=in_ap: b.dma_start(o, i),
               reads=reads, writes=writes, fd=512, dma=queue)

    def dma_acc(out_ap, in_ap, reads, writes):
        em.add("pool", lambda b, o=out_ap, i=in_ap: b.dma_start(o, i, accum_op=ALU.add),
               reads=reads, writes=writes, fd=512, dma="dma_pool")

    def cvap(col, p1=False):
        a = cons[:, col:col + 1]
        return a[0:1, :] if p1 else a

    def smap(name, idx):
        c = SM[name] + idx
        return smal[:, c:c + 1]

    dma("dma_sp", cons[:, :], consts, [], ["cons"])
    dma("dma_sp", smal[:, :], smalls, [], ["smal"])
    dma("dma_sp", ident[:, :], idmat, [], ["ident"])
    dma("dma_pool", id_bf[:, :], idmat, [], ["id_bf"])
    dma("dma_pool", ones_bf[:, :], onescol, [], ["ones_bf"])
    dma("dma_sp", onesrow_f[:, :], onesrow, [], ["onesrow_f"])
    dma("dma_pool", onesrow_b[:, :], onesrow, [], ["onesrow_b"])

    def tok_slice(d_ap, t0, w):
        return d_ap[:, t0:t0 + w].rearrange("(a p) t -> p a t", p=128)

    def ln_feed(ap, key, i, nblk, width, s1=PB_S1, s2=PB_S2):
        sl = i % 2
        lb, ls = ln_bf[sl], ln_sq[sl]
        em.add("act", lambda b, a=ap, lb=lb: b.copy(lb[:, :width], a),
               reads=[key], writes=[f"ln_bf{sl}"], fd=width)
        em.add("pe", lambda b, s=(i == 0), e=(i == nblk - 1), lb=lb: b.matmul(
            PB(s1)[0:1, :width], ones_bf[:, :], lb[:, :width], start=s, stop=e),
            reads=[f"ln_bf{sl}", "ones_bf"], writes=[PBK[s1]], fd=width)
        em.add("act", lambda b, a=ap, ls=ls: b.activation(ls[:, :width], a, AF.Square),
               reads=[key], writes=[f"ln_sq{sl}"], fd=width)
        em.add("pe", lambda b, s=(i == 0), e=(i == nblk - 1), ls=ls: b.matmul(
            PB(s2)[0:1, :width], ones_bf[:, :], ls[:, :width], start=s, stop=e),
            reads=[f"ln_sq{sl}", "ones_bf"], writes=[PBK[s2]], fd=width)

    def ln_rows(width, inv_col, eps_col):
        em.add("act", lambda b: b.activation(row(R_MU, width), PB(PB_S1)[0:1, :width],
                                             AF.Identity, scale=cvap(inv_col, True)),
               reads=[PBK[PB_S1]], writes=["rows"], sreads=["cons"], fd=width)
        em.add("act", lambda b: b.activation(row(R_M2, width), PB(PB_S2)[0:1, :width],
                                             AF.Identity, scale=cvap(inv_col, True)),
               reads=[PBK[PB_S2]], writes=["rows"], sreads=["cons"], fd=width)
        em.add("dve", lambda b: b.tensor_mul(row(R_VAR, width), row(R_MU, width),
                                             row(R_MU, width)),
               reads=["rows"], writes=["rows"], fd=width)
        em.add("dve", lambda b: b.tensor_sub(row(R_VAR, width), row(R_M2, width),
                                             row(R_VAR, width)),
               reads=["rows"], writes=["rows"], fd=width)
        em.add("act", lambda b: b.activation(row(R_R, width), row(R_VAR, width), AF.Ln,
                                             bias=cvap(eps_col, True)),
               reads=["rows"], writes=["rows"], sreads=["cons"], fd=width)
        em.add("act", lambda b: b.activation(row(R_R, width), row(R_R, width), AF.Exp,
                                             scale=cvap(CV_NH, True)),
               reads=["rows"], writes=["rows"], sreads=["cons"], fd=width)
        em.add("dve", lambda b: b.tensor_mul(row(R_MR, width), row(R_MU, width),
                                             row(R_R, width)),
               reads=["rows"], writes=["rows"], fd=width)
        em.add("pe", lambda b: b.matmul(PB(PB_R)[:, :width], onesrow_f[:, :],
                                        row(R_R, width), start=True, stop=True),
               reads=["rows", "onesrow_f"], writes=[PBK[PB_R]], fd=width)
        em.add("pe", lambda b: b.matmul(PB(PB_M)[:, :width], onesrow_f[:, :],
                                        row(R_MR, width), start=True, stop=True),
               reads=["rows", "onesrow_f"], writes=[PBK[PB_M]], fd=width)

    def ln_center(ap, key, width, sl=0):
        lt = ln_t1[sl]
        em.add("dve", lambda b, a=ap, lt=lt: b.tensor_mul(lt[:, :width], a,
                                                          PB(PB_R)[:, :width]),
               reads=[key, PBK[PB_R]], writes=[f"ln_t1_{sl}"], fd=width)
        em.add("dve", lambda b, lt=lt: b.tensor_sub(lt[:, :width], lt[:, :width],
                                                    PB(PB_M)[:, :width]),
               reads=[f"ln_t1_{sl}", PBK[PB_M]], writes=[f"ln_t1_{sl}"], fd=width)

    sta = ExitStack()
    set_psum(sta, TC)
    xat = sballoc(sta, "xat", [128, 8, TC])
    xnt = sballoc(sta, "xnt", [128, 8, TC])
    xnbt = sballoc(sta, "xnbt", [128, 8, TC], BF16)
    xabt = sballoc(sta, "xabt", [128, 8, TC], BF16)
    sel_bf = sballoc(sta, "sel_bf", [33, 34 * 128], BF16)
    sel_f0 = sballoc(sta, "sel_f0", [33, 128])
    xpT = sballoc(sta, "xpT", [128, 16, 33])
    wst = sballoc(sta, "wst", [128, 2, 8, 128], BF16)
    wost = sballoc(sta, "wost", [128, 16, 128], BF16)
    xi0 = sballoc(sta, "xi0", [128, 16, TC + 3])
    xi1 = sballoc(sta, "xi1", [128, 16, TC + 3])
    xi2 = sballoc(sta, "xi2", [128, 16, TC])
    sig = sballoc(sta, "sig", [128, 16, TC])
    projf = sballoc(sta, "projf", [33, TC])
    projb = sballoc(sta, "projb", [33, TC], BF16)
    brepb = sballoc(sta, "brepb", [128, 16, TC], BF16)
    crepb = sballoc(sta, "crepb", [128, 16, TC], BF16)
    zt = sballoc(sta, "zt", [128, TC])
    mt = sballoc(sta, "mt", [128, TC])
    t1t = sballoc(sta, "t1t", [128, TC])
    dtt = sballoc(sta, "dtt", [128, TC])
    dxt = sballoc(sta, "dxt", [128, TC])
    a_t = sballoc(sta, "a_t", [128, TC])
    b_t = sballoc(sta, "b_t", [128, TC])
    hg = [sballoc(sta, f"hg{i}", [128, 4, TC]) for i in range(2)]
    yht = [sballoc(sta, f"yht{i}", [128, TC], BF16) for i in range(2)]
    carry = sballoc(sta, "carry", [128, 256])
    y32 = sballoc(sta, "y32", [128, 16, TC])
    ygb = sballoc(sta, "ygb", [128, 16, TC], BF16)

    dma("dma_pool", sel_bf[:, :], selp, [], ["sel_bf"])
    dma("dma_sp", sel_f0[:, :], selp[:, 0:128], [], ["sel_f0"])

    for sq in range(BC):
        for c in range(NCH):
            t0 = sq * S + c * TC
            dma("dma_sp", xat[:, :, :], tok_slice(x0, t0, TC), ["dram_x0"], ["xat"])
            for i in range(8):
                ln_feed(xat[:, i, :], "xat", i, 8, TC)
            ln_rows(TC, CV_INVH, CV_EPSA)
            for i in range(8):
                sl = i % 2
                ln_center(xat[:, i, :], "xat", TC, sl)
                em.add("act", lambda b, i=i, lt=ln_t1[sl]: b.copy(xnt[:, i, :], lt[:, :TC]),
                       reads=[f"ln_t1_{sl}"], writes=["xnt"], fd=TC)
                em.add("act", lambda b, i=i, lt=ln_t1[sl]: b.copy(xnbt[:, i, :], lt[:, :TC]),
                       reads=[f"ln_t1_{sl}"], writes=["xnbt"], fd=TC)
            dma("dma_sp", tok_slice(xn32, t0, TC), xnt[:, :, :], ["xnt"], ["dram_xn32"])
            dma("dma_act", tok_slice(xnbf, t0, TC), xnbt[:, :, :], ["xnbt"], ["dram_xnbf"])
            for i in range(8):
                ln_feed(xnt[:, i, :], "xnt", i, 8, TC)
            ln_rows(TC, CV_INVH, CV_EPSA)
            for i in range(8):
                sl = i % 2
                ln_center(xnt[:, i, :], "xnt", TC, sl)
                em.add("act", lambda b, i=i, lt=ln_t1[sl]: b.activation(
                    xat[:, i, :], lt[:, :TC], AF.Identity,
                    scale=smap("in_norm_g", i), bias=smap("in_norm_b", i)),
                    reads=[f"ln_t1_{sl}"], writes=["xat"], sreads=["smal"], fd=TC)
            dma("dma_sp", tok_slice(xaping[0], t0, TC), xat[:, :, :], ["xat"], ["dram_xa0"])

    def conv_block(l, c, j):
        for (src, srck, ci) in ((xi0, "xi0", 0), (xi1, "xi1", 1)):
            if c == 0:
                em.add("dve", lambda b, src=src, j=j: b.memset(src[:, j, 0:3], 0.0),
                       reads=[], writes=[srck], fd=3)
            wbase = SM["conv_w"] + l * (2 * KCONV * 16) + ci * (KCONV * 16)
            cc, cck = cvt[j % 2], f"cvt{j % 2}"
            em.add("dve", lambda b, src=src, j=j, cc=cc,
                   col=wbase + (KCONV - 1) * 16 + j:
                   b.tensor_scalar(
                       cc[:, :TC], src[:, j, 3:3 + TC], smal[:, col:col + 1],
                       None, ALU.mult),
                   reads=[srck], writes=[cck], sreads=["smal"], fd=TC)
            for k in range(1, KCONV):
                em.add("dve", lambda b, src=src, j=j, k=k, cc=cc,
                       col=wbase + (KCONV - 1 - k) * 16 + j:
                       b.scalar_tensor_tensor(
                           cc[:, :TC], src[:, j, 3 - k:3 - k + TC],
                           smal[:, col:col + 1], cc[:, :TC], ALU.mult, ALU.add),
                       reads=[srck, cck], writes=[cck], sreads=["smal"], fd=TC)
            em.add("dve", lambda b, src=src, j=j: b.tensor_copy(
                src[:, j, 0:3], src[:, j, TC:TC + 3]),
                reads=[srck], writes=[srck], fd=3)
            bcol = SM["conv_b"] + l * 32 + ci * 16 + j
            if ci == 0:
                em.add("act", lambda b, j=j, bcol=bcol, cc=cc: b.activation(
                    xi1[:, j, 3:3 + TC], cc[:, :TC], AF.Silu,
                    bias=smal[:, bcol:bcol + 1]),
                    reads=[cck], writes=["xi1"], sreads=["smal"], fd=TC)
            else:
                em.add("act", lambda b, j=j, bcol=bcol, cc=cc: b.activation(
                    xi2[:, j, :], cc[:, :TC], AF.Silu,
                    bias=smal[:, bcol:bcol + 1]),
                    reads=[cck], writes=["xi2"], sreads=["smal"], fd=TC)

    def dt_chain(l, j):
        em.add("act", lambda b, j=j: b.activation(
            zt[:, :], PB(PB_DT)[:, :TC], AF.Identity,
            scale=smap("dt_w", l * 16 + j), bias=smap("dt_b", l * 16 + j)),
            reads=[PBK[PB_DT]], writes=["zt"], sreads=["smal"], fd=TC)
        em.add("dve", lambda b: b.tensor_scalar_max(mt[:, :], zt[:, :], 0.0),
               reads=["zt"], writes=["mt"], fd=TC)
        em.add("dve", lambda b: b.scalar_tensor_tensor(t1t[:, :], mt[:, :], -2.0,
                                                       zt[:, :], ALU.mult, ALU.add),
               reads=["mt", "zt"], writes=["t1t"], fd=TC)
        em.add("act", lambda b: b.activation(zt[:, :], t1t[:, :], AF.Exp),
               reads=["t1t"], writes=["zt"], fd=TC)
        em.add("act", lambda b: b.activation(t1t[:, :], zt[:, :], AF.Ln, bias=1.0),
               reads=["zt"], writes=["t1t"], fd=TC)
        em.add("dve", lambda b: b.tensor_add(dtt[:, :], mt[:, :], t1t[:, :]),
               reads=["mt", "t1t"], writes=["dtt"], fd=TC)
        em.add("dve", lambda b, j=j: b.tensor_mul(dxt[:, :], dtt[:, :], xi2[:, j, :]),
               reads=["dtt", "xi2"], writes=["dxt"], fd=TC)

    for l in range(L):
        xa_in, xa_out = xaping[l % 2], xaping[(l + 1) % 2]
        ki, ko = f"dram_xa{l % 2}", f"dram_xa{(l + 1) % 2}"
        dma("dma_pool", xpT[:, :, :], wxp[l].rearrange("a p n -> p a n"), [], ["xpT"])
        for sq in range(BC):
            for c in range(NCH):
                t0 = sq * S + c * TC
                dma("dma_sp", xat[:, :, :], tok_slice(xa_in, t0, TC), [ki], ["xat"])
                for i in range(8):
                    ln_feed(xat[:, i, :], "xat", i, 8, TC)
                ln_rows(TC, CV_INVH, CV_EPSA)
                for i in range(8):
                    sl = i % 2
                    ln_center(xat[:, i, :], "xat", TC, sl)
                    em.add("act", lambda b, i=i, lt=ln_t1[sl]: b.activation(
                        xabt[:, i, :], lt[:, :TC], AF.Identity,
                        scale=smap("ln_g", l * 8 + i), bias=smap("ln_b", l * 8 + i)),
                        reads=[f"ln_t1_{sl}"], writes=["xabt"], sreads=["smal"], fd=TC)
                if debug and l == 0:
                    dma("dma_sp", dbg_xl[:, t0:t0 + TC].rearrange("(a p) t -> p a t", p=128),
                        xabt[:, :, :], ["xabt"], ["dbg_xl"])
                for j in range(32):
                    sl = j % 2
                    dma("dma_pool", wst[:, sl, :, :],
                        win[l, j].rearrange("a p n -> p a n"), [], [f"wst{sl}"])
                    bank = PB_MAIN[j % 2]
                    for k in range(8):
                        em.add("pe", lambda b, k=k, sl=sl, bank=bank, s=(k == 0),
                               e=(k == 7): b.matmul(
                                   PB(bank)[:, :TC], wst[:, sl, k, :],
                                   xabt[:, k, :], start=s, stop=e),
                               reads=[f"wst{sl}", "xabt"], writes=[PBK[bank]], fd=TC)
                    if j < 16:
                        em.add("act", lambda b, j=j, bank=bank: b.activation(
                            xi0[:, j, 3:3 + TC], PB(bank)[:, :TC], AF.Silu),
                            reads=[PBK[bank]], writes=["xi0"], fd=TC)
                        conv_block(l, c, j)
                    else:
                        em.add("act", lambda b, j=j, bank=bank: b.activation(
                            sig[:, j - 16, :], PB(bank)[:, :TC], AF.Sigmoid),
                            reads=[PBK[bank]], writes=["sig"], fd=TC)
                for k in range(16):
                    em.add("pe", lambda b, k=k, s=(k == 0), e=(k == 15): b.matmul(
                        PB(PB_MAIN[0])[0:33, :TC], xpT[:, k, :], xi2[:, k, :],
                        start=s, stop=e),
                        reads=["xpT", "xi2"], writes=[PBK[PB_MAIN[0]]], fd=TC)
                em.add("act", lambda b: b.copy(projf[:, :], PB(PB_MAIN[0])[0:33, :TC]),
                       reads=[PBK[PB_MAIN[0]]], writes=["projf"], fd=TC)
                em.add("act", lambda b: b.copy(projb[:, :], projf[:, :]),
                       reads=["projf"], writes=["projb"], fd=TC)
                if debug and l == 0:
                    dma("dma_sp", dbg_xi2[:, t0:t0 + TC].rearrange("(a p) t -> p a t", p=128),
                        xi2[:, :, :], ["xi2"], ["dbg_xi2"])
                    dma("dma_sp", dbg_sig[:, t0:t0 + TC].rearrange("(a p) t -> p a t", p=128),
                        sig[:, :, :], ["sig"], ["dbg_sig"])
                    dma("dma_sp", dbg_proj[:, t0:t0 + TC], projf[:, :], ["projf"], ["dbg_proj"])
                em.add("pe", lambda b: b.matmul(PB(PB_DT)[:, :TC], sel_f0[:, :],
                                                projf[:, :], start=True, stop=True),
                       reads=["sel_f0", "projf"], writes=[PBK[PB_DT]], fd=TC)
                for n in range(NST):
                    bank = PB_MAIN[n % 2]
                    em.add("pe", lambda b, n=n, bank=bank: b.matmul(
                        PB(bank)[:, :TC], sel_bf[:, (1 + n) * 128:(2 + n) * 128],
                        projb[:, :], start=True, stop=True),
                        reads=["sel_bf", "projb"], writes=[PBK[bank]], fd=TC)
                    em.add("act", lambda b, n=n, bank=bank: b.copy(
                        brepb[:, n, :], PB(bank)[:, :TC]),
                        reads=[PBK[bank]], writes=["brepb"], fd=TC)
                    em.add("pe", lambda b, n=n, bank=bank: b.matmul(
                        PB(bank)[:, :TC], sel_bf[:, (17 + n) * 128:(18 + n) * 128],
                        projb[:, :], start=True, stop=True),
                        reads=["sel_bf", "projb"], writes=[PBK[bank]], fd=TC)
                    em.add("act", lambda b, n=n, bank=bank: b.copy(
                        crepb[:, n, :], PB(bank)[:, :TC]),
                        reads=[PBK[bank]], writes=["crepb"], fd=TC)
                for j in range(16):
                    dt_chain(l, j)
                    for g in range(4):
                        hp = hg[g % 2]
                        for nn in range(4):
                            n = g * 4 + nn
                            hk = f"hg{g % 2}_{nn}"
                            em.add("dve", lambda b, n=n: b.tensor_scalar(
                                a_t[:, :], dtt[:, :], -(float(n + 1)), 1.0,
                                ALU.mult, ALU.add),
                                reads=["dtt"], writes=["a_t"], fd=TC)
                            em.add("dve", lambda b, n=n: b.tensor_mul(
                                b_t[:, :], dxt[:, :], brepb[:, n, :]),
                                reads=["dxt", "brepb"], writes=["b_t"], fd=TC)
                            if c == 0:
                                init, sr = 0.0, []
                            else:
                                init = carry[:, j * 16 + n: j * 16 + n + 1]
                                sr = ["carry"]
                            em.add("dve", lambda b, nn=nn, hp=hp, init=init:
                                   b.tensor_tensor_scan(
                                       hp[:, nn, :], a_t[:, :], b_t[:, :], init,
                                       ALU.mult, ALU.add),
                                   reads=["a_t", "b_t"], writes=[hk], sreads=sr, fd=TC)
                            yp, yk = yht[nn % 2], f"yht{nn % 2}"
                            em.add("pool", lambda b, nn=nn, hp=hp, yp=yp, n=n:
                                   b.tensor_mul(yp[:, :], hp[:, nn, :],
                                                crepb[:, n, :]),
                                   reads=[hk, "crepb"], writes=[yk], fd=TC)
                            em.add("pe", lambda b, yp=yp, s=(n == 0), e=(n == NST - 1):
                                   b.matmul(PB(PB_Y)[:, :TC], id_bf[:, :], yp[:, :],
                                            start=s, stop=e),
                                   reads=[yk, "id_bf"], writes=[PBK[PB_Y]], fd=TC)
                        em.add("dve", lambda b, j=j, g=g, hp=hp: b.tensor_copy(
                            carry[:, j * 16 + g * 4: j * 16 + g * 4 + 4],
                            hp[:, :, TC - 1]),
                            reads=[f"hg{g % 2}_{i}" for i in range(4)],
                            writes=["carry"], fd=4)
                    em.add("dve", lambda b, j=j: b.scalar_tensor_tensor(
                        y32[:, j, :], xi2[:, j, :], smap("Dp", l * 16 + j),
                        PB(PB_Y)[:, :TC], ALU.mult, ALU.add),
                        reads=["xi2", PBK[PB_Y]], writes=["y32"], sreads=["smal"], fd=TC)
                    ln_feed(y32[:, j, :], "y32", j, 16, TC)
                if debug and l == 0:
                    dma("dma_sp", dbg_y[:, t0:t0 + TC].rearrange("(a p) t -> p a t", p=128),
                        y32[:, :, :], ["y32"], ["dbg_y"])
                ln_rows(TC, CV_INVDI, CV_EPSA)
                for j in range(16):
                    sl = j % 2
                    ln_center(y32[:, j, :], "y32", TC, sl)
                    em.add("act", lambda b, j=j, lt=ln_t1[sl], cc=cvt[sl]: b.activation(
                        cc[:, :TC], lt[:, :TC], AF.Identity,
                        scale=smap("mln_g", l * 16 + j), bias=smap("mln_b", l * 16 + j)),
                        reads=[f"ln_t1_{sl}"], writes=[f"cvt{sl}"], sreads=["smal"], fd=TC)
                    em.add("dve", lambda b, j=j, cc=cvt[sl]: b.tensor_mul(
                        ygb[:, j, :], cc[:, :TC], sig[:, j, :]),
                        reads=[f"cvt{sl}", "sig"], writes=["ygb"], fd=TC)
                for j in range(8):
                    dma("dma_pool", wost[:, :, :],
                        wout[l, j].rearrange("a p n -> p a n"), [], ["wost"])
                    bank = PB_MAIN[j % 2]
                    for k in range(16):
                        em.add("pe", lambda b, k=k, bank=bank, s=(k == 0), e=(k == 15):
                               b.matmul(PB(bank)[:, :TC], wost[:, k, :],
                                        ygb[:, k, :], start=s, stop=e),
                               reads=["wost", "ygb"], writes=[PBK[bank]], fd=TC)
                    em.add("dve", lambda b, j=j, bank=bank: b.tensor_add(
                        xat[:, j, :], PB(bank)[:, :TC], xat[:, j, :]),
                        reads=[PBK[bank], "xat"], writes=["xat"], fd=TC)
                dma("dma_sp", tok_slice(xa_out, t0, TC), xat[:, :, :], ["xat"], [ko])

    em.barrier()
    sta.close()

    stb = ExitStack()
    set_psum(stb, MC)
    op1T = sballoc(stb, "op1T", [128, 8, 1024], BF16)
    op2T = sballoc(stb, "op2T", [128, 8, 1024], BF16)
    mx32 = sballoc(stb, "mx32", [128, 8, MC])
    mxb = sballoc(stb, "mxb", [128, 8, MC], BF16)
    g1b = sballoc(stb, "g1b", [128, 8, MC], BF16)
    mst = sballoc(stb, "mst", [128, 8, MC])
    dma("dma_pool", op1T[:, :, :], wop1.rearrange("a p n -> p a n"), [], ["op1T"])
    dma("dma_pool", op2T[:, :, :], wop2.rearrange("a p n -> p a n"), [], ["op2T"])
    xa_fin = xaping[L % 2]
    kfin = f"dram_xa{L % 2}"
    for mc_i in range(NMC):
        t0 = mc_i * MC
        dma("dma_sp", mx32[:, :, :], tok_slice(xa_fin, t0, MC), [kfin], ["mx32"])
        for i in range(8):
            em.add("dve", lambda b, i=i: b.tensor_copy(mxb[:, i, :], mx32[:, i, :]),
                   reads=["mx32"], writes=["mxb"], fd=MC)
        for j in range(8):
            bank = PB_MAIN[j % 2]
            for k in range(8):
                em.add("pe", lambda b, j=j, k=k, bank=bank, s=(k == 0), e=(k == 7):
                       b.matmul(PB(bank)[:, :MC], op1T[:, k, j * 128:(j + 1) * 128],
                                mxb[:, k, :], start=s, stop=e),
                       reads=["op1T", "mxb"], writes=[PBK[bank]], fd=MC)
            em.add("act", lambda b, j=j, bank=bank: b.activation(
                g1b[:, j, :], PB(bank)[:, :MC], AF.Gelu, bias=smap("op1_b", j)),
                reads=[PBK[bank]], writes=["g1b"], sreads=["smal"], fd=MC)
        dma("dma_sp", mx32[:, :, :], tok_slice(xn32, t0, MC), ["dram_xn32"], ["mx32"])
        for j in range(8):
            bank = PB_MAIN[j % 2]
            for k in range(8):
                em.add("pe", lambda b, j=j, k=k, bank=bank, s=(k == 0), e=(k == 7):
                       b.matmul(PB(bank)[:, :MC], op2T[:, k, j * 128:(j + 1) * 128],
                                g1b[:, k, :], start=s, stop=e),
                       reads=["op2T", "g1b"], writes=[PBK[bank]], fd=MC)
            em.add("dve", lambda b, j=j, bank=bank: b.scalar_tensor_tensor(
                mst[:, j, :], PB(bank)[:, :MC], smap("op2_b", j), mx32[:, j, :],
                ALU.add, ALU.add),
                reads=[PBK[bank], "mx32"], writes=["mst"], sreads=["smal"], fd=MC)
        dma("dma_sp", tok_slice(s32, t0, MC), mst[:, :, :], ["mst"], ["dram_s32"])

    em.barrier()
    stb.close()

    stc = ExitStack()
    set_psum(stc, MC)
    gfm = sballoc(stc, "gfm", [8, TOK], BF16)
    sel8 = sballoc(stc, "sel8", [8, 8 * 128], BF16)
    dma("dma_pool", sel8[:, :], selp[0:8, 0:8 * 128], [], ["sel8"])
    std = ExitStack()
    rtb = sballoc(std, "rtb", [128, 8, 8])
    xng = sballoc(std, "xng", [128, 8, 128])
    ltok = sballoc(std, "ltok", [128, 8])
    gv1 = sballoc(std, "gv1", [128, 1])
    gm1 = sballoc(std, "gm1", [128, 8])
    gl2 = sballoc(std, "gl2", [128, 8])
    gv2 = sballoc(std, "gv2", [128, 1])
    gm2 = sballoc(std, "gm2", [128, 8])
    gdd = sballoc(std, "gdd", [128, 1])
    gee = sballoc(std, "gee", [128, 1])
    gsv = sballoc(std, "gsv", [128, 1])
    gw2 = sballoc(std, "gw2", [128, 1])
    ggt = sballoc(std, "ggt", [128, 8])
    dma("dma_sp", rtb[:, :, :], wrt.rearrange("a p n -> p a n"), [], ["rtb"])
    for tg in range(TOK // 128):
        t0 = tg * 128
        dma("dma_sp", xng[:, :, :], tok_slice(xn32, t0, 128), ["dram_xn32"], ["xng"])
        for k in range(8):
            em.add("pe", lambda b, k=k, s=(k == 0), e=(k == 7): b.matmul(
                PB(PB_MAIN[0])[:, 0:8], xng[:, k, :], rtb[:, k, :], start=s, stop=e),
                reads=["xng", "rtb"], writes=[PBK[PB_MAIN[0]]], fd=8)
        em.add("act", lambda b: b.copy(ltok[:, :], PB(PB_MAIN[0])[:, 0:8]),
               reads=[PBK[PB_MAIN[0]]], writes=["ltok"], fd=8)
        em.add("dve", lambda b: b.reduce_max(gv1[:, :], ltok[:, :], axis=AX.X),
               reads=["ltok"], writes=["gv1"], fd=8)
        em.add("dve", lambda b: b.tensor_scalar(gm1[:, :], ltok[:, :], gv1[:, :],
                                                None, ALU.is_ge),
               reads=["ltok"], writes=["gm1"], sreads=["gv1"], fd=8)
        em.add("dve", lambda b: b.scalar_tensor_tensor(gl2[:, :], gm1[:, :], -1e30,
                                                       ltok[:, :], ALU.mult, ALU.add),
               reads=["gm1", "ltok"], writes=["gl2"], fd=8)
        em.add("dve", lambda b: b.reduce_max(gv2[:, :], gl2[:, :], axis=AX.X),
               reads=["gl2"], writes=["gv2"], fd=8)
        em.add("dve", lambda b: b.tensor_scalar(gm2[:, :], gl2[:, :], gv2[:, :],
                                                None, ALU.is_ge),
               reads=["gl2"], writes=["gm2"], sreads=["gv2"], fd=8)
        em.add("dve", lambda b: b.tensor_sub(gdd[:, :], gv2[:, :], gv1[:, :]),
               reads=["gv2", "gv1"], writes=["gdd"], fd=1)
        em.add("act", lambda b: b.activation(gee[:, :], gdd[:, :], AF.Exp),
               reads=["gdd"], writes=["gee"], fd=1)
        em.add("dve", lambda b: b.tensor_scalar_add(gsv[:, :], gee[:, :], 1.0),
               reads=["gee"], writes=["gsv"], fd=1)
        em.add("dve", lambda b: b.reciprocal(gsv[:, :], gsv[:, :]),
               reads=["gsv"], writes=["gsv"], fd=1)
        em.add("dve", lambda b: b.tensor_mul(gw2[:, :], gee[:, :], gsv[:, :]),
               reads=["gee", "gsv"], writes=["gw2"], fd=1)
        em.add("dve", lambda b: b.tensor_scalar(ggt[:, :], gm1[:, :], gsv[:, :],
                                                None, ALU.mult),
               reads=["gm1"], writes=["ggt"], sreads=["gsv"], fd=8)
        em.add("dve", lambda b: b.scalar_tensor_tensor(ggt[:, :], gm2[:, :], gw2[:, :],
                                                       ggt[:, :], ALU.mult, ALU.add),
               reads=["gm2", "ggt"], writes=["ggt"], sreads=["gw2"], fd=8)
        em.add("pe", lambda b: b.transpose(PB(PB_MAIN[1])[0:8, 0:128], ggt[:, :],
                                           ident[:, :]),
               reads=["ggt", "ident"], writes=[PBK[PB_MAIN[1]]], fd=128)
        em.add("act", lambda b, t0=t0: b.copy(gfm[:, t0:t0 + 128],
                                              PB(PB_MAIN[1])[0:8, 0:128]),
               reads=[PBK[PB_MAIN[1]]], writes=["gfm"], fd=128)

    em.barrier()
    std.close()

    ste = ExitStack()
    w1h = sballoc(ste, "w1h", [128, 8, DFF // 2], BF16)
    w2h = sballoc(ste, "w2h", [128, 16, H], BF16)
    exb = sballoc(ste, "exb", [128, 8, MC], BF16)
    ex32 = sballoc(ste, "ex32", [128, 8, MC])
    h1h = sballoc(ste, "h1h", [128, 16, MC], BF16)
    rest = sballoc(ste, "rest", [128, 8, MC])
    ptile = sballoc(ste, "ptile", [128, 8, MC])
    for e in range(NEXP):
        for hf in range(2):
            f0 = hf * 16
            for a in range(8):
                dma("dma_pool", w1h[:, a, :],
                    w1e[e, a, :, hf * (DFF // 2):(hf + 1) * (DFF // 2)], [], ["w1h"])
            for a in range(16):
                dma("dma_pool", w2h[:, a, :], w2e[e, f0 + a], [], ["w2h"])
            for mc_i in range(NMC):
                t0 = mc_i * MC
                dma("dma_sp", exb[:, :, :], tok_slice(xnbf, t0, MC),
                    ["dram_xnbf"], ["exb"])
                for f in range(16):
                    bank = PB_MAIN[f % 2]
                    for k in range(8):
                        em.add("pe", lambda b, f=f, k=k, bank=bank, s=(k == 0),
                               e_=(k == 7): b.matmul(
                                   PB(bank)[:, :MC], w1h[:, k, f * 128:(f + 1) * 128],
                                   exb[:, k, :], start=s, stop=e_),
                               reads=["w1h", "exb"], writes=[PBK[bank]], fd=MC)
                    em.add("act", lambda b, f=f, e=e, f0=f0, bank=bank: b.activation(
                        h1h[:, f, :], PB(bank)[:, :MC], AF.Gelu,
                        bias=smap("b1", e * 32 + f0 + f)),
                        reads=[PBK[bank]], writes=["h1h"], sreads=["smal"], fd=MC)
                if hf == 1:
                    dma("dma_act", ptile[:, :, :], tok_slice(pdram, t0, MC),
                        ["dram_pdram"], ["ptile"])
                    dma("dma_act", ex32[:, :, :], tok_slice(xn32, t0, MC),
                        ["dram_xn32"], ["ex32"])
                    em.add("pe", lambda b, e=e, t0=t0: b.matmul(
                        PB(PB_DT)[:, :MC], sel8[:, e * 128:(e + 1) * 128],
                        gfm[:, t0:t0 + MC], start=True, stop=True),
                        reads=["gfm", "sel8"], writes=[PBK[PB_DT]], fd=MC)
                for dj in range(8):
                    bank = PB_MAIN[dj % 2]
                    for k in range(16):
                        em.add("pe", lambda b, dj=dj, k=k, bank=bank, s=(k == 0),
                               e_=(k == 15): b.matmul(
                                   PB(bank)[:, :MC], w2h[:, k, dj * 128:(dj + 1) * 128],
                                   h1h[:, k, :], start=s, stop=e_),
                               reads=["w2h", "h1h"], writes=[PBK[bank]], fd=MC)
                    if hf == 0:
                        em.add("act", lambda b, dj=dj, bank=bank: b.copy(
                            ptile[:, dj, :], PB(bank)[:, :MC]),
                            reads=[PBK[bank]], writes=["ptile"], fd=MC)
                    else:
                        em.add("dve", lambda b, dj=dj, bank=bank: b.tensor_add(
                            ptile[:, dj, :], PB(bank)[:, :MC], ptile[:, dj, :]),
                            reads=[PBK[bank], "ptile"], writes=["ptile"], fd=MC)
                        em.add("dve", lambda b, dj=dj, e=e: b.scalar_tensor_tensor(
                            rest[:, dj, :], ptile[:, dj, :], smap("b2", e * 8 + dj),
                            ex32[:, dj, :], ALU.add, ALU.add),
                            reads=["ptile", "ex32"], writes=["rest"],
                            sreads=["smal"], fd=MC)
                if hf == 0:
                    dma("dma_act", tok_slice(pdram, t0, MC), ptile[:, :, :],
                        ["ptile"], ["dram_pdram"])
                    continue
                for i in range(8):
                    ln_feed(rest[:, i, :], "rest", i, 8, MC)
                ln_rows(MC, CV_INVH, CV_EPSM)
                for dj in range(8):
                    sl = dj % 2
                    ln_center(rest[:, dj, :], "rest", MC, sl)
                    em.add("act", lambda b, dj=dj, e=e, lt=ln_t1[sl], cc=cvt[sl]:
                           b.activation(
                               cc[:, :MC], lt[:, :MC], AF.Identity,
                               scale=smap("gamma", e * 8 + dj),
                               bias=smap("beta", e * 8 + dj)),
                           reads=[f"ln_t1_{sl}"], writes=[f"cvt{sl}"],
                           sreads=["smal"], fd=MC)
                    em.add("dve", lambda b, dj=dj, cc=cvt[sl]: b.tensor_mul(
                        rest[:, dj, :], cc[:, :MC], PB(PB_DT)[:, :MC]),
                        reads=[f"cvt{sl}", PBK[PB_DT]], writes=["rest"], fd=MC)
                if e == 0:
                    dma("dma_pool", tok_slice(ymoe, t0, MC), rest[:, :, :],
                        ["rest"], ["dram_ymoe"])
                else:
                    dma_acc(tok_slice(ymoe, t0, MC), rest[:, :, :],
                            ["rest"], ["dram_ymoe"])

    em.barrier()
    ste.close()
    stc.close()

    stf = ExitStack()
    cs = sballoc(stf, "cs", [128, 8, MC])
    cy = sballoc(stf, "cy", [128, 8, MC])
    cx = sballoc(stf, "cx", [128, 8, MC])
    co = sballoc(stf, "co", [128, 8, MC])
    for mc_i in range(NMC):
        t0 = mc_i * MC
        dma("dma_sp", cs[:, :, :], tok_slice(s32, t0, MC), ["dram_s32"], ["cs"])
        dma("dma_act", cy[:, :, :], tok_slice(ymoe, t0, MC), ["dram_ymoe"], ["cy"])
        dma("dma_sp", cx[:, :, :], tok_slice(x0, t0, MC), ["dram_x0"], ["cx"])
        for i in range(8):
            em.add("act", lambda b, i=i: b.activation(cy[:, i, :], cy[:, i, :], AF.Silu),
                   reads=["cy"], writes=["cy"], fd=MC)
            em.add("dve", lambda b, i=i: b.tensor_mul(co[:, i, :], cs[:, i, :],
                                                      cy[:, i, :]),
                   reads=["cs", "cy"], writes=["co"], fd=MC)
            em.add("dve", lambda b, i=i: b.tensor_add(co[:, i, :], co[:, i, :],
                                                      cx[:, i, :]),
                   reads=["co", "cx"], writes=["co"], fd=MC)
        dma("dma_sp", tok_slice(out_fm, t0, MC), co[:, :, :], ["co"], ["dram_out"])

    em.barrier()
    em.emit(stx)
    stf.close()
    stx.close()
    return nc



def _pack_smalls(ip):
    sm = np.zeros((128, SM_COLS), np.float32)

    def put(name, vec, nblk, idx0=0):
        v = np.asarray(vec, np.float32).reshape(nblk, 128).T
        sm[:, SM[name] + idx0:SM[name] + idx0 + nblk] = v
    put("in_norm_g", ip["L0_attn_in_norm_g"], 8)
    put("in_norm_b", ip["L0_attn_in_norm_b"], 8)
    for l in range(L):
        put("ln_g", ip["L0_attn_ln_g"][l], 8, l * 8)
        put("ln_b", ip["L0_attn_ln_b"][l], 8, l * 8)
        for ci in range(2):
            for k in range(KCONV):
                put("conv_w", ip["L0_attn_conv_w"][l, ci, :, k], 16,
                    l * (2 * KCONV * 16) + ci * (KCONV * 16) + k * 16)
            put("conv_b", ip["L0_attn_conv_b"][l, ci], 16, l * 32 + ci * 16)
        put("dt_w", ip["L0_attn_dt_proj_w"][l, :, 0], 16, l * 16)
        put("dt_b", ip["L0_attn_dt_proj_b"][l], 16, l * 16)
        put("Dp", ip["L0_attn_D"][l], 16, l * 16)
        put("mln_g", ip["L0_attn_mamba_ln_g"][l], 16, l * 16)
        put("mln_b", ip["L0_attn_mamba_ln_b"][l], 16, l * 16)
    put("op1_b", ip["L0_attn_op1_b"], 8)
    put("op2_b", ip["L0_attn_op2_b"], 8)
    for e in range(NEXP):
        put("b1", ip["L0_ffn_b1"][e], 32, e * 32)
        put("b2", ip["L0_ffn_b2"][e], 8, e * 8)
        put("gamma", ip["L0_ffn_gamma"][e], 8, e * 8)
        put("beta", ip["L0_ffn_beta"][e], 8, e * 8)
    return sm


def _host_shared(ip):
    f = np.float32
    arrs = {}
    w = np.asarray(ip["L0_attn_in_proj_w"], f)
    arrs["win"] = np.ascontiguousarray(
        w.reshape(L, 32, 128, 8, 128).transpose(0, 1, 3, 4, 2))
    w = np.asarray(ip["L0_attn_out_proj_w"], f)
    arrs["wout"] = np.ascontiguousarray(
        w.reshape(L, 8, 128, 16, 128).transpose(0, 1, 3, 4, 2))
    w = np.asarray(ip["L0_attn_x_proj_w"], f)
    arrs["wxp"] = np.ascontiguousarray(w.transpose(0, 2, 1).reshape(L, 16, 128, 33))
    arrs["wop1"] = np.ascontiguousarray(
        np.asarray(ip["L0_attn_op1_w"], f).T.reshape(8, 128, 1024))
    arrs["wop2"] = np.ascontiguousarray(
        np.asarray(ip["L0_attn_op2_w"], f).T.reshape(8, 128, 1024))
    arrs["wrt"] = np.ascontiguousarray(
        np.asarray(ip["L0_ffn_router_w"], f).reshape(8, 128, 8))
    arrs["w1e"] = np.ascontiguousarray(
        np.asarray(ip["L0_ffn_w1"], f).reshape(NEXP, 8, 128, DFF))
    arrs["w2e"] = np.ascontiguousarray(
        np.asarray(ip["L0_ffn_w2"], f).reshape(NEXP, 32, 128, H))
    arrs["smalls"] = _pack_smalls(ip)
    cvv = np.zeros((128, NCONST), f)
    cvv[:, CV_ONE] = 1.0
    cvv[:, CV_EPSA] = EPS_A
    cvv[:, CV_EPSM] = EPS_M
    cvv[:, CV_NH] = -0.5
    cvv[:, CV_INVH] = 1.0 / H
    cvv[:, CV_INVDI] = 1.0 / DI
    arrs["consts"] = cvv
    sel = np.zeros((33, 34 * 128), f)
    for i in range(33):
        sel[i, i * 128:(i + 1) * 128] = 1.0
    arrs["selp"] = sel
    arrs["idmat"] = np.eye(128, dtype=f)
    arrs["onescol"] = np.ones((128, 1), f)
    arrs["onesrow"] = np.ones((1, 128), f)
    return arrs


def kernel(**inputs):
    hs = np.asarray(inputs["L0_attn_hidden_states"], np.float32)
    shared = _host_shared(inputs)
    nc = build_program(debug=False)
    in_maps = []
    for c in range(NCORE):
        m = dict(shared)
        xc = hs[c * BC:(c + 1) * BC]
        m["x0"] = np.ascontiguousarray(xc.transpose(2, 0, 1).reshape(H, TOK))
        in_maps.append(m)
    res = run_bass_kernel_spmd(nc, in_maps, core_ids=list(range(NCORE)))
    outs = []
    for c in range(NCORE):
        o = np.asarray(res.results[c]["out_fm"])
        outs.append(o.reshape(H, BC, S).transpose(1, 2, 0))
    return np.concatenate(outs, axis=0).astype(np.float32)


if __name__ == "__main__":
    import time
    t = time.time()
    nc = build_program()
    print(f"built in {time.time() - t:.1f}s")


# version 04
# speedup vs baseline: 1.0790x; 1.0129x over previous
import numpy as np
from contextlib import ExitStack

import concourse.bass as bass
import concourse.mybir as mybir
from concourse.bass_utils import run_bass_kernel_spmd

F32 = mybir.dt.float32
BF16 = mybir.dt.bfloat16
AF = mybir.ActivationFunctionType
ALU = mybir.AluOpType
AX = mybir.AxisListType

B_TOT, S, H, L, DI, NST, KCONV = 16, 2048, 1024, 4, 2048, 16, 4
DFF, NEXP, NCORE = 4096, 8, 8
BC = B_TOT // NCORE
TOK = BC * S
TC = 256
NCH = S // TC
MC = 512
NMC = TOK // MC
EPS_A, EPS_M = 1e-5, 1e-6
FIFO_SAFE_FD = 128

ENGINES = ("pe", "dve", "act", "pool", "sp")
DMAQ = ("dma_sp", "dma_act", "dma_pool")


def _sm_layout():
    cols, c = {}, 0

    def take(name, n):
        nonlocal c
        cols[name] = c
        c += n
    take("in_norm_g", 8); take("in_norm_b", 8)
    take("ln_g", L * 8); take("ln_b", L * 8)
    take("conv_w", L * 2 * KCONV * 16); take("conv_b", L * 2 * 16)
    take("dt_w", L * 16); take("dt_b", L * 16)
    take("Dp", L * 16)
    take("mln_g", L * 16); take("mln_b", L * 16)
    take("op1_b", 8); take("op2_b", 8)
    take("b1", NEXP * 32); take("b2", NEXP * 8)
    take("gamma", NEXP * 8); take("beta", NEXP * 8)
    return cols, c


SM, SM_COLS = _sm_layout()
CV_ONE, CV_EPSA, CV_EPSM, CV_NH, CV_INVH, CV_INVDI = 0, 1, 2, 3, 4, 5
CV_NEG0 = 6
NCONST = 22


class Op:
    __slots__ = ("eng", "fn", "reads", "writes", "sreads", "fd", "idx",
                 "inc_count", "waits", "dma", "meta")

    def __init__(self, eng, fn, reads, writes, sreads, fd, idx, dma):
        self.eng, self.fn = eng, fn
        self.reads, self.writes, self.sreads = list(reads), list(writes), list(sreads)
        self.fd, self.idx, self.dma = fd, idx, dma
        self.inc_count, self.waits = None, []
        self.meta = None


DMA_SLOTS = {"dma_sp": 1, "dma_act": 1, "dma_pool": 8}


class Emitter:

    def __init__(self, nc):
        self.nc = nc
        self.ops = []
        self.last_writer = {}
        self.readers = {}
        self.counts = {}
        for e in ENGINES:
            self.counts[e] = 0
        self.dma_qkeys = []
        for q, ns in DMA_SLOTS.items():
            for s in range(ns):
                k = f"{q}#{s}"
                self.counts[k] = 0
                self.dma_qkeys.append(k)
        self.dma_rr = {q: 0 for q in DMA_SLOTS}
        self.last_dma_on_slot = {}
        self.meta_provider = None
        self.meta_apply = None

    @staticmethod
    def _same_stream(a, b):
        if a.dma is not None or b.dma is not None:
            return False
        return a.eng == b.eng

    def _need_edge(self, prod, cons, scalar=False):
        if prod is None:
            return False
        if scalar:
            return True
        if self._same_stream(prod, cons):
            if prod.eng == "pe":
                return False
            return prod.fd < FIFO_SAFE_FD or cons.fd < FIFO_SAFE_FD
        return True

    def _assign_inc(self, d):
        if d.inc_count is None:
            d.inc_count = -1
            q = d.dma if d.dma is not None else d.eng
            self.counts[q] += 16 if d.dma is not None else 1

    def add(self, eng, fn, reads=(), writes=(), sreads=(), fd=512, dma=None):
        if dma is not None:
            slot = self.dma_rr[dma]
            self.dma_rr[dma] = (slot + 1) % DMA_SLOTS[dma]
            qkey = f"{dma}#{slot}"
        else:
            qkey = None
        op = Op(eng, fn, reads, writes, sreads, fd, len(self.ops), qkey)
        if self.meta_provider is not None:
            op.meta = self.meta_provider()
        deps = set()
        for b in op.reads:
            w = self.last_writer.get(b)
            if self._need_edge(w, op):
                deps.add(w)
        for b in op.sreads:
            w = self.last_writer.get(b)
            if self._need_edge(w, op, scalar=True):
                deps.add(w)
        for b in op.writes:
            w = self.last_writer.get(b)
            if self._need_edge(w, op):
                deps.add(w)
            for r in self.readers.get(b, ()):
                if r is not op and self._need_edge(r, op):
                    deps.add(r)
        by_q = {}
        for d in deps:
            q = d.dma if d.dma is not None else d.eng
            if d.dma is not None:
                by_q[(q, d.idx)] = d
            else:
                cur = by_q.get(q)
                if cur is None or cur.idx < d.idx:
                    by_q[q] = d
        if qkey is not None:
            prev = self.last_dma_on_slot.get(qkey)
            if prev is not None:
                op.waits.append(prev)
            self.last_dma_on_slot[qkey] = op
        for d in sorted(by_q.values(), key=lambda o: o.idx):
            self._assign_inc(d)
            op.waits.append(d)
        self.ops.append(op)
        if qkey is not None:
            self._assign_inc(op)
        for b in op.reads + op.sreads:
            self.readers.setdefault(b, []).append(op)
        for b in op.writes:
            self.last_writer[b] = op
            self.readers[b] = []
        return op

    def barrier(self):
        last_by_q = {}
        for o in self.ops:
            q = o.dma if o.dma is not None else o.eng
            last_by_q[q] = o
        for q, o in last_by_q.items():
            self._assign_inc(o)
        for eng in ENGINES:
            op = Op(eng, lambda b: b.nop(), [], [], [], 1, len(self.ops), None)
            for q, o in last_by_q.items():
                if q != eng:
                    op.waits.append(o)
            self.ops.append(op)

    def emit(self, st):
        nc = self.nc
        allq = list(ENGINES) + self.dma_qkeys
        ordinals = {q: 0 for q in allq}
        for op in self.ops:
            if op.inc_count is not None:
                q = op.dma if op.dma is not None else op.eng
                ordinals[q] += 16 if op.dma is not None else 1
                op.inc_count = ordinals[q]
        for q in allq:
            assert ordinals[q] < 60000, f"sem overflow {q}: {ordinals[q]}"
        sems = {q: st.enter_context(nc.semaphore(f"sem_{q.replace('#','_')}"))
                for q in allq}
        block = st.enter_context(nc.Block())
        per_eng = {e: [] for e in ENGINES}
        for op in self.ops:
            per_eng[op.eng].append(op)
        streams = {"pe": block.tensor, "dve": block.vector, "act": block.scalar,
                   "pool": block.gpsimd, "sp": block.sync}

        for eng in ENGINES:
            ops = per_eng[eng]

            def make(ops=ops):
                def body(b):
                    clock = {q: 0 for q in allq}
                    for op in ops:
                        for d in op.waits:
                            q = d.dma if d.dma is not None else d.eng
                            v = d.inc_count
                            if clock[q] < v:
                                b.wait_ge(sems[q], v)
                                clock[q] = v
                        if op.meta is not None and self.meta_apply is not None:
                            self.meta_apply(op.meta)
                        inst = op.fn(b)
                        if op.inc_count is not None:
                            q = op.dma if op.dma is not None else op.eng
                            step = 16 if op.dma is not None else 1
                            inst.then_inc(sems[q], step)
                return body
            streams[eng](make())


def build_program(debug=False):
    nc = bass.Bass("TRN2", target_bir_lowering=False, detect_race_conditions=False)
    em = Emitter(nc)
    stx = ExitStack()

    def din(name, shape, dt=F32):
        return nc.dram_tensor(name, list(shape), dt, kind="ExternalInput").ap()

    def dout(name, shape, dt=F32):
        return nc.dram_tensor(name, list(shape), dt, kind="ExternalOutput").ap()

    def dint(name, shape, dt=F32):
        kind = "ExternalOutput" if debug else "Internal"
        return nc.dram_tensor(name, list(shape), dt, kind=kind).ap()

    x0 = din("x0", [H, TOK])
    win = din("win", [L, 32, 8, 128, 128])
    wout = din("wout", [L, 8, 16, 128, 128])
    wxp = din("wxp", [L, 16, 128, 33])
    wop1 = din("wop1", [8, 128, 1024])
    wop2 = din("wop2", [8, 128, 1024])
    wrt = din("wrt", [8, 128, 8])
    w1e = din("w1e", [NEXP, 8, 128, DFF])
    w2e = din("w2e", [NEXP, 32, 128, H])
    smalls = din("smalls", [128, SM_COLS])
    consts = din("consts", [128, NCONST])
    selp = din("selp", [33, 34 * 128])
    idmat = din("idmat", [128, 128])
    onescol = din("onescol", [128, 1])
    onesrow = din("onesrow", [1, 128])

    out_fm = dout("out_fm", [H, TOK])
    xn32 = dint("xn32", [H, TOK])
    xnbf = dint("xnbf", [H, TOK], BF16)
    xaping = [dint("xa0", [H, TOK]), dint("xa1", [H, TOK])]
    s32 = dint("s32", [H, TOK])
    ymoe = dint("ymoe", [H, TOK])
    pdram = dint("pdram", [H, TOK])
    winb = [dint(f"winb{i}", [32, 8, 128, 128], BF16) for i in range(2)]
    woutb = [dint(f"woutb{i}", [8, 16, 128, 128], BF16) for i in range(2)]
    if debug:
        dbg_xl = dout("dbg_xl", [H, TOK], BF16)
        dbg_xi2 = dout("dbg_xi2", [DI, TOK])
        dbg_proj = dout("dbg_proj", [33, TOK])
        dbg_y = dout("dbg_y", [DI, TOK])
        dbg_sig = dout("dbg_sig", [DI, TOK])

    def sballoc(st, name, shape, dt=F32):
        return st.enter_context(nc.sbuf_tensor(name, shape, dt))

    cons = sballoc(stx, "cons", [128, NCONST])
    smal = sballoc(stx, "smal", [128, SM_COLS])
    ident = sballoc(stx, "ident", [128, 128])
    id_bf = sballoc(stx, "id_bf", [128, 128], BF16)
    ones_bf = sballoc(stx, "ones_bf", [128, 1], BF16)
    onesrow_f = sballoc(stx, "onesrow_f", [1, 128])
    onesrow_b = sballoc(stx, "onesrow_b", [1, 128], BF16)
    ln_bf = [sballoc(stx, f"ln_bf{i}", [128, MC], BF16) for i in range(2)]
    ln_sq = [sballoc(stx, f"ln_sq{i}", [128, MC], BF16) for i in range(2)]
    ln_t1 = [sballoc(stx, f"ln_t1_{i}", [128, MC]) for i in range(2)]
    cvt = [sballoc(stx, f"cvt{i}", [128, MC]) for i in range(2)]
    rows = sballoc(stx, "rows", [1, 5 * MC])

    def row(i, width):
        return rows[:, i * MC:i * MC + width]
    R_MU, R_M2, R_VAR, R_R, R_MR = range(5)

    _pb_gens = {}
    _pbcur = {"t": None, "gen": 0}

    def set_psum(st_, width):
        _pbcur["gen"] += 1
        g = _pbcur["gen"]
        _pbcur["t"] = [st_.enter_context(
            nc.psum_tensor(f"pb{g}_{i}", [128, width], F32)) for i in range(8)]
        _pb_gens[g] = _pbcur["t"]

    def PB(i):
        return _pbcur["t"][i]

    em.meta_provider = lambda: _pbcur["gen"]
    em.meta_apply = lambda g: (_pbcur.update(t=_pb_gens[g]) if g in _pb_gens else None)

    PBK = [f"pb{i}" for i in range(8)]
    PB_MAIN = (0, 1)
    PB_R, PB_M, PB_Y, PB_DT, PB_S1, PB_S2 = 2, 3, 4, 5, 6, 7

    def dma(queue, out_ap, in_ap, reads, writes):
        eng = {"dma_sp": "sp", "dma_act": "act", "dma_pool": "pool"}[queue]
        em.add(eng, lambda b, o=out_ap, i=in_ap: b.dma_start(o, i),
               reads=reads, writes=writes, fd=512, dma=queue)

    def dma_acc(out_ap, in_ap, reads, writes):
        em.add("pool", lambda b, o=out_ap, i=in_ap: b.dma_start(o, i, accum_op=ALU.add),
               reads=reads, writes=writes, fd=512, dma="dma_pool")

    def cvap(col, p1=False):
        a = cons[:, col:col + 1]
        return a[0:1, :] if p1 else a

    def smap(name, idx):
        c = SM[name] + idx
        return smal[:, c:c + 1]

    dma("dma_sp", cons[:, :], consts, [], ["cons"])
    dma("dma_sp", smal[:, :], smalls, [], ["smal"])
    dma("dma_sp", ident[:, :], idmat, [], ["ident"])
    dma("dma_pool", id_bf[:, :], idmat, [], ["id_bf"])
    dma("dma_pool", ones_bf[:, :], onescol, [], ["ones_bf"])
    dma("dma_sp", onesrow_f[:, :], onesrow, [], ["onesrow_f"])
    dma("dma_pool", onesrow_b[:, :], onesrow, [], ["onesrow_b"])

    def tok_slice(d_ap, t0, w):
        return d_ap[:, t0:t0 + w].rearrange("(a p) t -> p a t", p=128)

    def ln_feed(ap, key, i, nblk, width, s1=PB_S1, s2=PB_S2):
        sl = i % 2
        lb, ls = ln_bf[sl], ln_sq[sl]
        em.add("act", lambda b, a=ap, lb=lb: b.copy(lb[:, :width], a),
               reads=[key], writes=[f"ln_bf{sl}"], fd=width)
        em.add("pe", lambda b, s=(i == 0), e=(i == nblk - 1), lb=lb: b.matmul(
            PB(s1)[0:1, :width], ones_bf[:, :], lb[:, :width], start=s, stop=e),
            reads=[f"ln_bf{sl}", "ones_bf"], writes=[PBK[s1]], fd=width)
        em.add("act", lambda b, a=ap, ls=ls: b.activation(ls[:, :width], a, AF.Square),
               reads=[key], writes=[f"ln_sq{sl}"], fd=width)
        em.add("pe", lambda b, s=(i == 0), e=(i == nblk - 1), ls=ls: b.matmul(
            PB(s2)[0:1, :width], ones_bf[:, :], ls[:, :width], start=s, stop=e),
            reads=[f"ln_sq{sl}", "ones_bf"], writes=[PBK[s2]], fd=width)

    def ln_rows(width, inv_col, eps_col):
        em.add("act", lambda b: b.activation(row(R_MU, width), PB(PB_S1)[0:1, :width],
                                             AF.Identity, scale=cvap(inv_col, True)),
               reads=[PBK[PB_S1]], writes=["rows"], sreads=["cons"], fd=width)
        em.add("act", lambda b: b.activation(row(R_M2, width), PB(PB_S2)[0:1, :width],
                                             AF.Identity, scale=cvap(inv_col, True)),
               reads=[PBK[PB_S2]], writes=["rows"], sreads=["cons"], fd=width)
        em.add("dve", lambda b: b.tensor_mul(row(R_VAR, width), row(R_MU, width),
                                             row(R_MU, width)),
               reads=["rows"], writes=["rows"], fd=width)
        em.add("dve", lambda b: b.tensor_sub(row(R_VAR, width), row(R_M2, width),
                                             row(R_VAR, width)),
               reads=["rows"], writes=["rows"], fd=width)
        em.add("act", lambda b: b.activation(row(R_R, width), row(R_VAR, width), AF.Ln,
                                             bias=cvap(eps_col, True)),
               reads=["rows"], writes=["rows"], sreads=["cons"], fd=width)
        em.add("act", lambda b: b.activation(row(R_R, width), row(R_R, width), AF.Exp,
                                             scale=cvap(CV_NH, True)),
               reads=["rows"], writes=["rows"], sreads=["cons"], fd=width)
        em.add("dve", lambda b: b.tensor_mul(row(R_MR, width), row(R_MU, width),
                                             row(R_R, width)),
               reads=["rows"], writes=["rows"], fd=width)
        em.add("pe", lambda b: b.matmul(PB(PB_R)[:, :width], onesrow_f[:, :],
                                        row(R_R, width), start=True, stop=True),
               reads=["rows", "onesrow_f"], writes=[PBK[PB_R]], fd=width)
        em.add("pe", lambda b: b.matmul(PB(PB_M)[:, :width], onesrow_f[:, :],
                                        row(R_MR, width), start=True, stop=True),
               reads=["rows", "onesrow_f"], writes=[PBK[PB_M]], fd=width)

    def ln_center(ap, key, width, sl=0):
        lt = ln_t1[sl]
        em.add("dve", lambda b, a=ap, lt=lt: b.tensor_mul(lt[:, :width], a,
                                                          PB(PB_R)[:, :width]),
               reads=[key, PBK[PB_R]], writes=[f"ln_t1_{sl}"], fd=width)
        em.add("dve", lambda b, lt=lt: b.tensor_sub(lt[:, :width], lt[:, :width],
                                                    PB(PB_M)[:, :width]),
               reads=[f"ln_t1_{sl}", PBK[PB_M]], writes=[f"ln_t1_{sl}"], fd=width)

    sta = ExitStack()
    set_psum(sta, TC)
    xat = sballoc(sta, "xat", [128, 8, TC])
    xnt = sballoc(sta, "xnt", [128, 8, TC])
    xnbt = sballoc(sta, "xnbt", [128, 8, TC], BF16)
    xabt = sballoc(sta, "xabt", [128, 8, TC], BF16)
    sel_bf = sballoc(sta, "sel_bf", [33, 34 * 128], BF16)
    sel_f0 = sballoc(sta, "sel_f0", [33, 128])
    xpT = sballoc(sta, "xpT", [128, 16, 33])
    wst = sballoc(sta, "wst", [128, 4, 8, 128], BF16)
    wost = sballoc(sta, "wost", [128, 2, 16, 128], BF16)
    xi0 = sballoc(sta, "xi0", [128, 16, TC + 3])
    xi1 = sballoc(sta, "xi1", [128, 16, TC + 3])
    xi2 = sballoc(sta, "xi2", [128, 16, TC])
    sig = sballoc(sta, "sig", [128, 16, TC])
    projf = sballoc(sta, "projf", [33, TC])
    projb = sballoc(sta, "projb", [33, TC], BF16)
    brepb = sballoc(sta, "brepb", [128, 16, TC], BF16)
    crepb = sballoc(sta, "crepb", [128, 16, TC], BF16)
    zt = sballoc(sta, "zt", [128, TC])
    mt = sballoc(sta, "mt", [128, TC])
    t1t = sballoc(sta, "t1t", [128, TC])
    dtt = sballoc(sta, "dtt", [128, TC])
    dxt = sballoc(sta, "dxt", [128, TC])
    a_t = sballoc(sta, "a_t", [128, TC])
    b_t = sballoc(sta, "b_t", [128, TC])
    hg = [sballoc(sta, f"hg{i}", [128, 4, TC]) for i in range(2)]
    yht = [sballoc(sta, f"yht{i}", [128, TC], BF16) for i in range(2)]
    carry = sballoc(sta, "carry", [128, 256])
    y32 = sballoc(sta, "y32", [128, 16, TC])
    ygb = sballoc(sta, "ygb", [128, 16, TC], BF16)

    dma("dma_pool", sel_bf[:, :], selp, [], ["sel_bf"])
    dma("dma_sp", sel_f0[:, :], selp[:, 0:128], [], ["sel_f0"])

    for sq in range(BC):
        for c in range(NCH):
            t0 = sq * S + c * TC
            dma("dma_sp", xat[:, :, :], tok_slice(x0, t0, TC), ["dram_x0"], ["xat"])
            for i in range(8):
                ln_feed(xat[:, i, :], "xat", i, 8, TC)
            ln_rows(TC, CV_INVH, CV_EPSA)
            for i in range(8):
                sl = i % 2
                ln_center(xat[:, i, :], "xat", TC, sl)
                em.add("act", lambda b, i=i, lt=ln_t1[sl]: b.copy(xnt[:, i, :], lt[:, :TC]),
                       reads=[f"ln_t1_{sl}"], writes=["xnt"], fd=TC)
                em.add("act", lambda b, i=i, lt=ln_t1[sl]: b.copy(xnbt[:, i, :], lt[:, :TC]),
                       reads=[f"ln_t1_{sl}"], writes=["xnbt"], fd=TC)
            dma("dma_sp", tok_slice(xn32, t0, TC), xnt[:, :, :], ["xnt"], ["dram_xn32"])
            dma("dma_act", tok_slice(xnbf, t0, TC), xnbt[:, :, :], ["xnbt"], ["dram_xnbf"])
            for i in range(8):
                ln_feed(xnt[:, i, :], "xnt", i, 8, TC)
            ln_rows(TC, CV_INVH, CV_EPSA)
            for i in range(8):
                sl = i % 2
                ln_center(xnt[:, i, :], "xnt", TC, sl)
                em.add("act", lambda b, i=i, lt=ln_t1[sl]: b.activation(
                    xat[:, i, :], lt[:, :TC], AF.Identity,
                    scale=smap("in_norm_g", i), bias=smap("in_norm_b", i)),
                    reads=[f"ln_t1_{sl}"], writes=["xat"], sreads=["smal"], fd=TC)
            dma("dma_sp", tok_slice(xaping[0], t0, TC), xat[:, :, :], ["xat"], ["dram_xa0"])

    def conv_block(l, c, j):
        for (src, srck, ci) in ((xi0, "xi0", 0), (xi1, "xi1", 1)):
            if c == 0:
                em.add("dve", lambda b, src=src, j=j: b.memset(src[:, j, 0:3], 0.0),
                       reads=[], writes=[srck], fd=3)
            wbase = SM["conv_w"] + l * (2 * KCONV * 16) + ci * (KCONV * 16)
            cc, cck = cvt[j % 2], f"cvt{j % 2}"
            em.add("dve", lambda b, src=src, j=j, cc=cc,
                   col=wbase + (KCONV - 1) * 16 + j:
                   b.tensor_scalar(
                       cc[:, :TC], src[:, j, 3:3 + TC], smal[:, col:col + 1],
                       None, ALU.mult),
                   reads=[srck], writes=[cck], sreads=["smal"], fd=TC)
            for k in range(1, KCONV):
                em.add("dve", lambda b, src=src, j=j, k=k, cc=cc,
                       col=wbase + (KCONV - 1 - k) * 16 + j:
                       b.scalar_tensor_tensor(
                           cc[:, :TC], src[:, j, 3 - k:3 - k + TC],
                           smal[:, col:col + 1], cc[:, :TC], ALU.mult, ALU.add),
                       reads=[srck, cck], writes=[cck], sreads=["smal"], fd=TC)
            em.add("dve", lambda b, src=src, j=j: b.tensor_copy(
                src[:, j, 0:3], src[:, j, TC:TC + 3]),
                reads=[srck], writes=[srck], fd=3)
            bcol = SM["conv_b"] + l * 32 + ci * 16 + j
            if ci == 0:
                em.add("act", lambda b, j=j, bcol=bcol, cc=cc: b.activation(
                    xi1[:, j, 3:3 + TC], cc[:, :TC], AF.Silu,
                    bias=smal[:, bcol:bcol + 1]),
                    reads=[cck], writes=["xi1"], sreads=["smal"], fd=TC)
            else:
                em.add("act", lambda b, j=j, bcol=bcol, cc=cc: b.activation(
                    xi2[:, j, :], cc[:, :TC], AF.Silu,
                    bias=smal[:, bcol:bcol + 1]),
                    reads=[cck], writes=["xi2"], sreads=["smal"], fd=TC)

    def dt_chain(l, j):
        em.add("act", lambda b, j=j: b.activation(
            zt[:, :], PB(PB_DT)[:, :TC], AF.Identity,
            scale=smap("dt_w", l * 16 + j), bias=smap("dt_b", l * 16 + j)),
            reads=[PBK[PB_DT]], writes=["zt"], sreads=["smal"], fd=TC)
        em.add("dve", lambda b: b.tensor_scalar_max(mt[:, :], zt[:, :], 0.0),
               reads=["zt"], writes=["mt"], fd=TC)
        em.add("dve", lambda b: b.scalar_tensor_tensor(t1t[:, :], mt[:, :], -2.0,
                                                       zt[:, :], ALU.mult, ALU.add),
               reads=["mt", "zt"], writes=["t1t"], fd=TC)
        em.add("act", lambda b: b.activation(zt[:, :], t1t[:, :], AF.Exp),
               reads=["t1t"], writes=["zt"], fd=TC)
        em.add("act", lambda b: b.activation(t1t[:, :], zt[:, :], AF.Ln, bias=1.0),
               reads=["zt"], writes=["t1t"], fd=TC)
        em.add("dve", lambda b: b.tensor_add(dtt[:, :], mt[:, :], t1t[:, :]),
               reads=["mt", "t1t"], writes=["dtt"], fd=TC)
        em.add("dve", lambda b, j=j: b.tensor_mul(dxt[:, :], dtt[:, :], xi2[:, j, :]),
               reads=["dtt", "xi2"], writes=["dxt"], fd=TC)

    def precast_weights(l):
        p = l % 2
        dma("dma_pool", winb[p][:, :, :, :], win[l], [], [f"dram_winb{p}"])
        dma("dma_pool", woutb[p][:, :, :, :], wout[l], [], [f"dram_woutb{p}"])

    precast_weights(0)
    for l in range(L):
        xa_in, xa_out = xaping[l % 2], xaping[(l + 1) % 2]
        ki, ko = f"dram_xa{l % 2}", f"dram_xa{(l + 1) % 2}"
        if l + 1 < L:
            precast_weights(l + 1)
        dma("dma_pool", xpT[:, :, :], wxp[l].rearrange("a p n -> p a n"), [], ["xpT"])
        for sq in range(BC):
            for c in range(NCH):
                t0 = sq * S + c * TC
                dma("dma_sp", xat[:, :, :], tok_slice(xa_in, t0, TC), [ki], ["xat"])
                for i in range(8):
                    ln_feed(xat[:, i, :], "xat", i, 8, TC)
                ln_rows(TC, CV_INVH, CV_EPSA)
                for i in range(8):
                    sl = i % 2
                    ln_center(xat[:, i, :], "xat", TC, sl)
                    em.add("act", lambda b, i=i, lt=ln_t1[sl]: b.activation(
                        xabt[:, i, :], lt[:, :TC], AF.Identity,
                        scale=smap("ln_g", l * 8 + i), bias=smap("ln_b", l * 8 + i)),
                        reads=[f"ln_t1_{sl}"], writes=["xabt"], sreads=["smal"], fd=TC)
                if debug and l == 0:
                    dma("dma_sp", dbg_xl[:, t0:t0 + TC].rearrange("(a p) t -> p a t", p=128),
                        xabt[:, :, :], ["xabt"], ["dbg_xl"])
                for j in range(32):
                    sl = j % 4
                    dma("dma_sp" if j % 2 == 0 else "dma_act", wst[:, sl, :, :],
                        winb[l % 2][j].rearrange("a p n -> p a n"),
                        [f"dram_winb{l % 2}"], [f"wst{sl}"])
                    bank = PB_MAIN[j % 2]
                    for k in range(8):
                        em.add("pe", lambda b, k=k, sl=sl, bank=bank, s=(k == 0),
                               e=(k == 7): b.matmul(
                                   PB(bank)[:, :TC], wst[:, sl, k, :],
                                   xabt[:, k, :], start=s, stop=e),
                               reads=[f"wst{sl}", "xabt"], writes=[PBK[bank]], fd=TC)
                    if j < 16:
                        em.add("act", lambda b, j=j, bank=bank: b.activation(
                            xi0[:, j, 3:3 + TC], PB(bank)[:, :TC], AF.Silu),
                            reads=[PBK[bank]], writes=["xi0"], fd=TC)
                        conv_block(l, c, j)
                    else:
                        em.add("act", lambda b, j=j, bank=bank: b.activation(
                            sig[:, j - 16, :], PB(bank)[:, :TC], AF.Sigmoid),
                            reads=[PBK[bank]], writes=["sig"], fd=TC)
                for k in range(16):
                    em.add("pe", lambda b, k=k, s=(k == 0), e=(k == 15): b.matmul(
                        PB(PB_MAIN[0])[0:33, :TC], xpT[:, k, :], xi2[:, k, :],
                        start=s, stop=e),
                        reads=["xpT", "xi2"], writes=[PBK[PB_MAIN[0]]], fd=TC)
                em.add("act", lambda b: b.copy(projf[:, :], PB(PB_MAIN[0])[0:33, :TC]),
                       reads=[PBK[PB_MAIN[0]]], writes=["projf"], fd=TC)
                em.add("act", lambda b: b.copy(projb[:, :], projf[:, :]),
                       reads=["projf"], writes=["projb"], fd=TC)
                if debug and l == 0:
                    dma("dma_sp", dbg_xi2[:, t0:t0 + TC].rearrange("(a p) t -> p a t", p=128),
                        xi2[:, :, :], ["xi2"], ["dbg_xi2"])
                    dma("dma_sp", dbg_sig[:, t0:t0 + TC].rearrange("(a p) t -> p a t", p=128),
                        sig[:, :, :], ["sig"], ["dbg_sig"])
                    dma("dma_sp", dbg_proj[:, t0:t0 + TC], projf[:, :], ["projf"], ["dbg_proj"])
                em.add("pe", lambda b: b.matmul(PB(PB_DT)[:, :TC], sel_f0[:, :],
                                                projf[:, :], start=True, stop=True),
                       reads=["sel_f0", "projf"], writes=[PBK[PB_DT]], fd=TC)
                for n in range(NST):
                    bank = PB_MAIN[n % 2]
                    em.add("pe", lambda b, n=n, bank=bank: b.matmul(
                        PB(bank)[:, :TC], sel_bf[:, (1 + n) * 128:(2 + n) * 128],
                        projb[:, :], start=True, stop=True),
                        reads=["sel_bf", "projb"], writes=[PBK[bank]], fd=TC)
                    em.add("act", lambda b, n=n, bank=bank: b.copy(
                        brepb[:, n, :], PB(bank)[:, :TC]),
                        reads=[PBK[bank]], writes=["brepb"], fd=TC)
                    em.add("pe", lambda b, n=n, bank=bank: b.matmul(
                        PB(bank)[:, :TC], sel_bf[:, (17 + n) * 128:(18 + n) * 128],
                        projb[:, :], start=True, stop=True),
                        reads=["sel_bf", "projb"], writes=[PBK[bank]], fd=TC)
                    em.add("act", lambda b, n=n, bank=bank: b.copy(
                        crepb[:, n, :], PB(bank)[:, :TC]),
                        reads=[PBK[bank]], writes=["crepb"], fd=TC)
                for j in range(16):
                    dt_chain(l, j)
                    for g in range(4):
                        hp = hg[g % 2]
                        for nn in range(4):
                            n = g * 4 + nn
                            hk = f"hg{g % 2}_{nn}"
                            em.add("act", lambda b, n=n: b.activation(
                                a_t[:, :], dtt[:, :], AF.Identity,
                                scale=cvap(CV_NEG0 + n), bias=1.0),
                                reads=["dtt"], writes=["a_t"], sreads=["cons"], fd=TC)
                            em.add("dve", lambda b, n=n: b.tensor_mul(
                                b_t[:, :], dxt[:, :], brepb[:, n, :]),
                                reads=["dxt", "brepb"], writes=["b_t"], fd=TC)
                            if c == 0:
                                init, sr = 0.0, []
                            else:
                                init = carry[:, j * 16 + n: j * 16 + n + 1]
                                sr = ["carry"]
                            em.add("dve", lambda b, nn=nn, hp=hp, init=init:
                                   b.tensor_tensor_scan(
                                       hp[:, nn, :], a_t[:, :], b_t[:, :], init,
                                       ALU.mult, ALU.add),
                                   reads=["a_t", "b_t"], writes=[hk], sreads=sr, fd=TC)
                            yp, yk = yht[nn % 2], f"yht{nn % 2}"
                            em.add("pool", lambda b, nn=nn, hp=hp, yp=yp, n=n:
                                   b.tensor_mul(yp[:, :], hp[:, nn, :],
                                                crepb[:, n, :]),
                                   reads=[hk, "crepb"], writes=[yk], fd=TC)
                            em.add("pe", lambda b, yp=yp, s=(n == 0), e=(n == NST - 1):
                                   b.matmul(PB(PB_Y)[:, :TC], id_bf[:, :], yp[:, :],
                                            start=s, stop=e),
                                   reads=[yk, "id_bf"], writes=[PBK[PB_Y]], fd=TC)
                        em.add("dve", lambda b, j=j, g=g, hp=hp: b.tensor_copy(
                            carry[:, j * 16 + g * 4: j * 16 + g * 4 + 4],
                            hp[:, :, TC - 1]),
                            reads=[f"hg{g % 2}_{i}" for i in range(4)],
                            writes=["carry"], fd=4)
                    em.add("dve", lambda b, j=j: b.scalar_tensor_tensor(
                        y32[:, j, :], xi2[:, j, :], smap("Dp", l * 16 + j),
                        PB(PB_Y)[:, :TC], ALU.mult, ALU.add),
                        reads=["xi2", PBK[PB_Y]], writes=["y32"], sreads=["smal"], fd=TC)
                    ln_feed(y32[:, j, :], "y32", j, 16, TC)
                if debug and l == 0:
                    dma("dma_sp", dbg_y[:, t0:t0 + TC].rearrange("(a p) t -> p a t", p=128),
                        y32[:, :, :], ["y32"], ["dbg_y"])
                ln_rows(TC, CV_INVDI, CV_EPSA)
                for j in range(16):
                    sl = j % 2
                    ln_center(y32[:, j, :], "y32", TC, sl)
                    em.add("act", lambda b, j=j, lt=ln_t1[sl], cc=cvt[sl]: b.activation(
                        cc[:, :TC], lt[:, :TC], AF.Identity,
                        scale=smap("mln_g", l * 16 + j), bias=smap("mln_b", l * 16 + j)),
                        reads=[f"ln_t1_{sl}"], writes=[f"cvt{sl}"], sreads=["smal"], fd=TC)
                    em.add("dve", lambda b, j=j, cc=cvt[sl]: b.tensor_mul(
                        ygb[:, j, :], cc[:, :TC], sig[:, j, :]),
                        reads=[f"cvt{sl}", "sig"], writes=["ygb"], fd=TC)
                for j in range(8):
                    dma("dma_sp" if j % 2 == 0 else "dma_act", wost[:, j % 2, :, :],
                        woutb[l % 2][j].rearrange("a p n -> p a n"),
                        [f"dram_woutb{l % 2}"], [f"wost{j % 2}"])
                    bank = PB_MAIN[j % 2]
                    for k in range(16):
                        em.add("pe", lambda b, k=k, j=j, bank=bank, s=(k == 0),
                               e=(k == 15):
                               b.matmul(PB(bank)[:, :TC], wost[:, j % 2, k, :],
                                        ygb[:, k, :], start=s, stop=e),
                               reads=[f"wost{j % 2}", "ygb"], writes=[PBK[bank]], fd=TC)
                    em.add("dve", lambda b, j=j, bank=bank: b.tensor_add(
                        xat[:, j, :], PB(bank)[:, :TC], xat[:, j, :]),
                        reads=[PBK[bank], "xat"], writes=["xat"], fd=TC)
                dma("dma_sp", tok_slice(xa_out, t0, TC), xat[:, :, :], ["xat"], [ko])

    em.barrier()
    sta.close()

    stb = ExitStack()
    set_psum(stb, MC)
    op1T = sballoc(stb, "op1T", [128, 8, 1024], BF16)
    op2T = sballoc(stb, "op2T", [128, 8, 1024], BF16)
    mx32 = sballoc(stb, "mx32", [128, 8, MC])
    mxb = sballoc(stb, "mxb", [128, 8, MC], BF16)
    g1b = sballoc(stb, "g1b", [128, 8, MC], BF16)
    mst = sballoc(stb, "mst", [128, 8, MC])
    dma("dma_pool", op1T[:, :, :], wop1.rearrange("a p n -> p a n"), [], ["op1T"])
    dma("dma_pool", op2T[:, :, :], wop2.rearrange("a p n -> p a n"), [], ["op2T"])
    xa_fin = xaping[L % 2]
    kfin = f"dram_xa{L % 2}"
    for mc_i in range(NMC):
        t0 = mc_i * MC
        dma("dma_sp", mx32[:, :, :], tok_slice(xa_fin, t0, MC), [kfin], ["mx32"])
        for i in range(8):
            em.add("dve", lambda b, i=i: b.tensor_copy(mxb[:, i, :], mx32[:, i, :]),
                   reads=["mx32"], writes=["mxb"], fd=MC)
        for j in range(8):
            bank = PB_MAIN[j % 2]
            for k in range(8):
                em.add("pe", lambda b, j=j, k=k, bank=bank, s=(k == 0), e=(k == 7):
                       b.matmul(PB(bank)[:, :MC], op1T[:, k, j * 128:(j + 1) * 128],
                                mxb[:, k, :], start=s, stop=e),
                       reads=["op1T", "mxb"], writes=[PBK[bank]], fd=MC)
            em.add("act", lambda b, j=j, bank=bank: b.activation(
                g1b[:, j, :], PB(bank)[:, :MC], AF.Gelu, bias=smap("op1_b", j)),
                reads=[PBK[bank]], writes=["g1b"], sreads=["smal"], fd=MC)
        dma("dma_sp", mx32[:, :, :], tok_slice(xn32, t0, MC), ["dram_xn32"], ["mx32"])
        for j in range(8):
            bank = PB_MAIN[j % 2]
            for k in range(8):
                em.add("pe", lambda b, j=j, k=k, bank=bank, s=(k == 0), e=(k == 7):
                       b.matmul(PB(bank)[:, :MC], op2T[:, k, j * 128:(j + 1) * 128],
                                g1b[:, k, :], start=s, stop=e),
                       reads=["op2T", "g1b"], writes=[PBK[bank]], fd=MC)
            em.add("dve", lambda b, j=j, bank=bank: b.scalar_tensor_tensor(
                mst[:, j, :], PB(bank)[:, :MC], smap("op2_b", j), mx32[:, j, :],
                ALU.add, ALU.add),
                reads=[PBK[bank], "mx32"], writes=["mst"], sreads=["smal"], fd=MC)
        dma("dma_sp", tok_slice(s32, t0, MC), mst[:, :, :], ["mst"], ["dram_s32"])

    em.barrier()
    stb.close()

    stc = ExitStack()
    set_psum(stc, MC)
    gfm = sballoc(stc, "gfm", [8, TOK], BF16)
    sel8 = sballoc(stc, "sel8", [8, 8 * 128], BF16)
    dma("dma_pool", sel8[:, :], selp[0:8, 0:8 * 128], [], ["sel8"])
    std = ExitStack()
    rtb = sballoc(std, "rtb", [128, 8, 8])
    xng = sballoc(std, "xng", [128, 8, 128])
    ltok = sballoc(std, "ltok", [128, 8])
    gv1 = sballoc(std, "gv1", [128, 1])
    gm1 = sballoc(std, "gm1", [128, 8])
    gl2 = sballoc(std, "gl2", [128, 8])
    gv2 = sballoc(std, "gv2", [128, 1])
    gm2 = sballoc(std, "gm2", [128, 8])
    gdd = sballoc(std, "gdd", [128, 1])
    gee = sballoc(std, "gee", [128, 1])
    gsv = sballoc(std, "gsv", [128, 1])
    gw2 = sballoc(std, "gw2", [128, 1])
    ggt = sballoc(std, "ggt", [128, 8])
    dma("dma_sp", rtb[:, :, :], wrt.rearrange("a p n -> p a n"), [], ["rtb"])
    for tg in range(TOK // 128):
        t0 = tg * 128
        dma("dma_sp", xng[:, :, :], tok_slice(xn32, t0, 128), ["dram_xn32"], ["xng"])
        for k in range(8):
            em.add("pe", lambda b, k=k, s=(k == 0), e=(k == 7): b.matmul(
                PB(PB_MAIN[0])[:, 0:8], xng[:, k, :], rtb[:, k, :], start=s, stop=e),
                reads=["xng", "rtb"], writes=[PBK[PB_MAIN[0]]], fd=8)
        em.add("act", lambda b: b.copy(ltok[:, :], PB(PB_MAIN[0])[:, 0:8]),
               reads=[PBK[PB_MAIN[0]]], writes=["ltok"], fd=8)
        em.add("dve", lambda b: b.reduce_max(gv1[:, :], ltok[:, :], axis=AX.X),
               reads=["ltok"], writes=["gv1"], fd=8)
        em.add("dve", lambda b: b.tensor_scalar(gm1[:, :], ltok[:, :], gv1[:, :],
                                                None, ALU.is_ge),
               reads=["ltok"], writes=["gm1"], sreads=["gv1"], fd=8)
        em.add("dve", lambda b: b.scalar_tensor_tensor(gl2[:, :], gm1[:, :], -1e30,
                                                       ltok[:, :], ALU.mult, ALU.add),
               reads=["gm1", "ltok"], writes=["gl2"], fd=8)
        em.add("dve", lambda b: b.reduce_max(gv2[:, :], gl2[:, :], axis=AX.X),
               reads=["gl2"], writes=["gv2"], fd=8)
        em.add("dve", lambda b: b.tensor_scalar(gm2[:, :], gl2[:, :], gv2[:, :],
                                                None, ALU.is_ge),
               reads=["gl2"], writes=["gm2"], sreads=["gv2"], fd=8)
        em.add("dve", lambda b: b.tensor_sub(gdd[:, :], gv2[:, :], gv1[:, :]),
               reads=["gv2", "gv1"], writes=["gdd"], fd=1)
        em.add("act", lambda b: b.activation(gee[:, :], gdd[:, :], AF.Exp),
               reads=["gdd"], writes=["gee"], fd=1)
        em.add("dve", lambda b: b.tensor_scalar_add(gsv[:, :], gee[:, :], 1.0),
               reads=["gee"], writes=["gsv"], fd=1)
        em.add("dve", lambda b: b.reciprocal(gsv[:, :], gsv[:, :]),
               reads=["gsv"], writes=["gsv"], fd=1)
        em.add("dve", lambda b: b.tensor_mul(gw2[:, :], gee[:, :], gsv[:, :]),
               reads=["gee", "gsv"], writes=["gw2"], fd=1)
        em.add("dve", lambda b: b.tensor_scalar(ggt[:, :], gm1[:, :], gsv[:, :],
                                                None, ALU.mult),
               reads=["gm1"], writes=["ggt"], sreads=["gsv"], fd=8)
        em.add("dve", lambda b: b.scalar_tensor_tensor(ggt[:, :], gm2[:, :], gw2[:, :],
                                                       ggt[:, :], ALU.mult, ALU.add),
               reads=["gm2", "ggt"], writes=["ggt"], sreads=["gw2"], fd=8)
        em.add("pe", lambda b: b.transpose(PB(PB_MAIN[1])[0:8, 0:128], ggt[:, :],
                                           ident[:, :]),
               reads=["ggt", "ident"], writes=[PBK[PB_MAIN[1]]], fd=128)
        em.add("act", lambda b, t0=t0: b.copy(gfm[:, t0:t0 + 128],
                                              PB(PB_MAIN[1])[0:8, 0:128]),
               reads=[PBK[PB_MAIN[1]]], writes=["gfm"], fd=128)

    em.barrier()
    std.close()

    ste = ExitStack()
    w1h = sballoc(ste, "w1h", [128, 8, DFF // 2], BF16)
    w2h = sballoc(ste, "w2h", [128, 16, H], BF16)
    exb = sballoc(ste, "exb", [128, 8, MC], BF16)
    ex32 = sballoc(ste, "ex32", [128, 8, MC])
    h1h = sballoc(ste, "h1h", [128, 16, MC], BF16)
    rest = sballoc(ste, "rest", [128, 8, MC])
    ptile = sballoc(ste, "ptile", [128, 8, MC])
    for e in range(NEXP):
        for hf in range(2):
            f0 = hf * 16
            for a in range(8):
                dma("dma_pool", w1h[:, a, :],
                    w1e[e, a, :, hf * (DFF // 2):(hf + 1) * (DFF // 2)], [], ["w1h"])
            for a in range(16):
                dma("dma_pool", w2h[:, a, :], w2e[e, f0 + a], [], ["w2h"])
            for mc_i in range(NMC):
                t0 = mc_i * MC
                dma("dma_sp", exb[:, :, :], tok_slice(xnbf, t0, MC),
                    ["dram_xnbf"], ["exb"])
                for f in range(16):
                    bank = PB_MAIN[f % 2]
                    for k in range(8):
                        em.add("pe", lambda b, f=f, k=k, bank=bank, s=(k == 0),
                               e_=(k == 7): b.matmul(
                                   PB(bank)[:, :MC], w1h[:, k, f * 128:(f + 1) * 128],
                                   exb[:, k, :], start=s, stop=e_),
                               reads=["w1h", "exb"], writes=[PBK[bank]], fd=MC)
                    em.add("act", lambda b, f=f, e=e, f0=f0, bank=bank: b.activation(
                        h1h[:, f, :], PB(bank)[:, :MC], AF.Gelu,
                        bias=smap("b1", e * 32 + f0 + f)),
                        reads=[PBK[bank]], writes=["h1h"], sreads=["smal"], fd=MC)
                if hf == 1:
                    dma("dma_act", ptile[:, :, :], tok_slice(pdram, t0, MC),
                        ["dram_pdram"], ["ptile"])
                    dma("dma_act", ex32[:, :, :], tok_slice(xn32, t0, MC),
                        ["dram_xn32"], ["ex32"])
                    em.add("pe", lambda b, e=e, t0=t0: b.matmul(
                        PB(PB_DT)[:, :MC], sel8[:, e * 128:(e + 1) * 128],
                        gfm[:, t0:t0 + MC], start=True, stop=True),
                        reads=["gfm", "sel8"], writes=[PBK[PB_DT]], fd=MC)
                for dj in range(8):
                    bank = PB_MAIN[dj % 2]
                    for k in range(16):
                        em.add("pe", lambda b, dj=dj, k=k, bank=bank, s=(k == 0),
                               e_=(k == 15): b.matmul(
                                   PB(bank)[:, :MC], w2h[:, k, dj * 128:(dj + 1) * 128],
                                   h1h[:, k, :], start=s, stop=e_),
                               reads=["w2h", "h1h"], writes=[PBK[bank]], fd=MC)
                    if hf == 0:
                        em.add("act", lambda b, dj=dj, bank=bank: b.copy(
                            ptile[:, dj, :], PB(bank)[:, :MC]),
                            reads=[PBK[bank]], writes=["ptile"], fd=MC)
                    else:
                        em.add("dve", lambda b, dj=dj, bank=bank: b.tensor_add(
                            ptile[:, dj, :], PB(bank)[:, :MC], ptile[:, dj, :]),
                            reads=[PBK[bank], "ptile"], writes=["ptile"], fd=MC)
                        em.add("dve", lambda b, dj=dj, e=e: b.scalar_tensor_tensor(
                            rest[:, dj, :], ptile[:, dj, :], smap("b2", e * 8 + dj),
                            ex32[:, dj, :], ALU.add, ALU.add),
                            reads=["ptile", "ex32"], writes=["rest"],
                            sreads=["smal"], fd=MC)
                if hf == 0:
                    dma("dma_act", tok_slice(pdram, t0, MC), ptile[:, :, :],
                        ["ptile"], ["dram_pdram"])
                    continue
                for i in range(8):
                    ln_feed(rest[:, i, :], "rest", i, 8, MC)
                ln_rows(MC, CV_INVH, CV_EPSM)
                for dj in range(8):
                    sl = dj % 2
                    ln_center(rest[:, dj, :], "rest", MC, sl)
                    em.add("act", lambda b, dj=dj, e=e, lt=ln_t1[sl], cc=cvt[sl]:
                           b.activation(
                               cc[:, :MC], lt[:, :MC], AF.Identity,
                               scale=smap("gamma", e * 8 + dj),
                               bias=smap("beta", e * 8 + dj)),
                           reads=[f"ln_t1_{sl}"], writes=[f"cvt{sl}"],
                           sreads=["smal"], fd=MC)
                    em.add("dve", lambda b, dj=dj, cc=cvt[sl]: b.tensor_mul(
                        rest[:, dj, :], cc[:, :MC], PB(PB_DT)[:, :MC]),
                        reads=[f"cvt{sl}", PBK[PB_DT]], writes=["rest"], fd=MC)
                if e == 0:
                    dma("dma_pool", tok_slice(ymoe, t0, MC), rest[:, :, :],
                        ["rest"], ["dram_ymoe"])
                else:
                    dma_acc(tok_slice(ymoe, t0, MC), rest[:, :, :],
                            ["rest"], ["dram_ymoe"])

    em.barrier()
    ste.close()
    stc.close()

    stf = ExitStack()
    cs = sballoc(stf, "cs", [128, 8, MC])
    cy = sballoc(stf, "cy", [128, 8, MC])
    cx = sballoc(stf, "cx", [128, 8, MC])
    co = sballoc(stf, "co", [128, 8, MC])
    for mc_i in range(NMC):
        t0 = mc_i * MC
        dma("dma_sp", cs[:, :, :], tok_slice(s32, t0, MC), ["dram_s32"], ["cs"])
        dma("dma_act", cy[:, :, :], tok_slice(ymoe, t0, MC), ["dram_ymoe"], ["cy"])
        dma("dma_sp", cx[:, :, :], tok_slice(x0, t0, MC), ["dram_x0"], ["cx"])
        for i in range(8):
            em.add("act", lambda b, i=i: b.activation(cy[:, i, :], cy[:, i, :], AF.Silu),
                   reads=["cy"], writes=["cy"], fd=MC)
            em.add("dve", lambda b, i=i: b.tensor_mul(co[:, i, :], cs[:, i, :],
                                                      cy[:, i, :]),
                   reads=["cs", "cy"], writes=["co"], fd=MC)
            em.add("dve", lambda b, i=i: b.tensor_add(co[:, i, :], co[:, i, :],
                                                      cx[:, i, :]),
                   reads=["co", "cx"], writes=["co"], fd=MC)
        dma("dma_sp", tok_slice(out_fm, t0, MC), co[:, :, :], ["co"], ["dram_out"])

    em.barrier()
    em.emit(stx)
    stf.close()
    stx.close()
    return nc



def _pack_smalls(ip):
    sm = np.zeros((128, SM_COLS), np.float32)

    def put(name, vec, nblk, idx0=0):
        v = np.asarray(vec, np.float32).reshape(nblk, 128).T
        sm[:, SM[name] + idx0:SM[name] + idx0 + nblk] = v
    put("in_norm_g", ip["L0_attn_in_norm_g"], 8)
    put("in_norm_b", ip["L0_attn_in_norm_b"], 8)
    for l in range(L):
        put("ln_g", ip["L0_attn_ln_g"][l], 8, l * 8)
        put("ln_b", ip["L0_attn_ln_b"][l], 8, l * 8)
        for ci in range(2):
            for k in range(KCONV):
                put("conv_w", ip["L0_attn_conv_w"][l, ci, :, k], 16,
                    l * (2 * KCONV * 16) + ci * (KCONV * 16) + k * 16)
            put("conv_b", ip["L0_attn_conv_b"][l, ci], 16, l * 32 + ci * 16)
        put("dt_w", ip["L0_attn_dt_proj_w"][l, :, 0], 16, l * 16)
        put("dt_b", ip["L0_attn_dt_proj_b"][l], 16, l * 16)
        put("Dp", ip["L0_attn_D"][l], 16, l * 16)
        put("mln_g", ip["L0_attn_mamba_ln_g"][l], 16, l * 16)
        put("mln_b", ip["L0_attn_mamba_ln_b"][l], 16, l * 16)
    put("op1_b", ip["L0_attn_op1_b"], 8)
    put("op2_b", ip["L0_attn_op2_b"], 8)
    for e in range(NEXP):
        put("b1", ip["L0_ffn_b1"][e], 32, e * 32)
        put("b2", ip["L0_ffn_b2"][e], 8, e * 8)
        put("gamma", ip["L0_ffn_gamma"][e], 8, e * 8)
        put("beta", ip["L0_ffn_beta"][e], 8, e * 8)
    return sm


def _host_shared(ip):
    f = np.float32
    arrs = {}
    w = np.asarray(ip["L0_attn_in_proj_w"], f)
    arrs["win"] = np.ascontiguousarray(
        w.reshape(L, 32, 128, 8, 128).transpose(0, 1, 3, 4, 2))
    w = np.asarray(ip["L0_attn_out_proj_w"], f)
    arrs["wout"] = np.ascontiguousarray(
        w.reshape(L, 8, 128, 16, 128).transpose(0, 1, 3, 4, 2))
    w = np.asarray(ip["L0_attn_x_proj_w"], f)
    arrs["wxp"] = np.ascontiguousarray(w.transpose(0, 2, 1).reshape(L, 16, 128, 33))
    arrs["wop1"] = np.ascontiguousarray(
        np.asarray(ip["L0_attn_op1_w"], f).T.reshape(8, 128, 1024))
    arrs["wop2"] = np.ascontiguousarray(
        np.asarray(ip["L0_attn_op2_w"], f).T.reshape(8, 128, 1024))
    arrs["wrt"] = np.ascontiguousarray(
        np.asarray(ip["L0_ffn_router_w"], f).reshape(8, 128, 8))
    arrs["w1e"] = np.ascontiguousarray(
        np.asarray(ip["L0_ffn_w1"], f).reshape(NEXP, 8, 128, DFF))
    arrs["w2e"] = np.ascontiguousarray(
        np.asarray(ip["L0_ffn_w2"], f).reshape(NEXP, 32, 128, H))
    arrs["smalls"] = _pack_smalls(ip)
    cvv = np.zeros((128, NCONST), f)
    cvv[:, CV_ONE] = 1.0
    cvv[:, CV_EPSA] = EPS_A
    cvv[:, CV_EPSM] = EPS_M
    cvv[:, CV_NH] = -0.5
    cvv[:, CV_INVH] = 1.0 / H
    cvv[:, CV_INVDI] = 1.0 / DI
    for n in range(NST):
        cvv[:, CV_NEG0 + n] = -(float(n + 1))
    arrs["consts"] = cvv
    sel = np.zeros((33, 34 * 128), f)
    for i in range(33):
        sel[i, i * 128:(i + 1) * 128] = 1.0
    arrs["selp"] = sel
    arrs["idmat"] = np.eye(128, dtype=f)
    arrs["onescol"] = np.ones((128, 1), f)
    arrs["onesrow"] = np.ones((1, 128), f)
    return arrs


def kernel(**inputs):
    hs = np.asarray(inputs["L0_attn_hidden_states"], np.float32)
    shared = _host_shared(inputs)
    nc = build_program(debug=False)
    in_maps = []
    for c in range(NCORE):
        m = dict(shared)
        xc = hs[c * BC:(c + 1) * BC]
        m["x0"] = np.ascontiguousarray(xc.transpose(2, 0, 1).reshape(H, TOK))
        in_maps.append(m)
    res = run_bass_kernel_spmd(nc, in_maps, core_ids=list(range(NCORE)))
    outs = []
    for c in range(NCORE):
        o = np.asarray(res.results[c]["out_fm"])
        outs.append(o.reshape(H, BC, S).transpose(1, 2, 0))
    return np.concatenate(outs, axis=0).astype(np.float32)


if __name__ == "__main__":
    import time
    t = time.time()
    nc = build_program()
    print(f"built in {time.time() - t:.1f}s")


# version 05
# speedup vs baseline: 1.1111x; 1.0297x over previous
import numpy as np
from contextlib import ExitStack

import concourse.bass as bass
import concourse.mybir as mybir
from concourse.bass_utils import run_bass_kernel_spmd

F32 = mybir.dt.float32
BF16 = mybir.dt.bfloat16
AF = mybir.ActivationFunctionType
ALU = mybir.AluOpType
AX = mybir.AxisListType

B_TOT, S, H, L, DI, NST, KCONV = 16, 2048, 1024, 4, 2048, 16, 4
DFF, NEXP, NCORE = 4096, 8, 8
BC = B_TOT // NCORE
TOK = BC * S
TC = 256
NCH = S // TC
MC = 512
NMC = TOK // MC
EPS_A, EPS_M = 1e-5, 1e-6
FIFO_SAFE_FD = 128

ENGINES = ("pe", "dve", "act", "pool", "sp")
DMAQ = ("dma_sp", "dma_act", "dma_pool")


def _sm_layout():
    cols, c = {}, 0

    def take(name, n):
        nonlocal c
        cols[name] = c
        c += n
    take("in_norm_g", 8); take("in_norm_b", 8)
    take("ln_g", L * 8); take("ln_b", L * 8)
    take("conv_w", L * 2 * KCONV * 16); take("conv_b", L * 2 * 16)
    take("dt_w", L * 16); take("dt_b", L * 16)
    take("Dp", L * 16)
    take("mln_g", L * 16); take("mln_b", L * 16)
    take("op1_b", 8); take("op2_b", 8)
    take("b1", NEXP * 32); take("b2", NEXP * 8)
    take("gamma", NEXP * 8); take("beta", NEXP * 8)
    return cols, c


SM, SM_COLS = _sm_layout()
CV_ONE, CV_EPSA, CV_EPSM, CV_NH, CV_INVH, CV_INVDI = 0, 1, 2, 3, 4, 5
CV_NEG0 = 6
NCONST = 22


class Op:
    __slots__ = ("eng", "fn", "reads", "writes", "sreads", "fd", "idx",
                 "inc_count", "waits", "dma", "meta")

    def __init__(self, eng, fn, reads, writes, sreads, fd, idx, dma):
        self.eng, self.fn = eng, fn
        self.reads, self.writes, self.sreads = list(reads), list(writes), list(sreads)
        self.fd, self.idx, self.dma = fd, idx, dma
        self.inc_count, self.waits = None, []
        self.meta = None


DMA_SLOTS = {"dma_sp": 1, "dma_act": 1, "dma_pool": 8}


class Emitter:

    def __init__(self, nc):
        self.nc = nc
        self.ops = []
        self.last_writer = {}
        self.readers = {}
        self.counts = {}
        for e in ENGINES:
            self.counts[e] = 0
        self.dma_qkeys = []
        for q, ns in DMA_SLOTS.items():
            for s in range(ns):
                k = f"{q}#{s}"
                self.counts[k] = 0
                self.dma_qkeys.append(k)
        self.dma_rr = {q: 0 for q in DMA_SLOTS}
        self.last_dma_on_slot = {}
        self.meta_provider = None
        self.meta_apply = None

    @staticmethod
    def _same_stream(a, b):
        if a.dma is not None or b.dma is not None:
            return False
        return a.eng == b.eng

    def _need_edge(self, prod, cons, scalar=False):
        if prod is None:
            return False
        if scalar:
            return True
        if self._same_stream(prod, cons):
            if prod.eng == "pe":
                return False
            return prod.fd < FIFO_SAFE_FD or cons.fd < FIFO_SAFE_FD
        return True

    def _assign_inc(self, d):
        if d.inc_count is None:
            d.inc_count = -1
            q = d.dma if d.dma is not None else d.eng
            self.counts[q] += 16 if d.dma is not None else 1

    def add(self, eng, fn, reads=(), writes=(), sreads=(), fd=512, dma=None):
        if dma is not None:
            slot = self.dma_rr[dma]
            self.dma_rr[dma] = (slot + 1) % DMA_SLOTS[dma]
            qkey = f"{dma}#{slot}"
        else:
            qkey = None
        op = Op(eng, fn, reads, writes, sreads, fd, len(self.ops), qkey)
        if self.meta_provider is not None:
            op.meta = self.meta_provider()
        deps = set()
        for b in op.reads:
            w = self.last_writer.get(b)
            if self._need_edge(w, op):
                deps.add(w)
        for b in op.sreads:
            w = self.last_writer.get(b)
            if self._need_edge(w, op, scalar=True):
                deps.add(w)
        for b in op.writes:
            w = self.last_writer.get(b)
            if self._need_edge(w, op):
                deps.add(w)
            for r in self.readers.get(b, ()):
                if r is not op and self._need_edge(r, op):
                    deps.add(r)
        by_q = {}
        for d in deps:
            q = d.dma if d.dma is not None else d.eng
            if d.dma is not None:
                by_q[(q, d.idx)] = d
            else:
                cur = by_q.get(q)
                if cur is None or cur.idx < d.idx:
                    by_q[q] = d
        if qkey is not None:
            prev = self.last_dma_on_slot.get(qkey)
            if prev is not None:
                op.waits.append(prev)
            self.last_dma_on_slot[qkey] = op
        for d in sorted(by_q.values(), key=lambda o: o.idx):
            self._assign_inc(d)
            op.waits.append(d)
        self.ops.append(op)
        if qkey is not None:
            self._assign_inc(op)
        for b in op.reads + op.sreads:
            self.readers.setdefault(b, []).append(op)
        for b in op.writes:
            self.last_writer[b] = op
            self.readers[b] = []
        return op

    def barrier(self):
        last_by_q = {}
        for o in self.ops:
            q = o.dma if o.dma is not None else o.eng
            last_by_q[q] = o
        for q, o in last_by_q.items():
            self._assign_inc(o)
        for eng in ENGINES:
            op = Op(eng, lambda b: b.nop(), [], [], [], 1, len(self.ops), None)
            for q, o in last_by_q.items():
                if q != eng:
                    op.waits.append(o)
            self.ops.append(op)

    def emit(self, st):
        nc = self.nc
        allq = list(ENGINES) + self.dma_qkeys
        ordinals = {q: 0 for q in allq}
        for op in self.ops:
            if op.inc_count is not None:
                q = op.dma if op.dma is not None else op.eng
                ordinals[q] += 16 if op.dma is not None else 1
                op.inc_count = ordinals[q]
        for q in allq:
            assert ordinals[q] < 60000, f"sem overflow {q}: {ordinals[q]}"
        sems = {q: st.enter_context(nc.semaphore(f"sem_{q.replace('#','_')}"))
                for q in allq}
        block = st.enter_context(nc.Block())
        per_eng = {e: [] for e in ENGINES}
        for op in self.ops:
            per_eng[op.eng].append(op)
        streams = {"pe": block.tensor, "dve": block.vector, "act": block.scalar,
                   "pool": block.gpsimd, "sp": block.sync}

        for eng in ENGINES:
            ops = per_eng[eng]

            def make(ops=ops):
                def body(b):
                    clock = {q: 0 for q in allq}
                    for op in ops:
                        for d in op.waits:
                            q = d.dma if d.dma is not None else d.eng
                            v = d.inc_count
                            if clock[q] < v:
                                b.wait_ge(sems[q], v)
                                clock[q] = v
                        if op.meta is not None and self.meta_apply is not None:
                            self.meta_apply(op.meta)
                        inst = op.fn(b)
                        if op.inc_count is not None:
                            q = op.dma if op.dma is not None else op.eng
                            step = 16 if op.dma is not None else 1
                            inst.then_inc(sems[q], step)
                return body
            streams[eng](make())


def build_program(debug=False):
    nc = bass.Bass("TRN2", target_bir_lowering=False, detect_race_conditions=False)
    em = Emitter(nc)
    stx = ExitStack()

    def din(name, shape, dt=F32):
        return nc.dram_tensor(name, list(shape), dt, kind="ExternalInput").ap()

    def dout(name, shape, dt=F32):
        return nc.dram_tensor(name, list(shape), dt, kind="ExternalOutput").ap()

    def dint(name, shape, dt=F32):
        kind = "ExternalOutput" if debug else "Internal"
        return nc.dram_tensor(name, list(shape), dt, kind=kind).ap()

    x0 = din("x0", [H, TOK])
    win = din("win", [L, 32, 8, 128, 128])
    wout = din("wout", [L, 8, 16, 128, 128])
    wxp = din("wxp", [L, 16, 128, 33])
    wop1 = din("wop1", [8, 128, 1024])
    wop2 = din("wop2", [8, 128, 1024])
    wrt = din("wrt", [8, 128, 8])
    w1e = din("w1e", [NEXP, 8, 128, DFF])
    w2e = din("w2e", [NEXP, 32, 128, H])
    smalls = din("smalls", [128, SM_COLS])
    consts = din("consts", [128, NCONST])
    selp = din("selp", [33, 34 * 128])
    idmat = din("idmat", [128, 128])
    onescol = din("onescol", [128, 1])
    onesrow = din("onesrow", [1, 128])

    out_fm = dout("out_fm", [H, TOK])
    xn32 = dint("xn32", [H, TOK])
    xnbf = dint("xnbf", [H, TOK], BF16)
    xaping = [dint("xa0", [H, TOK]), dint("xa1", [H, TOK])]
    s32 = dint("s32", [H, TOK])
    ymoe = dint("ymoe", [H, TOK])
    pdram = dint("pdram", [H, TOK])
    winb = [dint(f"winb{i}", [32, 8, 128, 128], BF16) for i in range(2)]
    woutb = [dint(f"woutb{i}", [8, 16, 128, 128], BF16) for i in range(2)]
    if debug:
        dbg_xl = dout("dbg_xl", [H, TOK], BF16)
        dbg_xi2 = dout("dbg_xi2", [DI, TOK])
        dbg_proj = dout("dbg_proj", [33, TOK])
        dbg_y = dout("dbg_y", [DI, TOK])
        dbg_sig = dout("dbg_sig", [DI, TOK])

    def sballoc(st, name, shape, dt=F32):
        return st.enter_context(nc.sbuf_tensor(name, shape, dt))

    cons = sballoc(stx, "cons", [128, NCONST])
    smal = sballoc(stx, "smal", [128, SM_COLS])
    ident = sballoc(stx, "ident", [128, 128])
    id_bf = sballoc(stx, "id_bf", [128, 128], BF16)
    ones_bf = sballoc(stx, "ones_bf", [128, 1], BF16)
    onesrow_f = sballoc(stx, "onesrow_f", [1, 128])
    onesrow_b = sballoc(stx, "onesrow_b", [1, 128], BF16)
    ln_bf = [sballoc(stx, f"ln_bf{i}", [128, MC], BF16) for i in range(2)]
    ln_sq = [sballoc(stx, f"ln_sq{i}", [128, MC], BF16) for i in range(2)]
    ln_t1 = [sballoc(stx, f"ln_t1_{i}", [128, MC]) for i in range(2)]
    cvt = [sballoc(stx, f"cvt{i}", [128, MC]) for i in range(2)]
    rows = sballoc(stx, "rows", [1, 5 * MC])

    def row(i, width):
        return rows[:, i * MC:i * MC + width]
    R_MU, R_M2, R_VAR, R_R, R_MR = range(5)

    _pb_gens = {}
    _pbcur = {"t": None, "gen": 0}

    def set_psum(st_, width):
        _pbcur["gen"] += 1
        g = _pbcur["gen"]
        _pbcur["t"] = [st_.enter_context(
            nc.psum_tensor(f"pb{g}_{i}", [128, width], F32)) for i in range(8)]
        _pb_gens[g] = _pbcur["t"]

    def PB(i):
        return _pbcur["t"][i]

    em.meta_provider = lambda: _pbcur["gen"]
    em.meta_apply = lambda g: (_pbcur.update(t=_pb_gens[g]) if g in _pb_gens else None)

    PBK = [f"pb{i}" for i in range(8)]
    PB_MAIN = (0, 1)
    PB_R, PB_M, PB_Y, PB_DT, PB_S1, PB_S2 = 2, 3, 4, 5, 6, 7

    def dma(queue, out_ap, in_ap, reads, writes):
        eng = {"dma_sp": "sp", "dma_act": "act", "dma_pool": "pool"}[queue]
        em.add(eng, lambda b, o=out_ap, i=in_ap: b.dma_start(o, i),
               reads=reads, writes=writes, fd=512, dma=queue)

    def dma_acc(out_ap, in_ap, reads, writes):
        em.add("pool", lambda b, o=out_ap, i=in_ap: b.dma_start(o, i, accum_op=ALU.add),
               reads=reads, writes=writes, fd=512, dma="dma_pool")

    def cvap(col, p1=False):
        a = cons[:, col:col + 1]
        return a[0:1, :] if p1 else a

    def smap(name, idx):
        c = SM[name] + idx
        return smal[:, c:c + 1]

    dma("dma_sp", cons[:, :], consts, [], ["cons"])
    dma("dma_sp", smal[:, :], smalls, [], ["smal"])
    dma("dma_sp", ident[:, :], idmat, [], ["ident"])
    dma("dma_pool", id_bf[:, :], idmat, [], ["id_bf"])
    dma("dma_pool", ones_bf[:, :], onescol, [], ["ones_bf"])
    dma("dma_sp", onesrow_f[:, :], onesrow, [], ["onesrow_f"])
    dma("dma_pool", onesrow_b[:, :], onesrow, [], ["onesrow_b"])

    def tok_slice(d_ap, t0, w):
        return d_ap[:, t0:t0 + w].rearrange("(a p) t -> p a t", p=128)

    def ln_feed(ap, key, i, nblk, width, s1=PB_S1, s2=PB_S2):
        sl = i % 2
        lb, ls = ln_bf[sl], ln_sq[sl]
        em.add("act", lambda b, a=ap, lb=lb: b.copy(lb[:, :width], a),
               reads=[key], writes=[f"ln_bf{sl}"], fd=width)
        em.add("pe", lambda b, s=(i == 0), e=(i == nblk - 1), lb=lb: b.matmul(
            PB(s1)[0:1, :width], ones_bf[:, :], lb[:, :width], start=s, stop=e),
            reads=[f"ln_bf{sl}", "ones_bf"], writes=[PBK[s1]], fd=width)
        em.add("act", lambda b, a=ap, ls=ls: b.activation(ls[:, :width], a, AF.Square),
               reads=[key], writes=[f"ln_sq{sl}"], fd=width)
        em.add("pe", lambda b, s=(i == 0), e=(i == nblk - 1), ls=ls: b.matmul(
            PB(s2)[0:1, :width], ones_bf[:, :], ls[:, :width], start=s, stop=e),
            reads=[f"ln_sq{sl}", "ones_bf"], writes=[PBK[s2]], fd=width)

    def ln_rows(width, inv_col, eps_col):
        em.add("act", lambda b: b.activation(row(R_MU, width), PB(PB_S1)[0:1, :width],
                                             AF.Identity, scale=cvap(inv_col, True)),
               reads=[PBK[PB_S1]], writes=["rows"], sreads=["cons"], fd=width)
        em.add("act", lambda b: b.activation(row(R_M2, width), PB(PB_S2)[0:1, :width],
                                             AF.Identity, scale=cvap(inv_col, True)),
               reads=[PBK[PB_S2]], writes=["rows"], sreads=["cons"], fd=width)
        em.add("dve", lambda b: b.tensor_mul(row(R_VAR, width), row(R_MU, width),
                                             row(R_MU, width)),
               reads=["rows"], writes=["rows"], fd=width)
        em.add("dve", lambda b: b.tensor_sub(row(R_VAR, width), row(R_M2, width),
                                             row(R_VAR, width)),
               reads=["rows"], writes=["rows"], fd=width)
        em.add("act", lambda b: b.activation(row(R_R, width), row(R_VAR, width), AF.Ln,
                                             bias=cvap(eps_col, True)),
               reads=["rows"], writes=["rows"], sreads=["cons"], fd=width)
        em.add("act", lambda b: b.activation(row(R_R, width), row(R_R, width), AF.Exp,
                                             scale=cvap(CV_NH, True)),
               reads=["rows"], writes=["rows"], sreads=["cons"], fd=width)
        em.add("dve", lambda b: b.tensor_mul(row(R_MR, width), row(R_MU, width),
                                             row(R_R, width)),
               reads=["rows"], writes=["rows"], fd=width)
        em.add("pe", lambda b: b.matmul(PB(PB_R)[:, :width], onesrow_f[:, :],
                                        row(R_R, width), start=True, stop=True),
               reads=["rows", "onesrow_f"], writes=[PBK[PB_R]], fd=width)
        em.add("pe", lambda b: b.matmul(PB(PB_M)[:, :width], onesrow_f[:, :],
                                        row(R_MR, width), start=True, stop=True),
               reads=["rows", "onesrow_f"], writes=[PBK[PB_M]], fd=width)

    def ln_center(ap, key, width, sl=0):
        lt = ln_t1[sl]
        em.add("dve", lambda b, a=ap, lt=lt: b.tensor_mul(lt[:, :width], a,
                                                          PB(PB_R)[:, :width]),
               reads=[key, PBK[PB_R]], writes=[f"ln_t1_{sl}"], fd=width)
        em.add("dve", lambda b, lt=lt: b.tensor_sub(lt[:, :width], lt[:, :width],
                                                    PB(PB_M)[:, :width]),
               reads=[f"ln_t1_{sl}", PBK[PB_M]], writes=[f"ln_t1_{sl}"], fd=width)

    sta = ExitStack()
    set_psum(sta, TC)
    xat = sballoc(sta, "xat", [128, 8, TC])
    xnt = sballoc(sta, "xnt", [128, 8, TC])
    xnbt = sballoc(sta, "xnbt", [128, 8, TC], BF16)
    xabt = sballoc(sta, "xabt", [128, 8, TC], BF16)
    sel_bf = sballoc(sta, "sel_bf", [33, 34 * 128], BF16)
    sel_f0 = sballoc(sta, "sel_f0", [33, 128])
    xpT = sballoc(sta, "xpT", [128, 16, 33])
    wst = sballoc(sta, "wst", [128, 3, 8, 128], BF16)
    wost = sballoc(sta, "wost", [128, 2, 16, 128], BF16)
    xi0 = sballoc(sta, "xi0", [128, 16, TC + 3])
    xi1 = sballoc(sta, "xi1", [128, 16, TC + 3])
    xi2 = sballoc(sta, "xi2", [128, 16, TC])
    sig = sballoc(sta, "sig", [128, 16, TC])
    projf = sballoc(sta, "projf", [33, TC])
    projb = sballoc(sta, "projb", [33, TC], BF16)
    brepb = sballoc(sta, "brepb", [128, 16, TC], BF16)
    crepb = sballoc(sta, "crepb", [128, 16, TC], BF16)
    zt2 = [sballoc(sta, f"zt{i}", [128, TC]) for i in range(2)]
    mt2 = [sballoc(sta, f"mt{i}", [128, TC]) for i in range(2)]
    t1t2 = [sballoc(sta, f"t1t{i}", [128, TC]) for i in range(2)]
    dtt2 = [sballoc(sta, f"dtt{i}", [128, TC]) for i in range(2)]
    dxt2 = [sballoc(sta, f"dxt{i}", [128, TC]) for i in range(2)]
    a_t = [sballoc(sta, f"a_t{i}", [128, TC]) for i in range(2)]
    b_t = sballoc(sta, "b_t", [128, TC])
    hg = [sballoc(sta, f"hg{i}", [128, 4, TC]) for i in range(2)]
    yht = [sballoc(sta, f"yht{i}", [128, TC], BF16) for i in range(2)]
    carry = sballoc(sta, "carry", [128, 256])
    y32 = sballoc(sta, "y32", [128, 16, TC])
    ygb = sballoc(sta, "ygb", [128, 16, TC], BF16)

    dma("dma_pool", sel_bf[:, :], selp, [], ["sel_bf"])
    dma("dma_sp", sel_f0[:, :], selp[:, 0:128], [], ["sel_f0"])

    for sq in range(BC):
        for c in range(NCH):
            t0 = sq * S + c * TC
            dma("dma_sp", xat[:, :, :], tok_slice(x0, t0, TC), ["dram_x0"], ["xat"])
            for i in range(8):
                ln_feed(xat[:, i, :], "xat", i, 8, TC)
            ln_rows(TC, CV_INVH, CV_EPSA)
            for i in range(8):
                sl = i % 2
                ln_center(xat[:, i, :], "xat", TC, sl)
                em.add("act", lambda b, i=i, lt=ln_t1[sl]: b.copy(xnt[:, i, :], lt[:, :TC]),
                       reads=[f"ln_t1_{sl}"], writes=["xnt"], fd=TC)
                em.add("act", lambda b, i=i, lt=ln_t1[sl]: b.copy(xnbt[:, i, :], lt[:, :TC]),
                       reads=[f"ln_t1_{sl}"], writes=["xnbt"], fd=TC)
            dma("dma_sp", tok_slice(xn32, t0, TC), xnt[:, :, :], ["xnt"], ["dram_xn32"])
            dma("dma_act", tok_slice(xnbf, t0, TC), xnbt[:, :, :], ["xnbt"], ["dram_xnbf"])
            for i in range(8):
                ln_feed(xnt[:, i, :], "xnt", i, 8, TC)
            ln_rows(TC, CV_INVH, CV_EPSA)
            for i in range(8):
                sl = i % 2
                ln_center(xnt[:, i, :], "xnt", TC, sl)
                em.add("act", lambda b, i=i, lt=ln_t1[sl]: b.activation(
                    xat[:, i, :], lt[:, :TC], AF.Identity,
                    scale=smap("in_norm_g", i), bias=smap("in_norm_b", i)),
                    reads=[f"ln_t1_{sl}"], writes=["xat"], sreads=["smal"], fd=TC)
            dma("dma_sp", tok_slice(xaping[0], t0, TC), xat[:, :, :], ["xat"], ["dram_xa0"])

    def conv_block(l, c, j):
        for (src, srck, ci) in ((xi0, "xi0", 0), (xi1, "xi1", 1)):
            if c == 0:
                em.add("dve", lambda b, src=src, j=j: b.memset(src[:, j, 0:3], 0.0),
                       reads=[], writes=[srck], fd=3)
            wbase = SM["conv_w"] + l * (2 * KCONV * 16) + ci * (KCONV * 16)
            cc, cck = cvt[j % 2], f"cvt{j % 2}"
            em.add("dve", lambda b, src=src, j=j, cc=cc,
                   col=wbase + (KCONV - 1) * 16 + j:
                   b.tensor_scalar(
                       cc[:, :TC], src[:, j, 3:3 + TC], smal[:, col:col + 1],
                       None, ALU.mult),
                   reads=[srck], writes=[cck], sreads=["smal"], fd=TC)
            for k in range(1, KCONV):
                em.add("dve", lambda b, src=src, j=j, k=k, cc=cc,
                       col=wbase + (KCONV - 1 - k) * 16 + j:
                       b.scalar_tensor_tensor(
                           cc[:, :TC], src[:, j, 3 - k:3 - k + TC],
                           smal[:, col:col + 1], cc[:, :TC], ALU.mult, ALU.add),
                       reads=[srck, cck], writes=[cck], sreads=["smal"], fd=TC)
            em.add("dve", lambda b, src=src, j=j: b.tensor_copy(
                src[:, j, 0:3], src[:, j, TC:TC + 3]),
                reads=[srck], writes=[srck], fd=3)
            bcol = SM["conv_b"] + l * 32 + ci * 16 + j
            if ci == 0:
                em.add("act", lambda b, j=j, bcol=bcol, cc=cc: b.activation(
                    xi1[:, j, 3:3 + TC], cc[:, :TC], AF.Silu,
                    bias=smal[:, bcol:bcol + 1]),
                    reads=[cck], writes=["xi1"], sreads=["smal"], fd=TC)
            else:
                em.add("act", lambda b, j=j, bcol=bcol, cc=cc: b.activation(
                    xi2[:, j, :], cc[:, :TC], AF.Silu,
                    bias=smal[:, bcol:bcol + 1]),
                    reads=[cck], writes=["xi2"], sreads=["smal"], fd=TC)

    def dt_chain(l, j):
        p = j % 2
        zt, mt, t1t, dtt, dxt = zt2[p], mt2[p], t1t2[p], dtt2[p], dxt2[p]
        zk, mk, tk, dk, xk = f"zt{p}", f"mt{p}", f"t1t{p}", f"dtt{p}", f"dxt{p}"
        em.add("act", lambda b, j=j, zt=zt: b.activation(
            zt[:, :], PB(PB_DT)[:, :TC], AF.Identity,
            scale=smap("dt_w", l * 16 + j), bias=smap("dt_b", l * 16 + j)),
            reads=[PBK[PB_DT]], writes=[zk], sreads=["smal"], fd=TC)
        em.add("dve", lambda b, mt=mt, zt=zt: b.tensor_scalar_max(mt[:, :], zt[:, :], 0.0),
               reads=[zk], writes=[mk], fd=TC)
        em.add("dve", lambda b, t1t=t1t, mt=mt, zt=zt: b.scalar_tensor_tensor(
            t1t[:, :], mt[:, :], -2.0, zt[:, :], ALU.mult, ALU.add),
               reads=[mk, zk], writes=[tk], fd=TC)
        em.add("act", lambda b, zt=zt, t1t=t1t: b.activation(zt[:, :], t1t[:, :], AF.Exp),
               reads=[tk], writes=[zk], fd=TC)
        em.add("act", lambda b, t1t=t1t, zt=zt: b.activation(t1t[:, :], zt[:, :], AF.Ln, bias=1.0),
               reads=[zk], writes=[tk], fd=TC)
        em.add("dve", lambda b, dtt=dtt, mt=mt, t1t=t1t: b.tensor_add(dtt[:, :], mt[:, :], t1t[:, :]),
               reads=[mk, tk], writes=[dk], fd=TC)
        em.add("dve", lambda b, j=j, dxt=dxt, dtt=dtt: b.tensor_mul(dxt[:, :], dtt[:, :], xi2[:, j, :]),
               reads=[dk, "xi2"], writes=[xk], fd=TC)
        return zk, mk, tk, dk, xk, dtt, dxt

    def precast_weights(l):
        p = l % 2
        dma("dma_pool", winb[p][:, :, :, :], win[l], [], [f"dram_winb{p}"])
        dma("dma_pool", woutb[p][:, :, :, :], wout[l], [], [f"dram_woutb{p}"])

    precast_weights(0)
    for l in range(L):
        xa_in, xa_out = xaping[l % 2], xaping[(l + 1) % 2]
        ki, ko = f"dram_xa{l % 2}", f"dram_xa{(l + 1) % 2}"
        if l + 1 < L:
            precast_weights(l + 1)
        dma("dma_pool", xpT[:, :, :], wxp[l].rearrange("a p n -> p a n"), [], ["xpT"])
        for sq in range(BC):
            for c in range(NCH):
                t0 = sq * S + c * TC
                dma("dma_sp", xat[:, :, :], tok_slice(xa_in, t0, TC), [ki], ["xat"])
                for i in range(8):
                    ln_feed(xat[:, i, :], "xat", i, 8, TC)
                ln_rows(TC, CV_INVH, CV_EPSA)
                for i in range(8):
                    sl = i % 2
                    ln_center(xat[:, i, :], "xat", TC, sl)
                    em.add("act", lambda b, i=i, lt=ln_t1[sl]: b.activation(
                        xabt[:, i, :], lt[:, :TC], AF.Identity,
                        scale=smap("ln_g", l * 8 + i), bias=smap("ln_b", l * 8 + i)),
                        reads=[f"ln_t1_{sl}"], writes=["xabt"], sreads=["smal"], fd=TC)
                if debug and l == 0:
                    dma("dma_sp", dbg_xl[:, t0:t0 + TC].rearrange("(a p) t -> p a t", p=128),
                        xabt[:, :, :], ["xabt"], ["dbg_xl"])
                for j in range(32):
                    sl = j % 3
                    dma("dma_sp" if j % 2 == 0 else "dma_act", wst[:, sl, :, :],
                        winb[l % 2][j].rearrange("a p n -> p a n"),
                        [f"dram_winb{l % 2}"], [f"wst{sl}"])
                    bank = PB_MAIN[j % 2]
                    for k in range(8):
                        em.add("pe", lambda b, k=k, sl=sl, bank=bank, s=(k == 0),
                               e=(k == 7): b.matmul(
                                   PB(bank)[:, :TC], wst[:, sl, k, :],
                                   xabt[:, k, :], start=s, stop=e),
                               reads=[f"wst{sl}", "xabt"], writes=[PBK[bank]], fd=TC)
                    if j < 16:
                        em.add("act", lambda b, j=j, bank=bank: b.activation(
                            xi0[:, j, 3:3 + TC], PB(bank)[:, :TC], AF.Silu),
                            reads=[PBK[bank]], writes=["xi0"], fd=TC)
                        conv_block(l, c, j)
                    else:
                        em.add("act", lambda b, j=j, bank=bank: b.activation(
                            sig[:, j - 16, :], PB(bank)[:, :TC], AF.Sigmoid),
                            reads=[PBK[bank]], writes=["sig"], fd=TC)
                for k in range(16):
                    em.add("pe", lambda b, k=k, s=(k == 0), e=(k == 15): b.matmul(
                        PB(PB_MAIN[0])[0:33, :TC], xpT[:, k, :], xi2[:, k, :],
                        start=s, stop=e),
                        reads=["xpT", "xi2"], writes=[PBK[PB_MAIN[0]]], fd=TC)
                em.add("act", lambda b: b.copy(projf[:, :], PB(PB_MAIN[0])[0:33, :TC]),
                       reads=[PBK[PB_MAIN[0]]], writes=["projf"], fd=TC)
                em.add("act", lambda b: b.copy(projb[:, :], projf[:, :]),
                       reads=["projf"], writes=["projb"], fd=TC)
                if debug and l == 0:
                    dma("dma_sp", dbg_xi2[:, t0:t0 + TC].rearrange("(a p) t -> p a t", p=128),
                        xi2[:, :, :], ["xi2"], ["dbg_xi2"])
                    dma("dma_sp", dbg_sig[:, t0:t0 + TC].rearrange("(a p) t -> p a t", p=128),
                        sig[:, :, :], ["sig"], ["dbg_sig"])
                    dma("dma_sp", dbg_proj[:, t0:t0 + TC], projf[:, :], ["projf"], ["dbg_proj"])
                em.add("pe", lambda b: b.matmul(PB(PB_DT)[:, :TC], sel_f0[:, :],
                                                projf[:, :], start=True, stop=True),
                       reads=["sel_f0", "projf"], writes=[PBK[PB_DT]], fd=TC)
                for n in range(NST):
                    bank = PB_MAIN[n % 2]
                    em.add("pe", lambda b, n=n, bank=bank: b.matmul(
                        PB(bank)[:, :TC], sel_bf[:, (1 + n) * 128:(2 + n) * 128],
                        projb[:, :], start=True, stop=True),
                        reads=["sel_bf", "projb"], writes=[PBK[bank]], fd=TC)
                    em.add("act", lambda b, n=n, bank=bank: b.copy(
                        brepb[:, n, :], PB(bank)[:, :TC]),
                        reads=[PBK[bank]], writes=["brepb"], fd=TC)
                    em.add("pe", lambda b, n=n, bank=bank: b.matmul(
                        PB(bank)[:, :TC], sel_bf[:, (17 + n) * 128:(18 + n) * 128],
                        projb[:, :], start=True, stop=True),
                        reads=["sel_bf", "projb"], writes=[PBK[bank]], fd=TC)
                    em.add("act", lambda b, n=n, bank=bank: b.copy(
                        crepb[:, n, :], PB(bank)[:, :TC]),
                        reads=[PBK[bank]], writes=["crepb"], fd=TC)
                for j in range(16):
                    _, _, _, dk_, xk_, dtt, dxt = dt_chain(l, j)
                    for g in range(4):
                        hp = hg[g % 2]
                        for nn in range(4):
                            n = g * 4 + nn
                            hk = f"hg{g % 2}_{nn}"
                            av, avk = a_t[n % 2], f"a_t{n % 2}"
                            em.add("act", lambda b, n=n, av=av, dtt=dtt: b.activation(
                                av[:, :], dtt[:, :], AF.Identity,
                                scale=cvap(CV_NEG0 + n), bias=1.0),
                                reads=[dk_], writes=[avk], sreads=["cons"], fd=TC)
                            em.add("dve", lambda b, n=n: b.tensor_mul(
                                b_t[:, :], dxt[:, :], brepb[:, n, :]),
                                reads=["dxt", "brepb"], writes=["b_t"], fd=TC)
                            if c == 0:
                                init, sr = 0.0, []
                            else:
                                init = carry[:, j * 16 + n: j * 16 + n + 1]
                                sr = ["carry"]
                            em.add("dve", lambda b, nn=nn, hp=hp, init=init, av=av:
                                   b.tensor_tensor_scan(
                                       hp[:, nn, :], av[:, :], b_t[:, :], init,
                                       ALU.mult, ALU.add),
                                   reads=[avk, "b_t"], writes=[hk], sreads=sr, fd=TC)
                            yp, yk = yht[nn % 2], f"yht{nn % 2}"
                            em.add("pool", lambda b, nn=nn, hp=hp, yp=yp, n=n:
                                   b.tensor_mul(yp[:, :], hp[:, nn, :],
                                                crepb[:, n, :]),
                                   reads=[hk, "crepb"], writes=[yk], fd=TC)
                            em.add("pe", lambda b, yp=yp, s=(n == 0), e=(n == NST - 1):
                                   b.matmul(PB(PB_Y)[:, :TC], id_bf[:, :], yp[:, :],
                                            start=s, stop=e),
                                   reads=[yk, "id_bf"], writes=[PBK[PB_Y]], fd=TC)
                        em.add("dve", lambda b, j=j, g=g, hp=hp: b.tensor_copy(
                            carry[:, j * 16 + g * 4: j * 16 + g * 4 + 4],
                            hp[:, :, TC - 1]),
                            reads=[f"hg{g % 2}_{i}" for i in range(4)],
                            writes=["carry"], fd=4)
                    em.add("dve", lambda b, j=j: b.scalar_tensor_tensor(
                        y32[:, j, :], xi2[:, j, :], smap("Dp", l * 16 + j),
                        PB(PB_Y)[:, :TC], ALU.mult, ALU.add),
                        reads=["xi2", PBK[PB_Y]], writes=["y32"], sreads=["smal"], fd=TC)
                    ln_feed(y32[:, j, :], "y32", j, 16, TC)
                if debug and l == 0:
                    dma("dma_sp", dbg_y[:, t0:t0 + TC].rearrange("(a p) t -> p a t", p=128),
                        y32[:, :, :], ["y32"], ["dbg_y"])
                ln_rows(TC, CV_INVDI, CV_EPSA)
                for j in range(16):
                    sl = j % 2
                    ln_center(y32[:, j, :], "y32", TC, sl)
                    em.add("act", lambda b, j=j, lt=ln_t1[sl], cc=cvt[sl]: b.activation(
                        cc[:, :TC], lt[:, :TC], AF.Identity,
                        scale=smap("mln_g", l * 16 + j), bias=smap("mln_b", l * 16 + j)),
                        reads=[f"ln_t1_{sl}"], writes=[f"cvt{sl}"], sreads=["smal"], fd=TC)
                    em.add("dve", lambda b, j=j, cc=cvt[sl]: b.tensor_mul(
                        ygb[:, j, :], cc[:, :TC], sig[:, j, :]),
                        reads=[f"cvt{sl}", "sig"], writes=["ygb"], fd=TC)
                for j in range(8):
                    dma("dma_sp" if j % 2 == 0 else "dma_act", wost[:, j % 2, :, :],
                        woutb[l % 2][j].rearrange("a p n -> p a n"),
                        [f"dram_woutb{l % 2}"], [f"wost{j % 2}"])
                    bank = PB_MAIN[j % 2]
                    for k in range(16):
                        em.add("pe", lambda b, k=k, j=j, bank=bank, s=(k == 0),
                               e=(k == 15):
                               b.matmul(PB(bank)[:, :TC], wost[:, j % 2, k, :],
                                        ygb[:, k, :], start=s, stop=e),
                               reads=[f"wost{j % 2}", "ygb"], writes=[PBK[bank]], fd=TC)
                    em.add("dve", lambda b, j=j, bank=bank: b.tensor_add(
                        xat[:, j, :], PB(bank)[:, :TC], xat[:, j, :]),
                        reads=[PBK[bank], "xat"], writes=["xat"], fd=TC)
                dma("dma_sp", tok_slice(xa_out, t0, TC), xat[:, :, :], ["xat"], [ko])

    em.barrier()
    sta.close()

    stb = ExitStack()
    set_psum(stb, MC)
    op1T = sballoc(stb, "op1T", [128, 8, 1024], BF16)
    op2T = sballoc(stb, "op2T", [128, 8, 1024], BF16)
    mx32 = sballoc(stb, "mx32", [128, 8, MC])
    mxb = sballoc(stb, "mxb", [128, 8, MC], BF16)
    g1b = sballoc(stb, "g1b", [128, 8, MC], BF16)
    mst = sballoc(stb, "mst", [128, 8, MC])
    dma("dma_pool", op1T[:, :, :], wop1.rearrange("a p n -> p a n"), [], ["op1T"])
    dma("dma_pool", op2T[:, :, :], wop2.rearrange("a p n -> p a n"), [], ["op2T"])
    xa_fin = xaping[L % 2]
    kfin = f"dram_xa{L % 2}"
    for mc_i in range(NMC):
        t0 = mc_i * MC
        dma("dma_sp", mx32[:, :, :], tok_slice(xa_fin, t0, MC), [kfin], ["mx32"])
        for i in range(8):
            em.add("dve", lambda b, i=i: b.tensor_copy(mxb[:, i, :], mx32[:, i, :]),
                   reads=["mx32"], writes=["mxb"], fd=MC)
        for j in range(8):
            bank = PB_MAIN[j % 2]
            for k in range(8):
                em.add("pe", lambda b, j=j, k=k, bank=bank, s=(k == 0), e=(k == 7):
                       b.matmul(PB(bank)[:, :MC], op1T[:, k, j * 128:(j + 1) * 128],
                                mxb[:, k, :], start=s, stop=e),
                       reads=["op1T", "mxb"], writes=[PBK[bank]], fd=MC)
            em.add("act", lambda b, j=j, bank=bank: b.activation(
                g1b[:, j, :], PB(bank)[:, :MC], AF.Gelu, bias=smap("op1_b", j)),
                reads=[PBK[bank]], writes=["g1b"], sreads=["smal"], fd=MC)
        dma("dma_sp", mx32[:, :, :], tok_slice(xn32, t0, MC), ["dram_xn32"], ["mx32"])
        for j in range(8):
            bank = PB_MAIN[j % 2]
            for k in range(8):
                em.add("pe", lambda b, j=j, k=k, bank=bank, s=(k == 0), e=(k == 7):
                       b.matmul(PB(bank)[:, :MC], op2T[:, k, j * 128:(j + 1) * 128],
                                g1b[:, k, :], start=s, stop=e),
                       reads=["op2T", "g1b"], writes=[PBK[bank]], fd=MC)
            em.add("dve", lambda b, j=j, bank=bank: b.scalar_tensor_tensor(
                mst[:, j, :], PB(bank)[:, :MC], smap("op2_b", j), mx32[:, j, :],
                ALU.add, ALU.add),
                reads=[PBK[bank], "mx32"], writes=["mst"], sreads=["smal"], fd=MC)
        dma("dma_sp", tok_slice(s32, t0, MC), mst[:, :, :], ["mst"], ["dram_s32"])

    em.barrier()
    stb.close()

    stc = ExitStack()
    set_psum(stc, MC)
    gfm = sballoc(stc, "gfm", [8, TOK], BF16)
    sel8 = sballoc(stc, "sel8", [8, 8 * 128], BF16)
    dma("dma_pool", sel8[:, :], selp[0:8, 0:8 * 128], [], ["sel8"])
    std = ExitStack()
    rtb = sballoc(std, "rtb", [128, 8, 8])
    xng = sballoc(std, "xng", [128, 8, 128])
    ltok = sballoc(std, "ltok", [128, 8])
    gv1 = sballoc(std, "gv1", [128, 1])
    gm1 = sballoc(std, "gm1", [128, 8])
    gl2 = sballoc(std, "gl2", [128, 8])
    gv2 = sballoc(std, "gv2", [128, 1])
    gm2 = sballoc(std, "gm2", [128, 8])
    gdd = sballoc(std, "gdd", [128, 1])
    gee = sballoc(std, "gee", [128, 1])
    gsv = sballoc(std, "gsv", [128, 1])
    gw2 = sballoc(std, "gw2", [128, 1])
    ggt = sballoc(std, "ggt", [128, 8])
    dma("dma_sp", rtb[:, :, :], wrt.rearrange("a p n -> p a n"), [], ["rtb"])
    for tg in range(TOK // 128):
        t0 = tg * 128
        dma("dma_sp", xng[:, :, :], tok_slice(xn32, t0, 128), ["dram_xn32"], ["xng"])
        for k in range(8):
            em.add("pe", lambda b, k=k, s=(k == 0), e=(k == 7): b.matmul(
                PB(PB_MAIN[0])[:, 0:8], xng[:, k, :], rtb[:, k, :], start=s, stop=e),
                reads=["xng", "rtb"], writes=[PBK[PB_MAIN[0]]], fd=8)
        em.add("act", lambda b: b.copy(ltok[:, :], PB(PB_MAIN[0])[:, 0:8]),
               reads=[PBK[PB_MAIN[0]]], writes=["ltok"], fd=8)
        em.add("dve", lambda b: b.reduce_max(gv1[:, :], ltok[:, :], axis=AX.X),
               reads=["ltok"], writes=["gv1"], fd=8)
        em.add("dve", lambda b: b.tensor_scalar(gm1[:, :], ltok[:, :], gv1[:, :],
                                                None, ALU.is_ge),
               reads=["ltok"], writes=["gm1"], sreads=["gv1"], fd=8)
        em.add("dve", lambda b: b.scalar_tensor_tensor(gl2[:, :], gm1[:, :], -1e30,
                                                       ltok[:, :], ALU.mult, ALU.add),
               reads=["gm1", "ltok"], writes=["gl2"], fd=8)
        em.add("dve", lambda b: b.reduce_max(gv2[:, :], gl2[:, :], axis=AX.X),
               reads=["gl2"], writes=["gv2"], fd=8)
        em.add("dve", lambda b: b.tensor_scalar(gm2[:, :], gl2[:, :], gv2[:, :],
                                                None, ALU.is_ge),
               reads=["gl2"], writes=["gm2"], sreads=["gv2"], fd=8)
        em.add("dve", lambda b: b.tensor_sub(gdd[:, :], gv2[:, :], gv1[:, :]),
               reads=["gv2", "gv1"], writes=["gdd"], fd=1)
        em.add("act", lambda b: b.activation(gee[:, :], gdd[:, :], AF.Exp),
               reads=["gdd"], writes=["gee"], fd=1)
        em.add("dve", lambda b: b.tensor_scalar_add(gsv[:, :], gee[:, :], 1.0),
               reads=["gee"], writes=["gsv"], fd=1)
        em.add("dve", lambda b: b.reciprocal(gsv[:, :], gsv[:, :]),
               reads=["gsv"], writes=["gsv"], fd=1)
        em.add("dve", lambda b: b.tensor_mul(gw2[:, :], gee[:, :], gsv[:, :]),
               reads=["gee", "gsv"], writes=["gw2"], fd=1)
        em.add("dve", lambda b: b.tensor_scalar(ggt[:, :], gm1[:, :], gsv[:, :],
                                                None, ALU.mult),
               reads=["gm1"], writes=["ggt"], sreads=["gsv"], fd=8)
        em.add("dve", lambda b: b.scalar_tensor_tensor(ggt[:, :], gm2[:, :], gw2[:, :],
                                                       ggt[:, :], ALU.mult, ALU.add),
               reads=["gm2", "ggt"], writes=["ggt"], sreads=["gw2"], fd=8)
        em.add("pe", lambda b: b.transpose(PB(PB_MAIN[1])[0:8, 0:128], ggt[:, :],
                                           ident[:, :]),
               reads=["ggt", "ident"], writes=[PBK[PB_MAIN[1]]], fd=128)
        em.add("act", lambda b, t0=t0: b.copy(gfm[:, t0:t0 + 128],
                                              PB(PB_MAIN[1])[0:8, 0:128]),
               reads=[PBK[PB_MAIN[1]]], writes=["gfm"], fd=128)

    em.barrier()
    std.close()

    ste = ExitStack()
    w1h = sballoc(ste, "w1h", [128, 8, DFF // 2], BF16)
    w2h = sballoc(ste, "w2h", [128, 16, H], BF16)
    exb = sballoc(ste, "exb", [128, 8, MC], BF16)
    ex32 = sballoc(ste, "ex32", [128, 8, MC])
    h1h = sballoc(ste, "h1h", [128, 16, MC], BF16)
    rest = sballoc(ste, "rest", [128, 8, MC])
    ptile = sballoc(ste, "ptile", [128, 8, MC])
    for e in range(NEXP):
        for hf in range(2):
            f0 = hf * 16
            for a in range(8):
                dma("dma_pool", w1h[:, a, :],
                    w1e[e, a, :, hf * (DFF // 2):(hf + 1) * (DFF // 2)], [], ["w1h"])
            for a in range(16):
                dma("dma_pool", w2h[:, a, :], w2e[e, f0 + a], [], ["w2h"])
            for mc_i in range(NMC):
                t0 = mc_i * MC
                dma("dma_sp", exb[:, :, :], tok_slice(xnbf, t0, MC),
                    ["dram_xnbf"], ["exb"])
                for f in range(16):
                    bank = PB_MAIN[f % 2]
                    for k in range(8):
                        em.add("pe", lambda b, f=f, k=k, bank=bank, s=(k == 0),
                               e_=(k == 7): b.matmul(
                                   PB(bank)[:, :MC], w1h[:, k, f * 128:(f + 1) * 128],
                                   exb[:, k, :], start=s, stop=e_),
                               reads=["w1h", "exb"], writes=[PBK[bank]], fd=MC)
                    em.add("act", lambda b, f=f, e=e, f0=f0, bank=bank: b.activation(
                        h1h[:, f, :], PB(bank)[:, :MC], AF.Gelu,
                        bias=smap("b1", e * 32 + f0 + f)),
                        reads=[PBK[bank]], writes=["h1h"], sreads=["smal"], fd=MC)
                if hf == 1:
                    dma("dma_act", ptile[:, :, :], tok_slice(pdram, t0, MC),
                        ["dram_pdram"], ["ptile"])
                    dma("dma_act", ex32[:, :, :], tok_slice(xn32, t0, MC),
                        ["dram_xn32"], ["ex32"])
                    em.add("pe", lambda b, e=e, t0=t0: b.matmul(
                        PB(PB_DT)[:, :MC], sel8[:, e * 128:(e + 1) * 128],
                        gfm[:, t0:t0 + MC], start=True, stop=True),
                        reads=["gfm", "sel8"], writes=[PBK[PB_DT]], fd=MC)
                for dj in range(8):
                    bank = PB_MAIN[dj % 2]
                    for k in range(16):
                        em.add("pe", lambda b, dj=dj, k=k, bank=bank, s=(k == 0),
                               e_=(k == 15): b.matmul(
                                   PB(bank)[:, :MC], w2h[:, k, dj * 128:(dj + 1) * 128],
                                   h1h[:, k, :], start=s, stop=e_),
                               reads=["w2h", "h1h"], writes=[PBK[bank]], fd=MC)
                    if hf == 0:
                        em.add("act", lambda b, dj=dj, bank=bank: b.copy(
                            ptile[:, dj, :], PB(bank)[:, :MC]),
                            reads=[PBK[bank]], writes=["ptile"], fd=MC)
                    else:
                        em.add("dve", lambda b, dj=dj, bank=bank: b.tensor_add(
                            ptile[:, dj, :], PB(bank)[:, :MC], ptile[:, dj, :]),
                            reads=[PBK[bank], "ptile"], writes=["ptile"], fd=MC)
                        em.add("dve", lambda b, dj=dj, e=e: b.scalar_tensor_tensor(
                            rest[:, dj, :], ptile[:, dj, :], smap("b2", e * 8 + dj),
                            ex32[:, dj, :], ALU.add, ALU.add),
                            reads=["ptile", "ex32"], writes=["rest"],
                            sreads=["smal"], fd=MC)
                if hf == 0:
                    dma("dma_act", tok_slice(pdram, t0, MC), ptile[:, :, :],
                        ["ptile"], ["dram_pdram"])
                    continue
                for i in range(8):
                    ln_feed(rest[:, i, :], "rest", i, 8, MC)
                ln_rows(MC, CV_INVH, CV_EPSM)
                for dj in range(8):
                    sl = dj % 2
                    ln_center(rest[:, dj, :], "rest", MC, sl)
                    em.add("act", lambda b, dj=dj, e=e, lt=ln_t1[sl], cc=cvt[sl]:
                           b.activation(
                               cc[:, :MC], lt[:, :MC], AF.Identity,
                               scale=smap("gamma", e * 8 + dj),
                               bias=smap("beta", e * 8 + dj)),
                           reads=[f"ln_t1_{sl}"], writes=[f"cvt{sl}"],
                           sreads=["smal"], fd=MC)
                    em.add("dve", lambda b, dj=dj, cc=cvt[sl]: b.tensor_mul(
                        rest[:, dj, :], cc[:, :MC], PB(PB_DT)[:, :MC]),
                        reads=[f"cvt{sl}", PBK[PB_DT]], writes=["rest"], fd=MC)
                if e == 0:
                    dma("dma_pool", tok_slice(ymoe, t0, MC), rest[:, :, :],
                        ["rest"], ["dram_ymoe"])
                else:
                    dma_acc(tok_slice(ymoe, t0, MC), rest[:, :, :],
                            ["rest"], ["dram_ymoe"])

    em.barrier()
    ste.close()
    stc.close()

    stf = ExitStack()
    cs = sballoc(stf, "cs", [128, 8, MC])
    cy = sballoc(stf, "cy", [128, 8, MC])
    cx = sballoc(stf, "cx", [128, 8, MC])
    co = sballoc(stf, "co", [128, 8, MC])
    for mc_i in range(NMC):
        t0 = mc_i * MC
        dma("dma_sp", cs[:, :, :], tok_slice(s32, t0, MC), ["dram_s32"], ["cs"])
        dma("dma_act", cy[:, :, :], tok_slice(ymoe, t0, MC), ["dram_ymoe"], ["cy"])
        dma("dma_sp", cx[:, :, :], tok_slice(x0, t0, MC), ["dram_x0"], ["cx"])
        for i in range(8):
            em.add("act", lambda b, i=i: b.activation(cy[:, i, :], cy[:, i, :], AF.Silu),
                   reads=["cy"], writes=["cy"], fd=MC)
            em.add("dve", lambda b, i=i: b.tensor_mul(co[:, i, :], cs[:, i, :],
                                                      cy[:, i, :]),
                   reads=["cs", "cy"], writes=["co"], fd=MC)
            em.add("dve", lambda b, i=i: b.tensor_add(co[:, i, :], co[:, i, :],
                                                      cx[:, i, :]),
                   reads=["co", "cx"], writes=["co"], fd=MC)
        dma("dma_sp", tok_slice(out_fm, t0, MC), co[:, :, :], ["co"], ["dram_out"])

    em.barrier()
    em.emit(stx)
    stf.close()
    stx.close()
    return nc



def _pack_smalls(ip):
    sm = np.zeros((128, SM_COLS), np.float32)

    def put(name, vec, nblk, idx0=0):
        v = np.asarray(vec, np.float32).reshape(nblk, 128).T
        sm[:, SM[name] + idx0:SM[name] + idx0 + nblk] = v
    put("in_norm_g", ip["L0_attn_in_norm_g"], 8)
    put("in_norm_b", ip["L0_attn_in_norm_b"], 8)
    for l in range(L):
        put("ln_g", ip["L0_attn_ln_g"][l], 8, l * 8)
        put("ln_b", ip["L0_attn_ln_b"][l], 8, l * 8)
        for ci in range(2):
            for k in range(KCONV):
                put("conv_w", ip["L0_attn_conv_w"][l, ci, :, k], 16,
                    l * (2 * KCONV * 16) + ci * (KCONV * 16) + k * 16)
            put("conv_b", ip["L0_attn_conv_b"][l, ci], 16, l * 32 + ci * 16)
        put("dt_w", ip["L0_attn_dt_proj_w"][l, :, 0], 16, l * 16)
        put("dt_b", ip["L0_attn_dt_proj_b"][l], 16, l * 16)
        put("Dp", ip["L0_attn_D"][l], 16, l * 16)
        put("mln_g", ip["L0_attn_mamba_ln_g"][l], 16, l * 16)
        put("mln_b", ip["L0_attn_mamba_ln_b"][l], 16, l * 16)
    put("op1_b", ip["L0_attn_op1_b"], 8)
    put("op2_b", ip["L0_attn_op2_b"], 8)
    for e in range(NEXP):
        put("b1", ip["L0_ffn_b1"][e], 32, e * 32)
        put("b2", ip["L0_ffn_b2"][e], 8, e * 8)
        put("gamma", ip["L0_ffn_gamma"][e], 8, e * 8)
        put("beta", ip["L0_ffn_beta"][e], 8, e * 8)
    return sm


def _host_shared(ip):
    f = np.float32
    arrs = {}
    w = np.asarray(ip["L0_attn_in_proj_w"], f)
    arrs["win"] = np.ascontiguousarray(
        w.reshape(L, 32, 128, 8, 128).transpose(0, 1, 3, 4, 2))
    w = np.asarray(ip["L0_attn_out_proj_w"], f)
    arrs["wout"] = np.ascontiguousarray(
        w.reshape(L, 8, 128, 16, 128).transpose(0, 1, 3, 4, 2))
    w = np.asarray(ip["L0_attn_x_proj_w"], f)
    arrs["wxp"] = np.ascontiguousarray(w.transpose(0, 2, 1).reshape(L, 16, 128, 33))
    arrs["wop1"] = np.ascontiguousarray(
        np.asarray(ip["L0_attn_op1_w"], f).T.reshape(8, 128, 1024))
    arrs["wop2"] = np.ascontiguousarray(
        np.asarray(ip["L0_attn_op2_w"], f).T.reshape(8, 128, 1024))
    arrs["wrt"] = np.ascontiguousarray(
        np.asarray(ip["L0_ffn_router_w"], f).reshape(8, 128, 8))
    arrs["w1e"] = np.ascontiguousarray(
        np.asarray(ip["L0_ffn_w1"], f).reshape(NEXP, 8, 128, DFF))
    arrs["w2e"] = np.ascontiguousarray(
        np.asarray(ip["L0_ffn_w2"], f).reshape(NEXP, 32, 128, H))
    arrs["smalls"] = _pack_smalls(ip)
    cvv = np.zeros((128, NCONST), f)
    cvv[:, CV_ONE] = 1.0
    cvv[:, CV_EPSA] = EPS_A
    cvv[:, CV_EPSM] = EPS_M
    cvv[:, CV_NH] = -0.5
    cvv[:, CV_INVH] = 1.0 / H
    cvv[:, CV_INVDI] = 1.0 / DI
    for n in range(NST):
        cvv[:, CV_NEG0 + n] = -(float(n + 1))
    arrs["consts"] = cvv
    sel = np.zeros((33, 34 * 128), f)
    for i in range(33):
        sel[i, i * 128:(i + 1) * 128] = 1.0
    arrs["selp"] = sel
    arrs["idmat"] = np.eye(128, dtype=f)
    arrs["onescol"] = np.ones((128, 1), f)
    arrs["onesrow"] = np.ones((1, 128), f)
    return arrs


def kernel(**inputs):
    hs = np.asarray(inputs["L0_attn_hidden_states"], np.float32)
    shared = _host_shared(inputs)
    nc = build_program(debug=False)
    in_maps = []
    for c in range(NCORE):
        m = dict(shared)
        xc = hs[c * BC:(c + 1) * BC]
        m["x0"] = np.ascontiguousarray(xc.transpose(2, 0, 1).reshape(H, TOK))
        in_maps.append(m)
    res = run_bass_kernel_spmd(nc, in_maps, core_ids=list(range(NCORE)))
    outs = []
    for c in range(NCORE):
        o = np.asarray(res.results[c]["out_fm"])
        outs.append(o.reshape(H, BC, S).transpose(1, 2, 0))
    return np.concatenate(outs, axis=0).astype(np.float32)


if __name__ == "__main__":
    import time
    t = time.time()
    nc = build_program()
    print(f"built in {time.time() - t:.1f}s")


# version 06
# speedup vs baseline: 1.1233x; 1.0110x over previous
import numpy as np
from contextlib import ExitStack

import concourse.bass as bass
import concourse.mybir as mybir
from concourse.bass_utils import run_bass_kernel_spmd

F32 = mybir.dt.float32
BF16 = mybir.dt.bfloat16
AF = mybir.ActivationFunctionType
ALU = mybir.AluOpType
AX = mybir.AxisListType

B_TOT, S, H, L, DI, NST, KCONV = 16, 2048, 1024, 4, 2048, 16, 4
DFF, NEXP, NCORE = 4096, 8, 8
BC = B_TOT // NCORE
TOK = BC * S
TC = 256
NCH = S // TC
MC = 512
NMC = TOK // MC
EPS_A, EPS_M = 1e-5, 1e-6
FIFO_SAFE_FD = 128

ENGINES = ("pe", "dve", "act", "pool", "sp")
DMAQ = ("dma_sp", "dma_act", "dma_pool")


def _sm_layout():
    cols, c = {}, 0

    def take(name, n):
        nonlocal c
        cols[name] = c
        c += n
    take("in_norm_g", 8); take("in_norm_b", 8)
    take("ln_g", L * 8); take("ln_b", L * 8)
    take("conv_w", L * 2 * KCONV * 16); take("conv_b", L * 2 * 16)
    take("dt_w", L * 16); take("dt_b", L * 16)
    take("Dp", L * 16)
    take("mln_g", L * 16); take("mln_b", L * 16)
    take("op1_b", 8); take("op2_b", 8)
    take("b1", NEXP * 32); take("b2", NEXP * 8)
    take("gamma", NEXP * 8); take("beta", NEXP * 8)
    return cols, c


SM, SM_COLS = _sm_layout()
CV_ONE, CV_EPSA, CV_EPSM, CV_NH, CV_INVH, CV_INVDI = 0, 1, 2, 3, 4, 5
CV_NEG0 = 6
NCONST = 22


class Op:
    __slots__ = ("eng", "fn", "reads", "writes", "sreads", "fd", "idx",
                 "inc_count", "waits", "dma", "meta")

    def __init__(self, eng, fn, reads, writes, sreads, fd, idx, dma):
        self.eng, self.fn = eng, fn
        self.reads, self.writes, self.sreads = list(reads), list(writes), list(sreads)
        self.fd, self.idx, self.dma = fd, idx, dma
        self.inc_count, self.waits = None, []
        self.meta = None


DMA_SLOTS = {"dma_sp": 1, "dma_act": 1, "dma_pool": 8}


class Emitter:

    def __init__(self, nc):
        self.nc = nc
        self.ops = []
        self.last_writer = {}
        self.readers = {}
        self.counts = {}
        for e in ENGINES:
            self.counts[e] = 0
        self.dma_qkeys = []
        for q, ns in DMA_SLOTS.items():
            for s in range(ns):
                k = f"{q}#{s}"
                self.counts[k] = 0
                self.dma_qkeys.append(k)
        self.dma_rr = {q: 0 for q in DMA_SLOTS}
        self.last_dma_on_slot = {}
        self.meta_provider = None
        self.meta_apply = None

    @staticmethod
    def _same_stream(a, b):
        if a.dma is not None or b.dma is not None:
            return False
        return a.eng == b.eng

    def _need_edge(self, prod, cons, scalar=False):
        if prod is None:
            return False
        if scalar:
            return True
        if self._same_stream(prod, cons):
            if prod.eng == "pe":
                return False
            return prod.fd < FIFO_SAFE_FD or cons.fd < FIFO_SAFE_FD
        return True

    def _assign_inc(self, d):
        if d.inc_count is None:
            d.inc_count = -1
            q = d.dma if d.dma is not None else d.eng
            self.counts[q] += 16 if d.dma is not None else 1

    def add(self, eng, fn, reads=(), writes=(), sreads=(), fd=512, dma=None):
        if dma is not None:
            slot = self.dma_rr[dma]
            self.dma_rr[dma] = (slot + 1) % DMA_SLOTS[dma]
            qkey = f"{dma}#{slot}"
        else:
            qkey = None
        op = Op(eng, fn, reads, writes, sreads, fd, len(self.ops), qkey)
        if self.meta_provider is not None:
            op.meta = self.meta_provider()
        deps = set()
        for b in op.reads:
            w = self.last_writer.get(b)
            if self._need_edge(w, op):
                deps.add(w)
        for b in op.sreads:
            w = self.last_writer.get(b)
            if self._need_edge(w, op, scalar=True):
                deps.add(w)
        for b in op.writes:
            w = self.last_writer.get(b)
            if self._need_edge(w, op):
                deps.add(w)
            for r in self.readers.get(b, ()):
                if r is not op and self._need_edge(r, op):
                    deps.add(r)
        by_q = {}
        for d in deps:
            q = d.dma if d.dma is not None else d.eng
            if d.dma is not None:
                by_q[(q, d.idx)] = d
            else:
                cur = by_q.get(q)
                if cur is None or cur.idx < d.idx:
                    by_q[q] = d
        if qkey is not None:
            prev = self.last_dma_on_slot.get(qkey)
            if prev is not None:
                op.waits.append(prev)
            self.last_dma_on_slot[qkey] = op
        for d in sorted(by_q.values(), key=lambda o: o.idx):
            self._assign_inc(d)
            op.waits.append(d)
        self.ops.append(op)
        if qkey is not None:
            self._assign_inc(op)
        for b in op.reads + op.sreads:
            self.readers.setdefault(b, []).append(op)
        for b in op.writes:
            self.last_writer[b] = op
            self.readers[b] = []
        return op

    def barrier(self):
        last_by_q = {}
        for o in self.ops:
            q = o.dma if o.dma is not None else o.eng
            last_by_q[q] = o
        for q, o in last_by_q.items():
            self._assign_inc(o)
        for eng in ENGINES:
            op = Op(eng, lambda b: b.nop(), [], [], [], 1, len(self.ops), None)
            for q, o in last_by_q.items():
                if q != eng:
                    op.waits.append(o)
            self.ops.append(op)

    def emit(self, st):
        nc = self.nc
        allq = list(ENGINES) + self.dma_qkeys
        ordinals = {q: 0 for q in allq}
        for op in self.ops:
            if op.inc_count is not None:
                q = op.dma if op.dma is not None else op.eng
                ordinals[q] += 16 if op.dma is not None else 1
                op.inc_count = ordinals[q]
        for q in allq:
            assert ordinals[q] < 60000, f"sem overflow {q}: {ordinals[q]}"
        sems = {q: st.enter_context(nc.semaphore(f"sem_{q.replace('#','_')}"))
                for q in allq}
        block = st.enter_context(nc.Block())
        per_eng = {e: [] for e in ENGINES}
        for op in self.ops:
            per_eng[op.eng].append(op)
        streams = {"pe": block.tensor, "dve": block.vector, "act": block.scalar,
                   "pool": block.gpsimd, "sp": block.sync}

        for eng in ENGINES:
            ops = per_eng[eng]

            def make(ops=ops):
                def body(b):
                    clock = {q: 0 for q in allq}
                    for op in ops:
                        for d in op.waits:
                            q = d.dma if d.dma is not None else d.eng
                            v = d.inc_count
                            if clock[q] < v:
                                b.wait_ge(sems[q], v)
                                clock[q] = v
                        if op.meta is not None and self.meta_apply is not None:
                            self.meta_apply(op.meta)
                        inst = op.fn(b)
                        if op.inc_count is not None:
                            q = op.dma if op.dma is not None else op.eng
                            step = 16 if op.dma is not None else 1
                            inst.then_inc(sems[q], step)
                return body
            streams[eng](make())


def build_program(debug=False):
    nc = bass.Bass("TRN2", target_bir_lowering=False, detect_race_conditions=False)
    em = Emitter(nc)
    stx = ExitStack()

    def din(name, shape, dt=F32):
        return nc.dram_tensor(name, list(shape), dt, kind="ExternalInput").ap()

    def dout(name, shape, dt=F32):
        return nc.dram_tensor(name, list(shape), dt, kind="ExternalOutput").ap()

    def dint(name, shape, dt=F32):
        kind = "ExternalOutput" if debug else "Internal"
        return nc.dram_tensor(name, list(shape), dt, kind=kind).ap()

    x0 = din("x0", [H, TOK])
    win = din("win", [L, 32, 8, 128, 128])
    wout = din("wout", [L, 8, 16, 128, 128])
    wxp = din("wxp", [L, 16, 128, 33])
    wop1 = din("wop1", [8, 128, 1024])
    wop2 = din("wop2", [8, 128, 1024])
    wrt = din("wrt", [8, 128, 8])
    w1e = din("w1e", [NEXP, 8, 128, DFF])
    w2e = din("w2e", [NEXP, 32, 128, H])
    smalls = din("smalls", [128, SM_COLS])
    consts = din("consts", [128, NCONST])
    selp = din("selp", [33, 34 * 128])
    idmat = din("idmat", [128, 128])
    onescol = din("onescol", [128, 1])
    onesrow = din("onesrow", [1, 128])

    out_fm = dout("out_fm", [H, TOK])
    xn32 = dint("xn32", [H, TOK])
    xnbf = dint("xnbf", [H, TOK], BF16)
    xaping = [dint("xa0", [H, TOK]), dint("xa1", [H, TOK])]
    s32 = dint("s32", [H, TOK])
    ymoe = dint("ymoe", [H, TOK])
    pdram = dint("pdram", [H, TOK])
    winb = [dint(f"winb{i}", [32, 8, 128, 128], BF16) for i in range(2)]
    woutb = [dint(f"woutb{i}", [8, 16, 128, 128], BF16) for i in range(2)]
    if debug:
        dbg_xl = dout("dbg_xl", [H, TOK], BF16)
        dbg_xi2 = dout("dbg_xi2", [DI, TOK])
        dbg_proj = dout("dbg_proj", [33, TOK])
        dbg_y = dout("dbg_y", [DI, TOK])
        dbg_sig = dout("dbg_sig", [DI, TOK])

    def sballoc(st, name, shape, dt=F32):
        return st.enter_context(nc.sbuf_tensor(name, shape, dt))

    cons = sballoc(stx, "cons", [128, NCONST])
    smal = sballoc(stx, "smal", [128, SM_COLS])
    ident = sballoc(stx, "ident", [128, 128])
    id_bf = sballoc(stx, "id_bf", [128, 128], BF16)
    ones_bf = sballoc(stx, "ones_bf", [128, 1], BF16)
    onesrow_f = sballoc(stx, "onesrow_f", [1, 128])
    onesrow_b = sballoc(stx, "onesrow_b", [1, 128], BF16)
    ln_bf = [sballoc(stx, f"ln_bf{i}", [128, MC], BF16) for i in range(2)]
    ln_sq = [sballoc(stx, f"ln_sq{i}", [128, MC], BF16) for i in range(2)]
    ln_t1 = [sballoc(stx, f"ln_t1_{i}", [128, MC]) for i in range(2)]
    cvt = [sballoc(stx, f"cvt{i}", [128, MC]) for i in range(2)]
    rows = sballoc(stx, "rows", [1, 5 * MC])

    def row(i, width):
        return rows[:, i * MC:i * MC + width]
    R_MU, R_M2, R_VAR, R_R, R_MR = range(5)

    _pb_gens = {}
    _pbcur = {"t": None, "gen": 0}

    def set_psum(st_, width):
        _pbcur["gen"] += 1
        g = _pbcur["gen"]
        _pbcur["t"] = [st_.enter_context(
            nc.psum_tensor(f"pb{g}_{i}", [128, width], F32)) for i in range(8)]
        _pb_gens[g] = _pbcur["t"]

    def PB(i):
        return _pbcur["t"][i]

    em.meta_provider = lambda: _pbcur["gen"]
    em.meta_apply = lambda g: (_pbcur.update(t=_pb_gens[g]) if g in _pb_gens else None)

    PBK = [f"pb{i}" for i in range(8)]
    PB_MAIN = (0, 1)
    PB_R, PB_M, PB_Y, PB_DT, PB_S1, PB_S2 = 2, 3, 4, 5, 6, 7

    def dma(queue, out_ap, in_ap, reads, writes):
        eng = {"dma_sp": "sp", "dma_act": "act", "dma_pool": "pool"}[queue]
        em.add(eng, lambda b, o=out_ap, i=in_ap: b.dma_start(o, i),
               reads=reads, writes=writes, fd=512, dma=queue)

    def dma_acc(out_ap, in_ap, reads, writes):
        em.add("pool", lambda b, o=out_ap, i=in_ap: b.dma_start(o, i, accum_op=ALU.add),
               reads=reads, writes=writes, fd=512, dma="dma_pool")

    def cvap(col, p1=False):
        a = cons[:, col:col + 1]
        return a[0:1, :] if p1 else a

    def smap(name, idx):
        c = SM[name] + idx
        return smal[:, c:c + 1]

    dma("dma_sp", cons[:, :], consts, [], ["cons"])
    dma("dma_sp", smal[:, :], smalls, [], ["smal"])
    dma("dma_sp", ident[:, :], idmat, [], ["ident"])
    dma("dma_pool", id_bf[:, :], idmat, [], ["id_bf"])
    dma("dma_pool", ones_bf[:, :], onescol, [], ["ones_bf"])
    dma("dma_sp", onesrow_f[:, :], onesrow, [], ["onesrow_f"])
    dma("dma_pool", onesrow_b[:, :], onesrow, [], ["onesrow_b"])

    def tok_slice(d_ap, t0, w):
        return d_ap[:, t0:t0 + w].rearrange("(a p) t -> p a t", p=128)

    def ln_feed(ap, key, i, nblk, width, s1=PB_S1, s2=PB_S2):
        sl = i % 2
        lb, ls = ln_bf[sl], ln_sq[sl]
        em.add("act", lambda b, a=ap, lb=lb: b.copy(lb[:, :width], a),
               reads=[key], writes=[f"ln_bf{sl}"], fd=width)
        em.add("pe", lambda b, s=(i == 0), e=(i == nblk - 1), lb=lb: b.matmul(
            PB(s1)[0:1, :width], ones_bf[:, :], lb[:, :width], start=s, stop=e),
            reads=[f"ln_bf{sl}", "ones_bf"], writes=[PBK[s1]], fd=width)
        em.add("act", lambda b, a=ap, ls=ls: b.activation(ls[:, :width], a, AF.Square),
               reads=[key], writes=[f"ln_sq{sl}"], fd=width)
        em.add("pe", lambda b, s=(i == 0), e=(i == nblk - 1), ls=ls: b.matmul(
            PB(s2)[0:1, :width], ones_bf[:, :], ls[:, :width], start=s, stop=e),
            reads=[f"ln_sq{sl}", "ones_bf"], writes=[PBK[s2]], fd=width)

    def ln_rows(width, inv_col, eps_col):
        em.add("act", lambda b: b.activation(row(R_MU, width), PB(PB_S1)[0:1, :width],
                                             AF.Identity, scale=cvap(inv_col, True)),
               reads=[PBK[PB_S1]], writes=["rows"], sreads=["cons"], fd=width)
        em.add("act", lambda b: b.activation(row(R_M2, width), PB(PB_S2)[0:1, :width],
                                             AF.Identity, scale=cvap(inv_col, True)),
               reads=[PBK[PB_S2]], writes=["rows"], sreads=["cons"], fd=width)
        em.add("dve", lambda b: b.tensor_mul(row(R_VAR, width), row(R_MU, width),
                                             row(R_MU, width)),
               reads=["rows"], writes=["rows"], fd=width)
        em.add("dve", lambda b: b.tensor_sub(row(R_VAR, width), row(R_M2, width),
                                             row(R_VAR, width)),
               reads=["rows"], writes=["rows"], fd=width)
        em.add("act", lambda b: b.activation(row(R_R, width), row(R_VAR, width), AF.Ln,
                                             bias=cvap(eps_col, True)),
               reads=["rows"], writes=["rows"], sreads=["cons"], fd=width)
        em.add("act", lambda b: b.activation(row(R_R, width), row(R_R, width), AF.Exp,
                                             scale=cvap(CV_NH, True)),
               reads=["rows"], writes=["rows"], sreads=["cons"], fd=width)
        em.add("dve", lambda b: b.tensor_mul(row(R_MR, width), row(R_MU, width),
                                             row(R_R, width)),
               reads=["rows"], writes=["rows"], fd=width)
        em.add("pe", lambda b: b.matmul(PB(PB_R)[:, :width], onesrow_f[:, :],
                                        row(R_R, width), start=True, stop=True),
               reads=["rows", "onesrow_f"], writes=[PBK[PB_R]], fd=width)
        em.add("pe", lambda b: b.matmul(PB(PB_M)[:, :width], onesrow_f[:, :],
                                        row(R_MR, width), start=True, stop=True),
               reads=["rows", "onesrow_f"], writes=[PBK[PB_M]], fd=width)

    def ln_center(ap, key, width, sl=0):
        lt = ln_t1[sl]
        em.add("dve", lambda b, a=ap, lt=lt: b.tensor_mul(lt[:, :width], a,
                                                          PB(PB_R)[:, :width]),
               reads=[key, PBK[PB_R]], writes=[f"ln_t1_{sl}"], fd=width)
        em.add("dve", lambda b, lt=lt: b.tensor_sub(lt[:, :width], lt[:, :width],
                                                    PB(PB_M)[:, :width]),
               reads=[f"ln_t1_{sl}", PBK[PB_M]], writes=[f"ln_t1_{sl}"], fd=width)

    sta = ExitStack()
    set_psum(sta, TC)
    xat = sballoc(sta, "xat", [128, 8, TC])
    xnt = sballoc(sta, "xnt", [128, 8, TC])
    xnbt = sballoc(sta, "xnbt", [128, 8, TC], BF16)
    xabt = sballoc(sta, "xabt", [128, 8, TC], BF16)
    sel_bf = sballoc(sta, "sel_bf", [33, 34 * 128], BF16)
    sel_f0 = sballoc(sta, "sel_f0", [33, 128])
    xpT = sballoc(sta, "xpT", [128, 16, 33])
    wst = sballoc(sta, "wst", [128, 3, 8, 128], BF16)
    wost = sballoc(sta, "wost", [128, 2, 16, 128], BF16)
    xi0 = sballoc(sta, "xi0", [128, 16, TC + 3])
    xi1 = sballoc(sta, "xi1", [128, 16, TC + 3])
    xi2 = sballoc(sta, "xi2", [128, 16, TC])
    sig = sballoc(sta, "sig", [128, 16, TC])
    projf = sballoc(sta, "projf", [33, TC])
    projb = sballoc(sta, "projb", [33, TC], BF16)
    brepb = sballoc(sta, "brepb", [128, 16, TC], BF16)
    crepb = sballoc(sta, "crepb", [128, 16, TC], BF16)
    zt2 = [sballoc(sta, f"zt{i}", [128, TC]) for i in range(2)]
    mt2 = [sballoc(sta, f"mt{i}", [128, TC]) for i in range(2)]
    t1t2 = [sballoc(sta, f"t1t{i}", [128, TC]) for i in range(2)]
    dtt2 = [sballoc(sta, f"dtt{i}", [128, TC]) for i in range(2)]
    dxt2 = [sballoc(sta, f"dxt{i}", [128, TC]) for i in range(2)]
    a_t = [sballoc(sta, f"a_t{i}", [128, TC]) for i in range(2)]
    b_t = sballoc(sta, "b_t", [128, TC])
    hg = [sballoc(sta, f"hg{i}", [128, 4, TC]) for i in range(2)]
    yht = [sballoc(sta, f"yht{i}", [128, TC], BF16) for i in range(2)]
    carry = sballoc(sta, "carry", [128, 256])
    y32 = sballoc(sta, "y32", [128, 16, TC])
    ygb = sballoc(sta, "ygb", [128, 16, TC], BF16)

    dma("dma_pool", sel_bf[:, :], selp, [], ["sel_bf"])
    dma("dma_sp", sel_f0[:, :], selp[:, 0:128], [], ["sel_f0"])

    for sq in range(BC):
        for c in range(NCH):
            t0 = sq * S + c * TC
            dma("dma_sp", xat[:, :, :], tok_slice(x0, t0, TC), ["dram_x0"], ["xat"])
            for i in range(8):
                ln_feed(xat[:, i, :], "xat", i, 8, TC)
            ln_rows(TC, CV_INVH, CV_EPSA)
            for i in range(8):
                sl = i % 2
                ln_center(xat[:, i, :], "xat", TC, sl)
                em.add("act", lambda b, i=i, lt=ln_t1[sl]: b.copy(xnt[:, i, :], lt[:, :TC]),
                       reads=[f"ln_t1_{sl}"], writes=["xnt"], fd=TC)
                em.add("act", lambda b, i=i, lt=ln_t1[sl]: b.copy(xnbt[:, i, :], lt[:, :TC]),
                       reads=[f"ln_t1_{sl}"], writes=["xnbt"], fd=TC)
            dma("dma_sp", tok_slice(xn32, t0, TC), xnt[:, :, :], ["xnt"], ["dram_xn32"])
            dma("dma_act", tok_slice(xnbf, t0, TC), xnbt[:, :, :], ["xnbt"], ["dram_xnbf"])
            for i in range(8):
                ln_feed(xnt[:, i, :], "xnt", i, 8, TC)
            ln_rows(TC, CV_INVH, CV_EPSA)
            for i in range(8):
                sl = i % 2
                ln_center(xnt[:, i, :], "xnt", TC, sl)
                em.add("act", lambda b, i=i, lt=ln_t1[sl]: b.activation(
                    xat[:, i, :], lt[:, :TC], AF.Identity,
                    scale=smap("in_norm_g", i), bias=smap("in_norm_b", i)),
                    reads=[f"ln_t1_{sl}"], writes=["xat"], sreads=["smal"], fd=TC)
            dma("dma_sp", tok_slice(xaping[0], t0, TC), xat[:, :, :], ["xat"], ["dram_xa0"])

    def conv_block(l, c, j):
        for (src, srck, ci) in ((xi0, "xi0", 0), (xi1, "xi1", 1)):
            if c == 0:
                em.add("dve", lambda b, src=src, j=j: b.memset(src[:, j, 0:3], 0.0),
                       reads=[], writes=[srck], fd=3)
            wbase = SM["conv_w"] + l * (2 * KCONV * 16) + ci * (KCONV * 16)
            cc, cck = cvt[j % 2], f"cvt{j % 2}"
            em.add("dve", lambda b, src=src, j=j, cc=cc,
                   col=wbase + (KCONV - 1) * 16 + j:
                   b.tensor_scalar(
                       cc[:, :TC], src[:, j, 3:3 + TC], smal[:, col:col + 1],
                       None, ALU.mult),
                   reads=[srck], writes=[cck], sreads=["smal"], fd=TC)
            for k in range(1, KCONV):
                em.add("dve", lambda b, src=src, j=j, k=k, cc=cc,
                       col=wbase + (KCONV - 1 - k) * 16 + j:
                       b.scalar_tensor_tensor(
                           cc[:, :TC], src[:, j, 3 - k:3 - k + TC],
                           smal[:, col:col + 1], cc[:, :TC], ALU.mult, ALU.add),
                       reads=[srck, cck], writes=[cck], sreads=["smal"], fd=TC)
            em.add("dve", lambda b, src=src, j=j: b.tensor_copy(
                src[:, j, 0:3], src[:, j, TC:TC + 3]),
                reads=[srck], writes=[srck], fd=3)
            bcol = SM["conv_b"] + l * 32 + ci * 16 + j
            if ci == 0:
                em.add("act", lambda b, j=j, bcol=bcol, cc=cc: b.activation(
                    xi1[:, j, 3:3 + TC], cc[:, :TC], AF.Silu,
                    bias=smal[:, bcol:bcol + 1]),
                    reads=[cck], writes=["xi1"], sreads=["smal"], fd=TC)
            else:
                em.add("act", lambda b, j=j, bcol=bcol, cc=cc: b.activation(
                    xi2[:, j, :], cc[:, :TC], AF.Silu,
                    bias=smal[:, bcol:bcol + 1]),
                    reads=[cck], writes=["xi2"], sreads=["smal"], fd=TC)

    def dt_chain(l, j):
        p = j % 2
        zt, mt, t1t, dtt, dxt = zt2[p], mt2[p], t1t2[p], dtt2[p], dxt2[p]
        zk, mk, tk, dk, xk = f"zt{p}", f"mt{p}", f"t1t{p}", f"dtt{p}", f"dxt{p}"
        em.add("act", lambda b, j=j, zt=zt: b.activation(
            zt[:, :], PB(PB_DT)[:, :TC], AF.Identity,
            scale=smap("dt_w", l * 16 + j), bias=smap("dt_b", l * 16 + j)),
            reads=[PBK[PB_DT]], writes=[zk], sreads=["smal"], fd=TC)
        em.add("dve", lambda b, mt=mt, zt=zt: b.tensor_scalar_max(mt[:, :], zt[:, :], 0.0),
               reads=[zk], writes=[mk], fd=TC)
        em.add("dve", lambda b, t1t=t1t, mt=mt, zt=zt: b.scalar_tensor_tensor(
            t1t[:, :], mt[:, :], -2.0, zt[:, :], ALU.mult, ALU.add),
               reads=[mk, zk], writes=[tk], fd=TC)
        em.add("act", lambda b, zt=zt, t1t=t1t: b.activation(zt[:, :], t1t[:, :], AF.Exp),
               reads=[tk], writes=[zk], fd=TC)
        em.add("act", lambda b, t1t=t1t, zt=zt: b.activation(t1t[:, :], zt[:, :], AF.Ln, bias=1.0),
               reads=[zk], writes=[tk], fd=TC)
        em.add("dve", lambda b, dtt=dtt, mt=mt, t1t=t1t: b.tensor_add(dtt[:, :], mt[:, :], t1t[:, :]),
               reads=[mk, tk], writes=[dk], fd=TC)
        em.add("dve", lambda b, j=j, dxt=dxt, dtt=dtt: b.tensor_mul(dxt[:, :], dtt[:, :], xi2[:, j, :]),
               reads=[dk, "xi2"], writes=[xk], fd=TC)
        return zk, mk, tk, dk, xk, dtt, dxt

    def precast_weights(l):
        p = l % 2
        dma("dma_pool", winb[p][:, :, :, :], win[l], [], [f"dram_winb{p}"])
        dma("dma_pool", woutb[p][:, :, :, :], wout[l], [], [f"dram_woutb{p}"])

    precast_weights(0)
    for l in range(L):
        xa_in, xa_out = xaping[l % 2], xaping[(l + 1) % 2]
        ki, ko = f"dram_xa{l % 2}", f"dram_xa{(l + 1) % 2}"
        if l + 1 < L:
            precast_weights(l + 1)
        dma("dma_pool", xpT[:, :, :], wxp[l].rearrange("a p n -> p a n"), [], ["xpT"])
        for sq in range(BC):
            for c in range(NCH):
                t0 = sq * S + c * TC
                dma("dma_sp", xat[:, :, :], tok_slice(xa_in, t0, TC), [ki], ["xat"])
                for i in range(8):
                    ln_feed(xat[:, i, :], "xat", i, 8, TC)
                ln_rows(TC, CV_INVH, CV_EPSA)
                for i in range(8):
                    sl = i % 2
                    ln_center(xat[:, i, :], "xat", TC, sl)
                    em.add("act", lambda b, i=i, lt=ln_t1[sl]: b.activation(
                        xabt[:, i, :], lt[:, :TC], AF.Identity,
                        scale=smap("ln_g", l * 8 + i), bias=smap("ln_b", l * 8 + i)),
                        reads=[f"ln_t1_{sl}"], writes=["xabt"], sreads=["smal"], fd=TC)
                if debug and l == 0:
                    dma("dma_sp", dbg_xl[:, t0:t0 + TC].rearrange("(a p) t -> p a t", p=128),
                        xabt[:, :, :], ["xabt"], ["dbg_xl"])
                for j in range(32):
                    sl = j % 3
                    dma("dma_sp" if j % 2 == 0 else "dma_act", wst[:, sl, :, :],
                        winb[l % 2][j].rearrange("a p n -> p a n"),
                        [f"dram_winb{l % 2}"], [f"wst{sl}"])
                    bank = PB_MAIN[j % 2]
                    for k in range(8):
                        em.add("pe", lambda b, k=k, sl=sl, bank=bank, s=(k == 0),
                               e=(k == 7): b.matmul(
                                   PB(bank)[:, :TC], wst[:, sl, k, :],
                                   xabt[:, k, :], start=s, stop=e),
                               reads=[f"wst{sl}", "xabt"], writes=[PBK[bank]], fd=TC)
                    if j < 16:
                        em.add("act", lambda b, j=j, bank=bank: b.activation(
                            xi0[:, j, 3:3 + TC], PB(bank)[:, :TC], AF.Silu),
                            reads=[PBK[bank]], writes=["xi0"], fd=TC)
                        conv_block(l, c, j)
                    else:
                        em.add("act", lambda b, j=j, bank=bank: b.activation(
                            sig[:, j - 16, :], PB(bank)[:, :TC], AF.Sigmoid),
                            reads=[PBK[bank]], writes=["sig"], fd=TC)
                for k in range(16):
                    em.add("pe", lambda b, k=k, s=(k == 0), e=(k == 15): b.matmul(
                        PB(PB_MAIN[0])[0:33, :TC], xpT[:, k, :], xi2[:, k, :],
                        start=s, stop=e),
                        reads=["xpT", "xi2"], writes=[PBK[PB_MAIN[0]]], fd=TC)
                em.add("act", lambda b: b.copy(projf[:, :], PB(PB_MAIN[0])[0:33, :TC]),
                       reads=[PBK[PB_MAIN[0]]], writes=["projf"], fd=TC)
                em.add("act", lambda b: b.copy(projb[:, :], projf[:, :]),
                       reads=["projf"], writes=["projb"], fd=TC)
                if debug and l == 0:
                    dma("dma_sp", dbg_xi2[:, t0:t0 + TC].rearrange("(a p) t -> p a t", p=128),
                        xi2[:, :, :], ["xi2"], ["dbg_xi2"])
                    dma("dma_sp", dbg_sig[:, t0:t0 + TC].rearrange("(a p) t -> p a t", p=128),
                        sig[:, :, :], ["sig"], ["dbg_sig"])
                    dma("dma_sp", dbg_proj[:, t0:t0 + TC], projf[:, :], ["projf"], ["dbg_proj"])
                em.add("pe", lambda b: b.matmul(PB(PB_DT)[:, :TC], sel_f0[:, :],
                                                projf[:, :], start=True, stop=True),
                       reads=["sel_f0", "projf"], writes=[PBK[PB_DT]], fd=TC)
                for n in range(NST):
                    bank = PB_MAIN[n % 2]
                    em.add("pe", lambda b, n=n, bank=bank: b.matmul(
                        PB(bank)[:, :TC], sel_bf[:, (1 + n) * 128:(2 + n) * 128],
                        projb[:, :], start=True, stop=True),
                        reads=["sel_bf", "projb"], writes=[PBK[bank]], fd=TC)
                    em.add("act", lambda b, n=n, bank=bank: b.copy(
                        brepb[:, n, :], PB(bank)[:, :TC]),
                        reads=[PBK[bank]], writes=["brepb"], fd=TC)
                    em.add("pe", lambda b, n=n, bank=bank: b.matmul(
                        PB(bank)[:, :TC], sel_bf[:, (17 + n) * 128:(18 + n) * 128],
                        projb[:, :], start=True, stop=True),
                        reads=["sel_bf", "projb"], writes=[PBK[bank]], fd=TC)
                    em.add("act", lambda b, n=n, bank=bank: b.copy(
                        crepb[:, n, :], PB(bank)[:, :TC]),
                        reads=[PBK[bank]], writes=["crepb"], fd=TC)
                for j in range(16):
                    _, _, _, dk_, xk_, dtt, dxt = dt_chain(l, j)
                    for g in range(4):
                        hp = hg[g % 2]
                        for nn in range(4):
                            n = g * 4 + nn
                            hk = f"hg{g % 2}_{nn}"
                            av, avk = a_t[n % 2], f"a_t{n % 2}"
                            em.add("act", lambda b, n=n, av=av, dtt=dtt: b.activation(
                                av[:, :], dtt[:, :], AF.Identity,
                                scale=cvap(CV_NEG0 + n), bias=1.0),
                                reads=[dk_], writes=[avk], sreads=["cons"], fd=TC)
                            em.add("dve", lambda b, n=n, dxt=dxt: b.tensor_mul(
                                b_t[:, :], dxt[:, :], brepb[:, n, :]),
                                reads=[xk_, "brepb"], writes=["b_t"], fd=TC)
                            if c == 0:
                                init, sr = 0.0, []
                            else:
                                init = carry[:, j * 16 + n: j * 16 + n + 1]
                                sr = ["carry"]
                            em.add("dve", lambda b, nn=nn, hp=hp, init=init, av=av:
                                   b.tensor_tensor_scan(
                                       hp[:, nn, :], av[:, :], b_t[:, :], init,
                                       ALU.mult, ALU.add),
                                   reads=[avk, "b_t"], writes=[hk], sreads=sr, fd=TC)
                            yp, yk = yht[nn % 2], f"yht{nn % 2}"
                            em.add("pool", lambda b, nn=nn, hp=hp, yp=yp, n=n:
                                   b.tensor_mul(yp[:, :], hp[:, nn, :],
                                                crepb[:, n, :]),
                                   reads=[hk, "crepb"], writes=[yk], fd=TC)
                            em.add("pe", lambda b, yp=yp, s=(n == 0), e=(n == NST - 1):
                                   b.matmul(PB(PB_Y)[:, :TC], id_bf[:, :], yp[:, :],
                                            start=s, stop=e),
                                   reads=[yk, "id_bf"], writes=[PBK[PB_Y]], fd=TC)
                        em.add("dve", lambda b, j=j, g=g, hp=hp: b.tensor_copy(
                            carry[:, j * 16 + g * 4: j * 16 + g * 4 + 4],
                            hp[:, :, TC - 1]),
                            reads=[f"hg{g % 2}_{i}" for i in range(4)],
                            writes=["carry"], fd=4)
                    em.add("dve", lambda b, j=j: b.scalar_tensor_tensor(
                        y32[:, j, :], xi2[:, j, :], smap("Dp", l * 16 + j),
                        PB(PB_Y)[:, :TC], ALU.mult, ALU.add),
                        reads=["xi2", PBK[PB_Y]], writes=["y32"], sreads=["smal"], fd=TC)
                    ln_feed(y32[:, j, :], "y32", j, 16, TC)
                if debug and l == 0:
                    dma("dma_sp", dbg_y[:, t0:t0 + TC].rearrange("(a p) t -> p a t", p=128),
                        y32[:, :, :], ["y32"], ["dbg_y"])
                ln_rows(TC, CV_INVDI, CV_EPSA)
                for j in range(16):
                    sl = j % 2
                    ln_center(y32[:, j, :], "y32", TC, sl)
                    em.add("act", lambda b, j=j, lt=ln_t1[sl], cc=cvt[sl]: b.activation(
                        cc[:, :TC], lt[:, :TC], AF.Identity,
                        scale=smap("mln_g", l * 16 + j), bias=smap("mln_b", l * 16 + j)),
                        reads=[f"ln_t1_{sl}"], writes=[f"cvt{sl}"], sreads=["smal"], fd=TC)
                    em.add("dve", lambda b, j=j, cc=cvt[sl]: b.tensor_mul(
                        ygb[:, j, :], cc[:, :TC], sig[:, j, :]),
                        reads=[f"cvt{sl}", "sig"], writes=["ygb"], fd=TC)
                for j in range(8):
                    dma("dma_sp" if j % 2 == 0 else "dma_act", wost[:, j % 2, :, :],
                        woutb[l % 2][j].rearrange("a p n -> p a n"),
                        [f"dram_woutb{l % 2}"], [f"wost{j % 2}"])
                    bank = PB_MAIN[j % 2]
                    for k in range(16):
                        em.add("pe", lambda b, k=k, j=j, bank=bank, s=(k == 0),
                               e=(k == 15):
                               b.matmul(PB(bank)[:, :TC], wost[:, j % 2, k, :],
                                        ygb[:, k, :], start=s, stop=e),
                               reads=[f"wost{j % 2}", "ygb"], writes=[PBK[bank]], fd=TC)
                    em.add("dve", lambda b, j=j, bank=bank: b.tensor_add(
                        xat[:, j, :], PB(bank)[:, :TC], xat[:, j, :]),
                        reads=[PBK[bank], "xat"], writes=["xat"], fd=TC)
                dma("dma_sp", tok_slice(xa_out, t0, TC), xat[:, :, :], ["xat"], [ko])

    em.barrier()
    sta.close()

    stb = ExitStack()
    set_psum(stb, MC)
    op1T = sballoc(stb, "op1T", [128, 8, 1024], BF16)
    op2T = sballoc(stb, "op2T", [128, 8, 1024], BF16)
    mx32 = sballoc(stb, "mx32", [128, 8, MC])
    mxb = sballoc(stb, "mxb", [128, 8, MC], BF16)
    g1b = sballoc(stb, "g1b", [128, 8, MC], BF16)
    mst = sballoc(stb, "mst", [128, 8, MC])
    dma("dma_pool", op1T[:, :, :], wop1.rearrange("a p n -> p a n"), [], ["op1T"])
    dma("dma_pool", op2T[:, :, :], wop2.rearrange("a p n -> p a n"), [], ["op2T"])
    xa_fin = xaping[L % 2]
    kfin = f"dram_xa{L % 2}"
    for mc_i in range(NMC):
        t0 = mc_i * MC
        dma("dma_sp", mx32[:, :, :], tok_slice(xa_fin, t0, MC), [kfin], ["mx32"])
        for i in range(8):
            em.add("dve", lambda b, i=i: b.tensor_copy(mxb[:, i, :], mx32[:, i, :]),
                   reads=["mx32"], writes=["mxb"], fd=MC)
        for j in range(8):
            bank = PB_MAIN[j % 2]
            for k in range(8):
                em.add("pe", lambda b, j=j, k=k, bank=bank, s=(k == 0), e=(k == 7):
                       b.matmul(PB(bank)[:, :MC], op1T[:, k, j * 128:(j + 1) * 128],
                                mxb[:, k, :], start=s, stop=e),
                       reads=["op1T", "mxb"], writes=[PBK[bank]], fd=MC)
            em.add("act", lambda b, j=j, bank=bank: b.activation(
                g1b[:, j, :], PB(bank)[:, :MC], AF.Gelu, bias=smap("op1_b", j)),
                reads=[PBK[bank]], writes=["g1b"], sreads=["smal"], fd=MC)
        dma("dma_sp", mx32[:, :, :], tok_slice(xn32, t0, MC), ["dram_xn32"], ["mx32"])
        for j in range(8):
            bank = PB_MAIN[j % 2]
            for k in range(8):
                em.add("pe", lambda b, j=j, k=k, bank=bank, s=(k == 0), e=(k == 7):
                       b.matmul(PB(bank)[:, :MC], op2T[:, k, j * 128:(j + 1) * 128],
                                g1b[:, k, :], start=s, stop=e),
                       reads=["op2T", "g1b"], writes=[PBK[bank]], fd=MC)
            em.add("dve", lambda b, j=j, bank=bank: b.scalar_tensor_tensor(
                mst[:, j, :], PB(bank)[:, :MC], smap("op2_b", j), mx32[:, j, :],
                ALU.add, ALU.add),
                reads=[PBK[bank], "mx32"], writes=["mst"], sreads=["smal"], fd=MC)
        dma("dma_sp", tok_slice(s32, t0, MC), mst[:, :, :], ["mst"], ["dram_s32"])

    em.barrier()
    stb.close()

    stc = ExitStack()
    set_psum(stc, MC)
    gfm = sballoc(stc, "gfm", [8, TOK], BF16)
    sel8 = sballoc(stc, "sel8", [8, 8 * 128], BF16)
    dma("dma_pool", sel8[:, :], selp[0:8, 0:8 * 128], [], ["sel8"])
    std = ExitStack()
    rtb = sballoc(std, "rtb", [128, 8, 8])
    xng = sballoc(std, "xng", [128, 8, 128])
    ltok = sballoc(std, "ltok", [128, 8])
    gv1 = sballoc(std, "gv1", [128, 1])
    gm1 = sballoc(std, "gm1", [128, 8])
    gl2 = sballoc(std, "gl2", [128, 8])
    gv2 = sballoc(std, "gv2", [128, 1])
    gm2 = sballoc(std, "gm2", [128, 8])
    gdd = sballoc(std, "gdd", [128, 1])
    gee = sballoc(std, "gee", [128, 1])
    gsv = sballoc(std, "gsv", [128, 1])
    gw2 = sballoc(std, "gw2", [128, 1])
    ggt = sballoc(std, "ggt", [128, 8])
    dma("dma_sp", rtb[:, :, :], wrt.rearrange("a p n -> p a n"), [], ["rtb"])
    for tg in range(TOK // 128):
        t0 = tg * 128
        dma("dma_sp", xng[:, :, :], tok_slice(xn32, t0, 128), ["dram_xn32"], ["xng"])
        for k in range(8):
            em.add("pe", lambda b, k=k, s=(k == 0), e=(k == 7): b.matmul(
                PB(PB_MAIN[0])[:, 0:8], xng[:, k, :], rtb[:, k, :], start=s, stop=e),
                reads=["xng", "rtb"], writes=[PBK[PB_MAIN[0]]], fd=8)
        em.add("act", lambda b: b.copy(ltok[:, :], PB(PB_MAIN[0])[:, 0:8]),
               reads=[PBK[PB_MAIN[0]]], writes=["ltok"], fd=8)
        em.add("dve", lambda b: b.reduce_max(gv1[:, :], ltok[:, :], axis=AX.X),
               reads=["ltok"], writes=["gv1"], fd=8)
        em.add("dve", lambda b: b.tensor_scalar(gm1[:, :], ltok[:, :], gv1[:, :],
                                                None, ALU.is_ge),
               reads=["ltok"], writes=["gm1"], sreads=["gv1"], fd=8)
        em.add("dve", lambda b: b.scalar_tensor_tensor(gl2[:, :], gm1[:, :], -1e30,
                                                       ltok[:, :], ALU.mult, ALU.add),
               reads=["gm1", "ltok"], writes=["gl2"], fd=8)
        em.add("dve", lambda b: b.reduce_max(gv2[:, :], gl2[:, :], axis=AX.X),
               reads=["gl2"], writes=["gv2"], fd=8)
        em.add("dve", lambda b: b.tensor_scalar(gm2[:, :], gl2[:, :], gv2[:, :],
                                                None, ALU.is_ge),
               reads=["gl2"], writes=["gm2"], sreads=["gv2"], fd=8)
        em.add("dve", lambda b: b.tensor_sub(gdd[:, :], gv2[:, :], gv1[:, :]),
               reads=["gv2", "gv1"], writes=["gdd"], fd=1)
        em.add("act", lambda b: b.activation(gee[:, :], gdd[:, :], AF.Exp),
               reads=["gdd"], writes=["gee"], fd=1)
        em.add("dve", lambda b: b.tensor_scalar_add(gsv[:, :], gee[:, :], 1.0),
               reads=["gee"], writes=["gsv"], fd=1)
        em.add("dve", lambda b: b.reciprocal(gsv[:, :], gsv[:, :]),
               reads=["gsv"], writes=["gsv"], fd=1)
        em.add("dve", lambda b: b.tensor_mul(gw2[:, :], gee[:, :], gsv[:, :]),
               reads=["gee", "gsv"], writes=["gw2"], fd=1)
        em.add("dve", lambda b: b.tensor_scalar(ggt[:, :], gm1[:, :], gsv[:, :],
                                                None, ALU.mult),
               reads=["gm1"], writes=["ggt"], sreads=["gsv"], fd=8)
        em.add("dve", lambda b: b.scalar_tensor_tensor(ggt[:, :], gm2[:, :], gw2[:, :],
                                                       ggt[:, :], ALU.mult, ALU.add),
               reads=["gm2", "ggt"], writes=["ggt"], sreads=["gw2"], fd=8)
        em.add("pe", lambda b: b.transpose(PB(PB_MAIN[1])[0:8, 0:128], ggt[:, :],
                                           ident[:, :]),
               reads=["ggt", "ident"], writes=[PBK[PB_MAIN[1]]], fd=128)
        em.add("act", lambda b, t0=t0: b.copy(gfm[:, t0:t0 + 128],
                                              PB(PB_MAIN[1])[0:8, 0:128]),
               reads=[PBK[PB_MAIN[1]]], writes=["gfm"], fd=128)

    em.barrier()
    std.close()

    ste = ExitStack()
    w1h = sballoc(ste, "w1h", [128, 8, DFF // 2], BF16)
    w2h = sballoc(ste, "w2h", [128, 16, H], BF16)
    exb = sballoc(ste, "exb", [128, 8, MC], BF16)
    ex32 = sballoc(ste, "ex32", [128, 8, MC])
    h1h = sballoc(ste, "h1h", [128, 16, MC], BF16)
    rest = sballoc(ste, "rest", [128, 8, MC])
    ptile = sballoc(ste, "ptile", [128, 8, MC])
    for e in range(NEXP):
        for hf in range(2):
            f0 = hf * 16
            for a in range(8):
                dma("dma_pool", w1h[:, a, :],
                    w1e[e, a, :, hf * (DFF // 2):(hf + 1) * (DFF // 2)], [], ["w1h"])
            for a in range(16):
                dma("dma_pool", w2h[:, a, :], w2e[e, f0 + a], [], ["w2h"])
            for mc_i in range(NMC):
                t0 = mc_i * MC
                dma("dma_sp", exb[:, :, :], tok_slice(xnbf, t0, MC),
                    ["dram_xnbf"], ["exb"])
                for f in range(16):
                    bank = PB_MAIN[f % 2]
                    for k in range(8):
                        em.add("pe", lambda b, f=f, k=k, bank=bank, s=(k == 0),
                               e_=(k == 7): b.matmul(
                                   PB(bank)[:, :MC], w1h[:, k, f * 128:(f + 1) * 128],
                                   exb[:, k, :], start=s, stop=e_),
                               reads=["w1h", "exb"], writes=[PBK[bank]], fd=MC)
                    em.add("act", lambda b, f=f, e=e, f0=f0, bank=bank: b.activation(
                        h1h[:, f, :], PB(bank)[:, :MC], AF.Gelu,
                        bias=smap("b1", e * 32 + f0 + f)),
                        reads=[PBK[bank]], writes=["h1h"], sreads=["smal"], fd=MC)
                if hf == 1:
                    dma("dma_act", ptile[:, :, :], tok_slice(pdram, t0, MC),
                        ["dram_pdram"], ["ptile"])
                    dma("dma_act", ex32[:, :, :], tok_slice(xn32, t0, MC),
                        ["dram_xn32"], ["ex32"])
                    em.add("pe", lambda b, e=e, t0=t0: b.matmul(
                        PB(PB_DT)[:, :MC], sel8[:, e * 128:(e + 1) * 128],
                        gfm[:, t0:t0 + MC], start=True, stop=True),
                        reads=["gfm", "sel8"], writes=[PBK[PB_DT]], fd=MC)
                for dj in range(8):
                    bank = PB_MAIN[dj % 2]
                    for k in range(16):
                        em.add("pe", lambda b, dj=dj, k=k, bank=bank, s=(k == 0),
                               e_=(k == 15): b.matmul(
                                   PB(bank)[:, :MC], w2h[:, k, dj * 128:(dj + 1) * 128],
                                   h1h[:, k, :], start=s, stop=e_),
                               reads=["w2h", "h1h"], writes=[PBK[bank]], fd=MC)
                    if hf == 0:
                        em.add("act", lambda b, dj=dj, bank=bank: b.copy(
                            ptile[:, dj, :], PB(bank)[:, :MC]),
                            reads=[PBK[bank]], writes=["ptile"], fd=MC)
                    else:
                        em.add("dve", lambda b, dj=dj, bank=bank: b.tensor_add(
                            ptile[:, dj, :], PB(bank)[:, :MC], ptile[:, dj, :]),
                            reads=[PBK[bank], "ptile"], writes=["ptile"], fd=MC)
                        em.add("dve", lambda b, dj=dj, e=e: b.scalar_tensor_tensor(
                            rest[:, dj, :], ptile[:, dj, :], smap("b2", e * 8 + dj),
                            ex32[:, dj, :], ALU.add, ALU.add),
                            reads=["ptile", "ex32"], writes=["rest"],
                            sreads=["smal"], fd=MC)
                if hf == 0:
                    dma("dma_act", tok_slice(pdram, t0, MC), ptile[:, :, :],
                        ["ptile"], ["dram_pdram"])
                    continue
                for i in range(8):
                    ln_feed(rest[:, i, :], "rest", i, 8, MC)
                ln_rows(MC, CV_INVH, CV_EPSM)
                for dj in range(8):
                    sl = dj % 2
                    ln_center(rest[:, dj, :], "rest", MC, sl)
                    em.add("act", lambda b, dj=dj, e=e, lt=ln_t1[sl], cc=cvt[sl]:
                           b.activation(
                               cc[:, :MC], lt[:, :MC], AF.Identity,
                               scale=smap("gamma", e * 8 + dj),
                               bias=smap("beta", e * 8 + dj)),
                           reads=[f"ln_t1_{sl}"], writes=[f"cvt{sl}"],
                           sreads=["smal"], fd=MC)
                    em.add("dve", lambda b, dj=dj, cc=cvt[sl]: b.tensor_mul(
                        rest[:, dj, :], cc[:, :MC], PB(PB_DT)[:, :MC]),
                        reads=[f"cvt{sl}", PBK[PB_DT]], writes=["rest"], fd=MC)
                if e == 0:
                    dma("dma_pool", tok_slice(ymoe, t0, MC), rest[:, :, :],
                        ["rest"], ["dram_ymoe"])
                else:
                    dma_acc(tok_slice(ymoe, t0, MC), rest[:, :, :],
                            ["rest"], ["dram_ymoe"])

    em.barrier()
    ste.close()
    stc.close()

    stf = ExitStack()
    cs = sballoc(stf, "cs", [128, 8, MC])
    cy = sballoc(stf, "cy", [128, 8, MC])
    cx = sballoc(stf, "cx", [128, 8, MC])
    co = sballoc(stf, "co", [128, 8, MC])
    for mc_i in range(NMC):
        t0 = mc_i * MC
        dma("dma_sp", cs[:, :, :], tok_slice(s32, t0, MC), ["dram_s32"], ["cs"])
        dma("dma_act", cy[:, :, :], tok_slice(ymoe, t0, MC), ["dram_ymoe"], ["cy"])
        dma("dma_sp", cx[:, :, :], tok_slice(x0, t0, MC), ["dram_x0"], ["cx"])
        for i in range(8):
            em.add("act", lambda b, i=i: b.activation(cy[:, i, :], cy[:, i, :], AF.Silu),
                   reads=["cy"], writes=["cy"], fd=MC)
            em.add("dve", lambda b, i=i: b.tensor_mul(co[:, i, :], cs[:, i, :],
                                                      cy[:, i, :]),
                   reads=["cs", "cy"], writes=["co"], fd=MC)
            em.add("dve", lambda b, i=i: b.tensor_add(co[:, i, :], co[:, i, :],
                                                      cx[:, i, :]),
                   reads=["co", "cx"], writes=["co"], fd=MC)
        dma("dma_sp", tok_slice(out_fm, t0, MC), co[:, :, :], ["co"], ["dram_out"])

    em.barrier()
    em.emit(stx)
    stf.close()
    stx.close()
    return nc



def _pack_smalls(ip):
    sm = np.zeros((128, SM_COLS), np.float32)

    def put(name, vec, nblk, idx0=0):
        v = np.asarray(vec, np.float32).reshape(nblk, 128).T
        sm[:, SM[name] + idx0:SM[name] + idx0 + nblk] = v
    put("in_norm_g", ip["L0_attn_in_norm_g"], 8)
    put("in_norm_b", ip["L0_attn_in_norm_b"], 8)
    for l in range(L):
        put("ln_g", ip["L0_attn_ln_g"][l], 8, l * 8)
        put("ln_b", ip["L0_attn_ln_b"][l], 8, l * 8)
        for ci in range(2):
            for k in range(KCONV):
                put("conv_w", ip["L0_attn_conv_w"][l, ci, :, k], 16,
                    l * (2 * KCONV * 16) + ci * (KCONV * 16) + k * 16)
            put("conv_b", ip["L0_attn_conv_b"][l, ci], 16, l * 32 + ci * 16)
        put("dt_w", ip["L0_attn_dt_proj_w"][l, :, 0], 16, l * 16)
        put("dt_b", ip["L0_attn_dt_proj_b"][l], 16, l * 16)
        put("Dp", ip["L0_attn_D"][l], 16, l * 16)
        put("mln_g", ip["L0_attn_mamba_ln_g"][l], 16, l * 16)
        put("mln_b", ip["L0_attn_mamba_ln_b"][l], 16, l * 16)
    put("op1_b", ip["L0_attn_op1_b"], 8)
    put("op2_b", ip["L0_attn_op2_b"], 8)
    for e in range(NEXP):
        put("b1", ip["L0_ffn_b1"][e], 32, e * 32)
        put("b2", ip["L0_ffn_b2"][e], 8, e * 8)
        put("gamma", ip["L0_ffn_gamma"][e], 8, e * 8)
        put("beta", ip["L0_ffn_beta"][e], 8, e * 8)
    return sm


def _host_shared(ip):
    f = np.float32
    arrs = {}
    w = np.asarray(ip["L0_attn_in_proj_w"], f)
    arrs["win"] = np.ascontiguousarray(
        w.reshape(L, 32, 128, 8, 128).transpose(0, 1, 3, 4, 2))
    w = np.asarray(ip["L0_attn_out_proj_w"], f)
    arrs["wout"] = np.ascontiguousarray(
        w.reshape(L, 8, 128, 16, 128).transpose(0, 1, 3, 4, 2))
    w = np.asarray(ip["L0_attn_x_proj_w"], f)
    arrs["wxp"] = np.ascontiguousarray(w.transpose(0, 2, 1).reshape(L, 16, 128, 33))
    arrs["wop1"] = np.ascontiguousarray(
        np.asarray(ip["L0_attn_op1_w"], f).T.reshape(8, 128, 1024))
    arrs["wop2"] = np.ascontiguousarray(
        np.asarray(ip["L0_attn_op2_w"], f).T.reshape(8, 128, 1024))
    arrs["wrt"] = np.ascontiguousarray(
        np.asarray(ip["L0_ffn_router_w"], f).reshape(8, 128, 8))
    arrs["w1e"] = np.ascontiguousarray(
        np.asarray(ip["L0_ffn_w1"], f).reshape(NEXP, 8, 128, DFF))
    arrs["w2e"] = np.ascontiguousarray(
        np.asarray(ip["L0_ffn_w2"], f).reshape(NEXP, 32, 128, H))
    arrs["smalls"] = _pack_smalls(ip)
    cvv = np.zeros((128, NCONST), f)
    cvv[:, CV_ONE] = 1.0
    cvv[:, CV_EPSA] = EPS_A
    cvv[:, CV_EPSM] = EPS_M
    cvv[:, CV_NH] = -0.5
    cvv[:, CV_INVH] = 1.0 / H
    cvv[:, CV_INVDI] = 1.0 / DI
    for n in range(NST):
        cvv[:, CV_NEG0 + n] = -(float(n + 1))
    arrs["consts"] = cvv
    sel = np.zeros((33, 34 * 128), f)
    for i in range(33):
        sel[i, i * 128:(i + 1) * 128] = 1.0
    arrs["selp"] = sel
    arrs["idmat"] = np.eye(128, dtype=f)
    arrs["onescol"] = np.ones((128, 1), f)
    arrs["onesrow"] = np.ones((1, 128), f)
    return arrs


def kernel(**inputs):
    hs = np.asarray(inputs["L0_attn_hidden_states"], np.float32)
    shared = _host_shared(inputs)
    nc = build_program(debug=False)
    in_maps = []
    for c in range(NCORE):
        m = dict(shared)
        xc = hs[c * BC:(c + 1) * BC]
        m["x0"] = np.ascontiguousarray(xc.transpose(2, 0, 1).reshape(H, TOK))
        in_maps.append(m)
    res = run_bass_kernel_spmd(nc, in_maps, core_ids=list(range(NCORE)))
    outs = []
    for c in range(NCORE):
        o = np.asarray(res.results[c]["out_fm"])
        outs.append(o.reshape(H, BC, S).transpose(1, 2, 0))
    return np.concatenate(outs, axis=0).astype(np.float32)


if __name__ == "__main__":
    import time
    t = time.time()
    nc = build_program()
    print(f"built in {time.time() - t:.1f}s")
